# version 09
# baseline (speedup 1.0000x reference)
import contextlib
import numpy as np
import ml_dtypes
import concourse.bass as bass
import concourse.mybir as mybir
from concourse.bass_utils import run_bass_kernel_spmd

F32 = mybir.dt.float32
BF16 = mybir.dt.bfloat16
AF = mybir.ActivationFunctionType
ALU = mybir.AluOpType
AX = mybir.AxisListType
import bass_rust as _br
RED = _br.ReduceOp

NCORES = 8
S = 4096
E = 1024
H = 16
D = 64
WIN = 512
NB = S // WIN
FF = 3072
NEC = E // 128
NFC = FF // 128
NTC = S // 512
MAGIC = 12582912.0
NEG_BIG = -3.0e38
QMAX = 32767.0
UMAX = 65535.0
EPS_LN = 1e-5
SLOPES = [float(np.float32(np.exp2(np.float64(-(h + 1) * 8.0 / H)))) for h in range(H)]

ENGINES = ("sync", "tensor", "vector", "scalar", "gpsimd")


class Prog:
    def __init__(self):
        self.ops = {e: [] for e in ENGINES}
        self.counts = {e: 0 for e in ENGINES}
        self.dma_counts = {e: 0 for e in ENGINES}

    def emit(self, eng, fn, inc=0):
        if inc:
            self.counts[eng] += inc

            def wrapped(h, sems, fn=fn, eng=eng, n=inc):
                fn(h, sems).then_inc(sems[eng], n)
            self.ops[eng].append(wrapped)
        else:
            self.ops[eng].append(fn)
        return (eng, self.counts[eng])

    def dma(self, eng, fn):
        self.dma_counts[eng] += 16
        tgt = self.dma_counts[eng]

        def wrapped(h, sems, fn=fn, eng=eng):
            fn(h, sems).then_inc(sems["dma_" + eng], 16)
        self.ops[eng].append(wrapped)
        return ("dma_" + eng, tgt)

    def mark(self, eng):
        key = eng if eng in self.counts else None
        return (eng, self.counts[eng])

    def wait(self, eng, marker):
        if marker is None:
            return
        sname, val = marker
        if val <= 0:
            return
        self.ops[eng].append(lambda h, sems, s=sname, v=val: h.wait_ge(sems[s], v))


def ts(i, n=128):
    return slice(i * n, (i + 1) * n)


def band_chunks(n):
    out = []
    for c in range(8):
        if n == 0 and c < 4:
            continue
        if n == 0:
            qlo, qhi = 128 * (c - 4), 512
        else:
            qlo, qhi = max(0, 128 * (c - 4)), min(512, 128 * (c + 1))
        out.append((c, qlo, qhi))
    return out


def build_program(debug=None, a2stop=None, ncores=NCORES):
    nc = bass.Bass("TRN2", target_bir_lowering=False)
    p = Prog()

    x_t = nc.dram_tensor("x_t", [E, S], F32, kind="ExternalInput").ap()
    w_in_t = nc.dram_tensor("w_in_t", [24, 128, 8, 128], F32, kind="ExternalInput").ap()
    w_out_t = nc.dram_tensor("w_out_t", [8, 128, 8, 128], F32, kind="ExternalInput").ap()
    w_gate = nc.dram_tensor("w_gate", [FF, E], F32, kind="ExternalInput").ap()
    w_up = nc.dram_tensor("w_up", [FF, E], F32, kind="ExternalInput").ap()
    w_down = nc.dram_tensor("w_down", [E, FF], F32, kind="ExternalInput").ap()
    out = nc.dram_tensor("out", [E, S], F32, kind="ExternalOutput").ap()

    kscr = nc.dram_tensor("kscr", [E, S], F32, kind="Internal").ap()
    qscr = nc.dram_tensor("qscr", [E, S], F32, kind="Internal").ap()
    vscr = nc.dram_tensor("vscr", [E, S], F32, kind="Internal").ap()
    yscr = nc.dram_tensor("yscr", [E, S], F32, kind="Internal").ap()
    btab = nc.dram_tensor("btab", [H, 2 * WIN], F32, kind="Internal").ap()
    wkind = "ExternalOutput" if (debug and "wdbg" in debug) else "Internal"
    upwh = nc.dram_tensor("upwh", [NFC, 128, 1024], BF16, kind=wkind).ap()
    upwl = nc.dram_tensor("upwl", [NFC, 128, 1024], BF16, kind=wkind).ap()
    gawh = nc.dram_tensor("gawh", [NFC, 128, 1024], BF16, kind="Internal").ap()
    gawl = nc.dram_tensor("gawl", [NFC, 128, 1024], BF16, kind="Internal").ap()
    dnwh = nc.dram_tensor("dnwh", [NEC, 3, 128, 1024], BF16, kind="Internal").ap()
    dnwl = nc.dram_tensor("dnwl", [NEC, 3, 128, 1024], BF16, kind="Internal").ap()
    upscr = nc.dram_tensor("upscr", [FF, S], F32, kind="Internal").ap()
    gascr = nc.dram_tensor("gascr", [FF, S], F32, kind="Internal").ap()
    oscr = nc.dram_tensor("oscr", [FF, S], F32, kind="Internal").ap()
    ccin = nc.dram_tensor("ccin", [1, 8], F32, kind="Internal").ap()
    ccout = nc.dram_tensor("ccout", [1, 8], F32, kind="Internal").ap()

    dbg_names = (debug or "").split(",") if debug else []
    mdbg = None
    if "mdbg" in dbg_names:
        mdbg = {
            "negb4": nc.dram_tensor("dbg_negb4", [128, 1], F32, kind="ExternalOutput").ap(),
            "ccs": nc.dram_tensor("dbg_ccs", [1, 8], F32, kind="ExternalOutput").ap(),
            "v2": nc.dram_tensor("dbg_v2", [128, 8320], F32, kind="ExternalOutput").ap(),
            "k2p0": nc.dram_tensor("dbg_k2p0", [66, 1024], F32, kind="ExternalOutput").ap(),
            "qp0": nc.dram_tensor("dbg_qp0", [66, 512], F32, kind="ExternalOutput").ap(),
            "ccs2": nc.dram_tensor("dbg_ccs2", [1, 8], F32, kind="ExternalOutput").ap(),
            "tiny": nc.dram_tensor("dbg_tiny", [128, 8], F32, kind="ExternalOutput").ap(),
            "rcol": nc.dram_tensor("dbg_rcol", [128, 144], F32, kind="ExternalOutput").ap(),
            "pv2": nc.dram_tensor("dbg_pv2", [128, 512], F32, kind="ExternalOutput").ap(),
            "pv3": nc.dram_tensor("dbg_pv3", [128, 512], F32, kind="ExternalOutput").ap(),
            "lnx": nc.dram_tensor("dbg_lnx", [128, 8], F32, kind="ExternalOutput").ap(),
            "rex": nc.dram_tensor("dbg_rex", [128, 8], F32, kind="ExternalOutput").ap(),
            "negb4h": nc.dram_tensor("dbg_negb4h", [128, 16], F32, kind="ExternalOutput").ap(),
            "mqf": nc.dram_tensor("dbg_mqf", [128, 8], F32, kind="ExternalOutput").ap(),
            "mkf": nc.dram_tensor("dbg_mkf", [128, 8], F32, kind="ExternalOutput").ap(),
        }
        dbg_names.remove("mdbg")
    odbg = None
    if "odbg" in dbg_names:
        odbg = nc.dram_tensor("dbg_odbg", [S, E], F32, kind="ExternalOutput").ap()
        dbg_names.remove("odbg")
    if "wdbg" in dbg_names:
        dbg_names.remove("wdbg")
    dbg = {}
    for name in dbg_names:
        src = {"kscr": kscr, "qscr": qscr, "vscr": vscr, "yscr": yscr,
               "upscr": upscr, "gascr": gascr, "oscr": oscr}[name]
        dbg[name] = (nc.dram_tensor("dbg_" + name, list(src.shape), F32,
                                    kind="ExternalOutput").ap(), src)

    est = contextlib.ExitStack()
    with est:
        def sbt(name, shape, dtype):
            return est.enter_context(nc.sbuf_tensor(name, shape, dtype))

        arA = sbt("arA", [128, 32768], BF16)
        arB = sbt("arB", [128, 32768], BF16)
        arC = sbt("arC", [128, 33536], BF16)

        def vf(ar, off, n, p0=0, p1=128):
            return ar.ap().bitcast(F32)[p0:p1, off:off + n]

        def vb(ar, off, n, p0=0, p1=128):
            return ar.ap()[p0:p1, off:off + n]

        qramp = sbt("qramp", [1, 512], F32)
        mrow = sbt("mrow", [1, 512], F32)
        vrow = sbt("vrow", [1, 512], F32)
        rrow = sbt("rrow", [1, 512], F32)
        mrrow = sbt("mrrow", [1, 512], F32)
        slopecol = sbt("slopecol", [16, 1], F32)
        s512col = sbt("s512col", [16, 1], F32)
        u20col = sbt("u20col", [128, 1], F32)
        onesrow = sbt("onesrow", [1, 128], F32)
        onescol = sbt("onescol", [128, 1], F32)
        eyebf = sbt("eyebf", [128, 128], BF16)
        eyef = sbt("eyef", [128, 128], F32)
        maskA = sbt("maskA", [128, 128], BF16)
        maskB = sbt("maskB", [128, 128], BF16)
        onetile = sbt("onetile", [128, 128], F32)
        scol = sbt("scol", [128, 64], F32)
        invcol = sbt("invcol", [128, 64], F32)
        sctp = vf(arC, 15360, 512, 0, 32)
        rcol = sbt("rcol", [128, 8], F32)
        lncol = sbt("lncol", [128, 8], F32)
        sccol = sbt("sccol", [128, 8], F32)
        sicol = sbt("sicol", [128, 8], F32)
        tiny = sbt("tiny", [128, 8], F32)
        lnx = sbt("lnx", [128, 8], F32)
        rmcol = sbt("rmcol", [128, 12], F32)
        eps1 = sbt("eps1", [1, 1], F32)
        mcol = sbt("mcol", [128, 1], F32)
        rex = sbt("rex", [128, 8], F32)
        ccs = sbt("ccs", [1, 8], F32)
        r327 = sbt("r327", [1, 1], F32)

        psb = [est.enter_context(nc.psum_tensor(f"ps{i}", [128, 512], F32))
               for i in range(8)]

        sems = {}
        for e in ENGINES:
            sems[e] = est.enter_context(nc.semaphore("sem_" + e))
            sems["dma_" + e] = est.enter_context(nc.semaphore("dsem_" + e))

        G = "gpsimd"
        p.emit(G, lambda h, s_: h.memset(onesrow[:, :], 1.0))
        p.emit(G, lambda h, s_: h.memset(onescol[:, :], 1.0))
        p.emit(G, lambda h, s_: h.memset(onetile[:, :], 1.0))
        p.emit(G, lambda h, s_: h.memset(eps1[:, :], EPS_LN))
        p.emit(G, lambda h, s_: h.memset(mcol[:, :], MAGIC))
        p.emit(G, lambda h, s_: h.memset(u20col[:, :], 20.0))
        p.emit(G, lambda h, s_: h.affine_select(
            out=eyef[:, :], in_=onetile[:, :], pattern=[[-1, 128]], base=0,
            channel_multiplier=1, compare_op=ALU.is_equal, fill=0.0))
        p.emit(G, lambda h, s_: h.affine_select(
            out=eyebf[:, :], in_=onetile[:, :], pattern=[[-1, 128]], base=0,
            channel_multiplier=1, compare_op=ALU.is_equal, fill=0.0))
        p.emit(G, lambda h, s_: h.memset(maskA[:, :], NEG_BIG))
        p.emit(G, lambda h, s_: h.affine_select(
            out=maskA[:, :], in_=maskA[:, :], pattern=[[-1, 128]], base=-1,
            channel_multiplier=1, compare_op=ALU.is_ge, fill=0.0))
        p.emit(G, lambda h, s_: h.memset(maskB[:, :], NEG_BIG))
        p.emit(G, lambda h, s_: h.affine_select(
            out=maskB[:, :], in_=maskB[:, :], pattern=[[1, 128]], base=-1,
            channel_multiplier=-1, compare_op=ALU.is_ge, fill=0.0))
        p.emit(G, lambda h, s_: h.iota(
            out=qramp[:, :], pattern=[[1, 512]], base=0, channel_multiplier=0, allow_small_or_imprecise_dtypes=True))
        p.emit(G, lambda h, s_: h.iota(
            out=vf(arC, 0, 1024, 0, 16), pattern=[[1, 1024]], base=0, channel_multiplier=0, allow_small_or_imprecise_dtypes=True))
        p.emit(G, lambda h, s_: h.iota(
            out=slopecol[:, :], pattern=[[1, 1]], base=1, channel_multiplier=1, allow_small_or_imprecise_dtypes=True),
               inc=1)
        m_g0 = p.mark(G)
        p.wait("scalar", m_g0)
        p.emit("scalar", lambda h, s_: h.activation(
            slopecol[:, :], slopecol[:, :], AF.Exp,
            scale=float(-np.log(2.0) / 2.0)), inc=1)
        m_s0 = p.mark("scalar")
        p.wait("vector", m_s0)
        p.emit("vector", lambda h, s_: h.tensor_scalar(
            out=slopecol[:, :], in0=slopecol[:, :], scalar1=-1.0, scalar2=None,
            op0=ALU.mult), inc=1)
        p.wait("vector", p.mark("vector"))
        p.emit("vector", lambda h, s_: h.tensor_scalar(
            out=s512col[:, :], in0=slopecol[:, :], scalar1=-512.0, scalar2=None,
            op0=ALU.mult), inc=1)
        p.wait("vector", p.mark("vector"))
        p.emit("vector", lambda h, s_: h.tensor_scalar(
            out=vf(arC, 0, 1024, 0, 16), in0=vf(arC, 0, 1024, 0, 16),
            scalar1=slopecol[:, :], scalar2=s512col[:, :],
            op0=ALU.mult, op1=ALU.add), inc=1)
        m_v0 = p.mark("vector")
        p.wait("sync", m_v0)
        m_btab = p.dma("sync", lambda h, s_: h.dma_start(out=btab[:, :], in_=vf(arC, 0, 1024, 0, 16)))
        consts_done = (m_g0, m_v0)

        XSTG = lambda: vf(arC, 0, 4096)
        A1_WSTG = 4096
        DQ_WSTG = 5120
        DQ_TMP = 6144
        DQ_DQ = 7168
        DQ_SREP = 8192
        DQ_IREP = 9216
        B_A1WH = [2 * 10240, 2 * 10240 + 1024]
        B_A1WL = [2 * 10240 + 2048, 2 * 10240 + 3072]
        B_DQH = 2 * 10240 + 4096
        B_DQL = 2 * 10240 + 5120
        B_TSTG = 2 * 10240 + 6144
        B_TSTG2 = 31488
        ST0 = 13696
        STG = [13696, 14208, 14720, 15232]
        wohscr = nc.dram_tensor("wohscr", [8, 128, 8, 128], BF16, kind="Internal").ap()
        wolscr = nc.dram_tensor("wolscr", [8, 128, 8, 128], BF16, kind="Internal").ap()

        last_v_use = {}
        prev_st = [None]
        e32_built = [False]

        def a1_x_split():
            p.wait("sync", m_btab)
            prev = None
            for ec in range(NEC):
                mld = p.dma("sync", lambda h, s_, ec=ec: h.dma_start(
                    out=XSTG(), in_=x_t[ts(ec), :]))
                p.wait("scalar", mld)
                if prev:
                    p.wait("scalar", prev)
                p.emit("scalar", lambda h, s_, ec=ec: h.copy(
                    vb(arA, ec * 4096, 4096), XSTG()), inc=1)
                mhi = p.mark("scalar")
                p.wait("vector", mhi)
                p.emit("vector", lambda h, s_, ec=ec: h.tensor_tensor(
                    out=vb(arB, ec * 4096, 4096), in0=XSTG(),
                    in1=vb(arA, ec * 4096, 4096), op=ALU.subtract), inc=1)
                prev = p.mark("vector")
                p.wait("sync", prev)
            return prev

        m_xsplit = a1_x_split()

        def a1_qkv():
            mm_done = None
            ev_prev = [None] * 4
            wh_free = [None, None]
            a1_bank_guard = [None, None]
            rix = [0]
            for fg in range(24):
                buf = fg % 2
                mld = p.dma("sync", lambda h, s_, fg=fg: h.dma_start(
                    out=vf(arC, A1_WSTG, 1024),
                    in_=w_in_t[fg, :, :, :].rearrange("p a b -> p (a b)")))
                p.wait("scalar", mld)
                if wh_free[buf]:
                    p.wait("scalar", wh_free[buf])
                p.emit("scalar", lambda h, s_, buf=buf: h.copy(
                    vb(arC, B_A1WH[buf], 1024), vf(arC, A1_WSTG, 1024)), inc=1)
                mhi = p.mark("scalar")
                p.wait("vector", mhi)
                p.emit("vector", lambda h, s_, buf=buf: h.tensor_tensor(
                    out=vb(arC, B_A1WL[buf], 1024), in0=vf(arC, A1_WSTG, 1024),
                    in1=vb(arC, B_A1WH[buf], 1024), op=ALU.subtract), inc=1)
                mlo = p.mark("vector")
                p.wait("sync", mlo)
                p.wait("tensor", mlo)
                if fg == 0:
                    p.wait("tensor", m_xsplit)
                for tc in range(NTC):
                    bank = psb[tc % 2]
                    if a1_bank_guard[tc % 2]:
                        p.wait("tensor", a1_bank_guard[tc % 2])
                    for pa, (wof, xar) in enumerate(
                            ((B_A1WH[buf], arA), (B_A1WH[buf], arB),
                             (B_A1WL[buf], arA))):
                        for ec in range(NEC):
                            st = (pa == 0 and ec == 0)
                            sp = (pa == 2 and ec == NEC - 1)
                            p.emit("tensor", lambda h, s_, wof=wof, xar=xar, ec=ec,
                                   tc=tc, bank=bank, st=st, sp=sp:
                                   h.matmul(bank[:, :],
                                            vb(arC, wof + ec * 128, 128),
                                            vb(xar, ec * 4096 + tc * 512, 512),
                                            start=st, stop=sp),
                                   inc=(1 if sp else 0))
                    mmm = p.mark("tensor")
                    sti = (fg * NTC + tc) % 4
                    p.wait("scalar", mmm)
                    if ev_prev[sti]:
                        p.wait("scalar", ev_prev[sti])
                    if fg < 8:
                        p.emit("scalar", lambda h, s_, bank=bank, sti=sti: h.activation(
                            vf(arC, STG[sti], 512), bank[:, :], AF.Identity,
                            scale=0.125), inc=1)
                    else:
                        p.emit("scalar", lambda h, s_, bank=bank, sti=sti: h.copy(
                            vf(arC, STG[sti], 512), bank[:, :]), inc=1)
                    mev = p.mark("scalar")
                    a1_bank_guard[tc % 2] = mev
                    p.wait("gpsimd", mev)
                    scr = qscr if fg < 8 else (kscr if fg < 16 else vscr)
                    row = (fg % 8) * 128
                    mst = p.dma("gpsimd", lambda h, s_, scr=scr, row=row, tc=tc, sti=sti:
                                h.dma_start(out=scr[row:row + 128, tc * 512:(tc + 1) * 512],
                                            in_=vf(arC, STG[sti], 512)))
                    ev_prev[sti] = mst
                wh_free[buf] = p.mark("tensor")
            return p.mark("tensor"), p.mark("sync"), ev_prev, rix[0]

        m_qkv_t, m_qkv_dma, a1_ev_prev, n_rcol_qk = a1_qkv()

        def wout_split():
            prevs = None
            p.wait("scalar", m_qkv_t)
            for fg in range(8):
                mld = p.dma("sync", lambda h, s_, fg=fg: h.dma_start(
                    out=vf(arC, A1_WSTG, 1024),
                    in_=w_out_t[fg, :, :, :].rearrange("p a b -> p (a b)")))
                p.wait("scalar", mld)
                if prevs:
                    p.wait("scalar", prevs)
                p.emit("scalar", lambda h, s_: h.copy(
                    vb(arC, B_A1WH[0], 1024), vf(arC, A1_WSTG, 1024)), inc=1)
                mhi = p.mark("scalar")
                p.wait("vector", mhi)
                p.emit("vector", lambda h, s_: h.tensor_tensor(
                    out=vb(arC, B_A1WL[0], 1024), in0=vf(arC, A1_WSTG, 1024),
                    in1=vb(arC, B_A1WH[0], 1024), op=ALU.subtract), inc=1)
                mlo = p.mark("vector")
                p.wait("sync", mlo)
                m1 = p.dma("sync", lambda h, s_, fg=fg: h.dma_start(
                    out=wohscr[fg, :, :, :].rearrange("p a b -> p (a b)"),
                    in_=vb(arC, B_A1WH[0], 1024)))
                m2 = p.dma("sync", lambda h, s_, fg=fg: h.dma_start(
                    out=wolscr[fg, :, :, :].rearrange("p a b -> p (a b)"),
                    in_=vb(arC, B_A1WL[0], 1024)))
                prevs = m2
                p.wait("scalar", m2)
            return m2

        m_wout = wout_split()

        def dequant_all():
            E32 = vf(arC, 0, 1024)
            p.wait("gpsimd", m_qkv_t)
            p.emit(G, lambda h, s_: h.memset(E32, 1.0))
            p.emit(G, lambda h, s_: h.affine_select(
                out=E32, in_=E32, pattern=[[1, 1024]], base=0,
                channel_multiplier=-32, compare_op=ALU.is_ge, fill=0.0))
            p.emit(G, lambda h, s_: h.affine_select(
                out=E32, in_=E32, pattern=[[-1, 1024]], base=31,
                channel_multiplier=32, compare_op=ALU.is_ge, fill=0.0),
                   inc=1)
            p.wait("tensor", p.mark("gpsimd"))
            for m_ in a1_ev_prev:
                p.wait("tensor", m_)

            def SW(st):
                return 1024 + st * 3584

            def ST(st):
                return SW(st) + 1024

            def SHB(st):
                return 2 * (SW(st) + 2048)

            def SLB(st):
                return SHB(st) + 1024

            def SSTG(st):
                return SHB(st) + 2048

            w_free = [None] * 4
            hl_free = [None] * 4
            stg_free = [None] * 4
            sc_free = [None, None]
            sctp_free = [None, None]
            tr_free = [None, None]
            it = 0
            for w_ap, hscr, lscr, nog, nic in (
                    (w_up, upwh, upwl, 24, 8), (w_gate, gawh, gawl, 24, 8),
                    (w_down, dnwh, dnwl, 8, 24)):
                for og in range(nog):
                    for ip in range(nic // 8):
                        st = it % 4
                        pr = it % 2
                        b0 = 4 * pr
                        it += 1
                        i0_ = ip * 1024
                        if w_free[st]:
                            p.wait("sync", w_free[st])
                        mld = p.dma("sync", lambda h, s_, w_ap=w_ap, og=og,
                                    i0_=i0_, st=st: h.dma_start(
                            out=vf(arC, SW(st), 1024),
                            in_=w_ap[ts(og), i0_:i0_ + 1024]))
                        p.wait("vector", mld)
                        if sc_free[pr]:
                            p.wait("vector", sc_free[pr])
                        p.emit("vector", lambda h, s_, st=st, pr=pr:
                               h.tensor_reduce(
                            out=scol[:, 32 * pr:32 * pr + 32],
                            in_=vf(arC, SW(st), 1024).rearrange(
                                "p (b w) -> p b w", w=32),
                            axis=AX.X, op=ALU.max, apply_absolute_value=True),
                               inc=1)
                        p.wait("vector", p.mark("vector"))
                        p.emit("vector", lambda h, s_, pr=pr: h.tensor_scalar(
                            out=scol[:, 32 * pr:32 * pr + 32],
                            in0=scol[:, 32 * pr:32 * pr + 32],
                            scalar1=float(np.float32(1.0) / np.float32(7.0)),
                            scalar2=1e-12, op0=ALU.mult, op1=ALU.max), inc=1)
                        p.wait("vector", p.mark("vector"))
                        p.emit("vector", lambda h, s_, pr=pr: h.reciprocal(
                            invcol[:, 32 * pr:32 * pr + 32],
                            scol[:, 32 * pr:32 * pr + 32]), inc=1)
                        msc = p.mark("vector")
                        p.wait("tensor", msc)
                        if tr_free[pr]:
                            p.wait("tensor", tr_free[pr])
                        p.emit("tensor", lambda h, s_, pr=pr, b0=b0: h.transpose(
                            psb[b0][0:32, 0:128], scol[:, 32 * pr:32 * pr + 32],
                            eyef[:, :]))
                        p.emit("tensor", lambda h, s_, pr=pr, b0=b0: h.transpose(
                            psb[b0][0:32, 128:256],
                            invcol[:, 32 * pr:32 * pr + 32], eyef[:, :]), inc=1)
                        mtr0 = p.mark("tensor")
                        sc_free[pr] = mtr0
                        p.wait("scalar", mtr0)
                        if sctp_free[pr]:
                            p.wait("scalar", sctp_free[pr])
                        p.emit("scalar", lambda h, s_, pr=pr, b0=b0: h.copy(
                            sctp[0:32, 256 * pr:256 * pr + 256],
                            psb[b0][0:32, 0:256]), inc=1)
                        mevs = p.mark("scalar")
                        p.wait("tensor", mevs)
                        for hlf in range(2):
                            p.emit("tensor", lambda h, s_, hlf=hlf, pr=pr, b0=b0,
                                   E32=E32: h.matmul(
                                psb[b0 + hlf][:, :],
                                sctp[0:32, 256 * pr + 128:256 * pr + 256],
                                E32[0:32, 512 * hlf:512 * hlf + 512],
                                start=True, stop=True))
                            p.emit("tensor", lambda h, s_, hlf=hlf, pr=pr, b0=b0,
                                   E32=E32: h.matmul(
                                psb[b0 + 2 + hlf][:, :],
                                sctp[0:32, 256 * pr:256 * pr + 128],
                                E32[0:32, 512 * hlf:512 * hlf + 512],
                                start=True, stop=True),
                                   inc=(1 if hlf == 1 else 0))
                        mrep = p.mark("tensor")
                        sctp_free[pr] = mrep
                        p.wait("vector", mrep)
                        for hlf in range(2):
                            p.emit("vector", lambda h, s_, st=st, hlf=hlf, b0=b0:
                                   h.tensor_tensor(
                                out=vf(arC, ST(st) + 512 * hlf, 512),
                                in0=vf(arC, SW(st) + 512 * hlf, 512),
                                in1=psb[b0 + hlf][:, :], op=ALU.mult),
                                   inc=(1 if hlf == 1 else 0))
                        w_free[st] = p.mark("vector")
                        p.emit("vector", lambda h, s_, st=st: h.tensor_scalar(
                            out=vf(arC, ST(st), 1024), in0=vf(arC, ST(st), 1024),
                            scalar1=MAGIC, scalar2=None, op0=ALU.add), inc=1)
                        p.wait("vector", p.mark("vector"))
                        for hlf in range(2):
                            p.emit("vector", lambda h, s_, st=st, hlf=hlf, b0=b0:
                                   h.scalar_tensor_tensor(
                                out=vf(arC, ST(st) + 512 * hlf, 512),
                                in0=vf(arC, ST(st) + 512 * hlf, 512),
                                scalar=-MAGIC, in1=psb[b0 + 2 + hlf][:, :],
                                op0=ALU.add, op1=ALU.mult),
                                   inc=(1 if hlf == 1 else 0))
                        mdq = p.mark("vector")
                        p.wait("scalar", mdq)
                        if hl_free[st]:
                            p.wait("scalar", hl_free[st])
                        p.emit("scalar", lambda h, s_, st=st: h.copy(
                            vb(arC, SHB(st), 1024), vf(arC, ST(st), 1024)),
                               inc=1)
                        mhi = p.mark("scalar")
                        p.wait("vector", mhi)
                        p.emit("vector", lambda h, s_, st=st: h.tensor_tensor(
                            out=vb(arC, SLB(st), 1024), in0=vf(arC, ST(st), 1024),
                            in1=vb(arC, SHB(st), 1024), op=ALU.subtract), inc=1)
                        mlo = p.mark("vector")
                        bview = psb[b0].ap().bitcast(BF16)
                        p.wait("tensor", mlo)
                        mevp = None
                        for part, boff in enumerate((SHB(st), SLB(st))):
                            if part == 1:
                                p.wait("tensor", mevp)
                            for jj in range(8):
                                p.emit("tensor", lambda h, s_, boff=boff, jj=jj,
                                       bview=bview: h.transpose(
                                    bview[:, jj * 128:jj * 128 + 128],
                                    vb(arC, boff + jj * 128, 128),
                                    eyebf[:, :]), inc=(1 if jj == 7 else 0))
                            mtrp = p.mark("tensor")
                            p.wait("scalar", mtrp)
                            if part == 0 and stg_free[st]:
                                p.wait("scalar", stg_free[st])
                            if part == 1:
                                p.wait("scalar", mdh)
                            p.emit("scalar", lambda h, s_, st=st, bview=bview:
                                   h.copy(vb(arC, SSTG(st), 1024),
                                          bview[:, 0:1024]), inc=1)
                            mevp = p.mark("scalar")
                            if part == 1:
                                tr_free[pr] = mevp
                                hl_free[st] = mtrp
                            p.wait("sync", mevp)
                            dscr = hscr if part == 0 else lscr
                            md = p.dma("sync", lambda h, s_, dscr=dscr,
                                       og=og, i0_=i0_, st=st, nic=nic:
                                       h.dma_start(
                                out=(dscr[og, :, :] if nic == 8
                                     else dscr[og, i0_ // 1024, :, :]),
                                in_=vb(arC, SSTG(st), 1024)))
                            if part == 0:
                                mdh = md
                            else:
                                stg_free[st] = md
            return ("dma_sync", p.dma_counts["sync"])

        m_dq_dn = dequant_all()

        A_V2, A_OT = 0, 8320
        B_P1, B_XRES, B_YPRE = 0, 5120, 9216
        B_SQ, B_TMP = 13312, 13824
        B_YCH = [14336, 14848]
        C_K2 = [0, 1024, 2048, 3072]
        C_QP = [4096, 4608, 5120, 5632]
        C_OTH, C_OTL = 12288, 20480
        C_WOH = [28672, 29696]
        C_WOL = [30720, 31744]
        PW = [128, 256, 384, 512, 512, 384, 256, 128]
        POFF = [0]
        for w_ in PW:
            POFF.append(POFF[-1] + w_)

        def k2v(i):
            return vf(arC, C_K2[i], 1024, 0, 66)

        def qv(i):
            return vf(arC, C_QP[i], 512, 0, 66)

        def p1v(par, c):
            return vf(arB, B_P1 + par * 2560 + POFF[c], PW[c])

        def a2(a2stop=a2stop):
            p.wait("gpsimd", m_qkv_t)
            p.wait("gpsimd", m_dq_dn)
            p.wait("gpsimd", m_wout)
            ones_ap = vf(arA, A_V2, 8320).rearrange(
                "p (c h o) -> p c h o", h=16, o=65)[:, :, :, 64:65]
            p.emit("gpsimd", lambda h, s_: h.memset(ones_ap, 1.0), inc=1)
            m_v2ones = p.mark("gpsimd")
            p.wait("vector", m_qkv_t)
            p.wait("vector", m_dq_dn)
            p.wait("vector", m_wout)
            for i in range(4):
                p.emit("vector", lambda h, s_, i=i: h.memset(k2v(i)[0:1, :], 1.0))
                p.emit("vector", lambda h, s_, i=i: h.memset(qv(i)[0:2, :], 1.0),
                       inc=(1 if i == 3 else 0))
            m_v2ones_v = p.mark("vector")
            p.wait("sync", m_v2ones)
            p.wait("sync", m_v2ones_v)

            pool_free = [None] * 4
            p1_free = [None, None]
            sc_bank_free = [None, None]
            pv_bank_free = [None, None]
            pv_evac_done = [None, None]
            ot_free = None
            wo_free = [None, None]
            ych_free = [None, None]
            ymax_count = [0]
            m_last_y = None
            m_blk_v = None
            sq_guard = [None]
            mx_guard = [None]
            rmx_guard = [None]

            for n in range(NB):
                nsl = slice(n * 512, (n + 1) * 512)
                half = n % 2
                if ot_free:
                    p.wait("sync", ot_free)
                if n == 0:
                    p.wait("sync", m_dq_dn)
                mvl = p.dma("sync", lambda h, s_, nsl=nsl: h.dma_start(
                    out=vf(arA, A_OT, 4096).rearrange("p (a b) -> p a b", a=8),
                    in_=vscr[:, nsl].rearrange("(a p) b -> p a b", p=128)))
                p.wait("tensor", mvl)
                if n == 0:
                    p.wait("tensor", m_v2ones)
                    p.wait("tensor", m_v2ones_v)
                if m_blk_v:
                    p.wait("tensor", m_blk_v)
                for ec in range(NEC):
                    for tsub in range(4):
                        bank = psb[4 + tsub % 2]
                        phys = half * 4 + tsub
                        p.emit("tensor", lambda h, s_, ec=ec, tsub=tsub, bank=bank:
                               h.transpose(bank[:, 0:128],
                                           vf(arA, A_OT + ec * 512 + tsub * 128, 128),
                                           eyef[:, :]), inc=1)
                        mtr = p.mark("tensor")
                        p.wait("scalar", mtr)
                        dst = vf(arA, A_V2 + phys * 1040 + 2 * ec * 65, 130).rearrange(
                            "p (a b) -> p a b", a=2)[:, :, 0:64]
                        p.emit("scalar", lambda h, s_, bank=bank, dst=dst: h.copy(
                            dst, bank[:, 0:128].rearrange("p (a b) -> p a b", a=2)),
                               inc=1)
                        mev = p.mark("scalar")
                        p.wait("tensor", mev)
                m_v2 = p.mark("scalar")
                if a2stop == "v2":
                    return p.mark("vector"), None

                chunks = band_chunks(n)
                pv_state = [None]

                def emit_pv(stv):
                    (hh, gi, slot, par, exp_last) = stv
                    half = n % 2
                    pvb = psb[2 + gi % 2]
                    p.wait("tensor", exp_last)
                    if pv_evac_done[gi % 2]:
                        p.wait("tensor", pv_evac_done[gi % 2])
                    for j in range(4):
                        cs = [c for (c, ql, qh) in chunks
                              if ql <= 128 * j and 128 * j + 128 <= qh]
                        for k_, c in enumerate(cs):
                            qlo = [ql for (cc, ql, qh) in chunks if cc == c][0]
                            phys = (half * 4 + (c - 4)) if c >= 4 else \
                                   (((n - 1) % 2) * 4 + c)
                            p.emit("tensor", lambda h, s_, par=par, c=c, j=j,
                                   qlo=qlo, phys=phys, hh=hh, pvb=pvb, k_=k_,
                                   last=(k_ == len(cs) - 1):
                                   h.matmul(pvb[:, 65 * j:65 * j + 65],
                                            p1v(par, c)[:, 128 * j - qlo:
                                                        128 * j - qlo + 128],
                                            vf(arA, A_V2 + phys * 1040 + 65 * hh, 65),
                                            start=(k_ == 0), stop=last),
                                   inc=(1 if (j == 3 and k_ == len(cs) - 1) else 0))
                    m_pv = p.mark("tensor")
                    p1_free[par] = m_pv

                    p.wait("scalar", m_pv)
                    sums = pvb.ap()[:, 64:324].rearrange(
                        "p (a b) -> p a b", b=65)[:, :, 0]
                    p.emit("scalar", lambda h, s_, sums=sums, par=par: h.activation(
                        lnx[:, 4 * par:4 * par + 4], sums, AF.Ln), inc=1)
                    p.wait("scalar", p.mark("scalar"))
                    p.emit("scalar", lambda h, s_, par=par: h.activation(
                        rex[:, 4 * par:4 * par + 4], lnx[:, 4 * par:4 * par + 4],
                        AF.Exp, scale=-1.0), inc=1)
                    m_rex = p.mark("scalar")
                    p.wait("vector", m_rex)
                    for j in range(4):
                        p.emit("vector", lambda h, s_, j=j, hh=hh, pvb=pvb, par=par:
                               h.tensor_scalar(
                                   out=vf(arA, A_OT + j * 1024 + 64 * hh, 64),
                                   in0=pvb[:, 65 * j:65 * j + 64],
                                   scalar1=rex[:, 4 * par + j:4 * par + j + 1],
                                   scalar2=None,
                                   op0=ALU.mult), inc=(1 if j == 3 else 0))
                    pv_evac_done[gi % 2] = p.mark("vector")
                    pool_free[slot] = m_pv


                for ph in range(H + 1):
                    if ph < H:
                        hh = ph
                        gi = n * H + hh
                        slot = gi % 4
                        par = gi % 2
                        if pool_free[slot]:
                            p.wait("sync", pool_free[slot])
                        if n == 0:
                            mk = p.dma("sync", lambda h, s_, hh=hh, slot=slot:
                                       h.dma_start(out=k2v(slot)[2:66, 512:1024],
                                                   in_=kscr[64 * hh:64 * hh + 64, 0:512]))
                        else:
                            mk = p.dma("sync", lambda h, s_, hh=hh, n=n, slot=slot:
                                       h.dma_start(out=k2v(slot)[2:66, :],
                                                   in_=kscr[64 * hh:64 * hh + 64,
                                                            (n - 1) * 512:(n + 1) * 512]))
                        mb = p.dma("sync", lambda h, s_, hh=hh, slot=slot: h.dma_start(
                            out=k2v(slot)[1:2, :], in_=btab[hh:hh + 1, :]))
                        mq = p.dma("sync", lambda h, s_, hh=hh, nsl=nsl, slot=slot:
                                   h.dma_start(out=qv(slot)[2:66, :],
                                               in_=qscr[64 * hh:64 * hh + 64, nsl]))
                        p.wait("scalar", mq)
                        p.emit("scalar", lambda h, s_, hh=hh, slot=slot: h.activation(
                            qv(slot)[0:1, :], qramp[:, :], AF.Identity,
                            scale=float(SLOPES[hh])), inc=1)
                        m_arow = p.mark("scalar")

                        p.wait("tensor", mk)
                        p.wait("tensor", mb)
                        p.wait("tensor", mq)
                        p.wait("tensor", m_arow)
                        if gi == 0:
                            p.wait("tensor", m_v2)
                        if mx_guard[0]:
                            p.wait("tensor", mx_guard[0])
                        red_marks = []
                        for j in range(4):
                            bA = psb[6 if j % 2 == 0 else 4]
                            bB = psb[7 if j % 2 == 0 else 5]
                            if j >= 2:
                                p.wait("tensor", red_marks[j - 2])
                            elif mx_guard[0]:
                                p.wait("tensor", mx_guard[0])
                            if n == 0:
                                wA = 128 * (j + 1)
                                p.emit("tensor", lambda h, s_, slot=slot, j=j, wA=wA,
                                       bA=bA:
                                       h.matmul(bA[:, 0:wA],
                                                qv(slot)[0:66, 128 * j:128 * j + 128],
                                                k2v(slot)[0:66, 512:512 + wA],
                                                start=True, stop=False))
                                p.emit("tensor", lambda h, s_, j=j, bA=bA: h.matmul(
                                    bA[:, 128 * j:128 * j + 128], maskA[:, :],
                                    eyebf[:, :], start=False, stop=True), inc=1)
                                mT = p.mark("tensor")
                                p.wait("vector", mT)
                                p.emit("vector", lambda h, s_, j=j, wA=wA, bA=bA:
                                       h.tensor_reduce(out=rmcol[:, 8 + j:9 + j],
                                                       in_=bA[:, 0:wA], axis=AX.X,
                                                       op=ALU.max), inc=1)
                                red_marks.append(p.mark("vector"))
                            else:
                                p.emit("tensor", lambda h, s_, slot=slot, j=j, bA=bA:
                                       h.matmul(bA[:, 0:512],
                                                qv(slot)[0:66, 128 * j:128 * j + 128],
                                                k2v(slot)[0:66, 128 * j:128 * j + 512],
                                                start=True, stop=False))
                                p.emit("tensor", lambda h, s_, bA=bA: h.matmul(
                                    bA[:, 0:128], maskB[:, :], eyebf[:, :],
                                    start=False, stop=True))
                                p.emit("tensor", lambda h, s_, slot=slot, j=j, bB=bB:
                                       h.matmul(bB[:, 0:128],
                                                qv(slot)[0:66, 128 * j:128 * j + 128],
                                                k2v(slot)[0:66, 128 * j + 512:
                                                          128 * j + 640],
                                                start=True, stop=False))
                                p.emit("tensor", lambda h, s_, bB=bB: h.matmul(
                                    bB[:, 0:128], maskA[:, :], eyebf[:, :],
                                    start=False, stop=True), inc=1)
                                mT = p.mark("tensor")
                                p.wait("vector", mT)
                                p.emit("vector", lambda h, s_, j=j, bA=bA:
                                       h.tensor_reduce(
                                           out=rmcol[:, j:j + 1], in_=bA[:, 0:512],
                                           axis=AX.X, op=ALU.max))
                                p.emit("vector", lambda h, s_, j=j, bB=bB:
                                       h.tensor_reduce(
                                           out=rmcol[:, 4 + j:5 + j], in_=bB[:, 0:128],
                                           axis=AX.X, op=ALU.max), inc=1)
                                red_marks.append(p.mark("vector"))
                        if n > 0:
                            p.wait("vector", red_marks[-1])
                            p.wait("vector", p.mark("vector"))
                            for j in range(4):
                                p.emit("vector", lambda h, s_, j=j: h.tensor_tensor(
                                    out=rmcol[:, 8 + j:9 + j], in0=rmcol[:, j:j + 1],
                                    in1=rmcol[:, 4 + j:5 + j], op=ALU.max),
                                       inc=(1 if j == 3 else 0))

                    if ph > 0:
                        emit_pv(pv_state[0])
                    if ph < H:
                        m_comb = p.mark("vector")
                        p.wait("tensor", m_comb)
                        if rmx_guard[0]:
                            p.wait("tensor", rmx_guard[0])
                        for j in range(4):
                            p.emit("tensor", lambda h, s_, j=j: h.transpose(
                                psb[7][0:1, 128 * j:128 * j + 128],
                                rmcol[:, 8 + j:9 + j], eyef[:, :]),
                                   inc=(1 if j == 3 else 0))
                        m_rmt = p.mark("tensor")
                        p.wait("vector", m_rmt)
                        p.emit("vector", lambda h, s_, hh=hh, slot=slot:
                               h.scalar_tensor_tensor(
                                   out=qv(slot)[0:1, :], in0=qramp[:, :],
                                   scalar=float(SLOPES[hh]), in1=psb[7][0:1, 0:512],
                                   op0=ALU.mult, op1=ALU.subtract), inc=1)
                        m_row0 = p.mark("vector")
                        rmx_guard[0] = m_row0
                        mx_guard[0] = m_row0

                        p.wait("tensor", m_row0)
                        exp_last = None
                        for ci, (c, qlo, qhi) in enumerate(chunks):
                            bank = psb[ci % 2]
                            W = qhi - qlo
                            if sc_bank_free[ci % 2]:
                                p.wait("tensor", sc_bank_free[ci % 2])
                            diag = (n > 0 and c <= 3)
                            p.emit("tensor", lambda h, s_, slot=slot, c=c, qlo=qlo,
                                   qhi=qhi, bank=bank, W=W:
                                   h.matmul(bank[:, 0:W], k2v(slot)[0:66, ts(c)],
                                            qv(slot)[0:66, qlo:qhi],
                                            start=True, stop=False))
                            mcol = 128 * c if diag else 0
                            msk = maskA if diag else maskB
                            p.emit("tensor", lambda h, s_, bank=bank, mcol=mcol, msk=msk:
                                   h.matmul(bank[:, mcol:mcol + 128], msk[:, :],
                                            eyebf[:, :], start=False, stop=True),
                                   inc=1)
                            msc = p.mark("tensor")
                            p.wait("scalar", msc)
                            if p1_free[par]:
                                p.wait("scalar", p1_free[par])
                            p.emit("scalar", lambda h, s_, par=par, c=c, bank=bank,
                                   W=W, hh=hh:
                                   h.activation(p1v(par, c)[:, 0:W], bank[:, 0:W],
                                                AF.Exp, bias=u20col[:, :]), inc=1)
                            sc_bank_free[ci % 2] = p.mark("scalar")
                            exp_last = sc_bank_free[ci % 2]
                        p1_free[par] = None


                        pv_state[0] = (hh, gi, slot, par, exp_last)
                m_heads_v = p.mark("vector")
                if odbg is not None:
                    p.wait("gpsimd", m_heads_v)
                    for j in range(4):
                        p.dma("gpsimd", lambda h, s_, n=n, j=j: h.dma_start(
                            out=odbg[n * 512 + j * 128:n * 512 + j * 128 + 128, :],
                            in_=vf(arA, A_OT + j * 1024, 1024)))

                p.wait("tensor", m_heads_v)
                for ec in range(NEC):
                    bank = psb[4 + ec % 2]
                    for j in range(4):
                        p.emit("tensor", lambda h, s_, ec=ec, j=j, bank=bank:
                               h.transpose(bank[:, ts(j)],
                                           vf(arA, A_OT + j * 1024 + ts(ec).start, 128),
                                           eyef[:, :]), inc=(1 if j == 3 else 0))
                    mtr = p.mark("tensor")
                    p.wait("scalar", mtr)
                    p.emit("scalar", lambda h, s_, ec=ec, bank=bank: h.copy(
                        vb(arC, C_OTH + ec * 512, 512), bank[:, 0:512]), inc=1)
                    mhi = p.mark("scalar")
                    p.wait("vector", mhi)
                    p.emit("vector", lambda h, s_, ec=ec, bank=bank: h.tensor_tensor(
                        out=vb(arC, C_OTL + ec * 512, 512), in0=bank[:, 0:512],
                        in1=vb(arC, C_OTH + ec * 512, 512), op=ALU.subtract), inc=1)
                    mlo = p.mark("vector")
                    p.wait("tensor", mlo)
                ot_free = p.mark("tensor")
                m_osplit = p.mark("vector")
                if a2stop == "osplit":
                    return p.mark("vector"), None

                if m_blk_v:
                    p.wait("sync", m_blk_v)
                mxr = p.dma("sync", lambda h, s_, nsl=nsl: h.dma_start(
                    out=vf(arB, B_XRES, 4096).rearrange("p (a b) -> p a b", a=8),
                    in_=x_t[:, nsl].rearrange("(a p) b -> p a b", p=128)))

                for fc in range(NEC):
                    wbuf = fc % 2
                    if wo_free[wbuf]:
                        p.wait("sync", wo_free[wbuf])
                    mwh = p.dma("sync", lambda h, s_, fc=fc, wbuf=wbuf: h.dma_start(
                        out=vb(arC, C_WOH[wbuf], 1024),
                        in_=wohscr[fc, :, :, :].rearrange("p a b -> p (a b)")))
                    mwl = p.dma("sync", lambda h, s_, fc=fc, wbuf=wbuf: h.dma_start(
                        out=vb(arC, C_WOL[wbuf], 1024),
                        in_=wolscr[fc, :, :, :].rearrange("p a b -> p (a b)")))
                    bank = psb[4 + fc % 2]
                    p.wait("tensor", mwh)
                    p.wait("tensor", mwl)
                    if fc == 0:
                        p.wait("tensor", m_osplit)
                        p.wait("tensor", m_wout)
                    for pa, (woff, ooff) in enumerate(
                            ((C_WOH[wbuf], C_OTH), (C_WOH[wbuf], C_OTL),
                             (C_WOL[wbuf], C_OTH))):
                        for ec in range(NEC):
                            st_ = (pa == 0 and ec == 0)
                            sp_ = (pa == 2 and ec == NEC - 1)
                            p.emit("tensor", lambda h, s_, woff=woff, ooff=ooff,
                                   ec=ec, bank=bank, st=st_, sp=sp_:
                                   h.matmul(bank[:, :],
                                            vb(arC, woff + ec * 128, 128),
                                            vb(arC, ooff + ec * 512, 512),
                                            start=st, stop=sp),
                                   inc=(1 if sp_ else 0))
                    mmm = p.mark("tensor")
                    wo_free[wbuf] = mmm
                    p.wait("vector", mmm)
                    p.wait("vector", mxr)
                    p.emit("vector", lambda h, s_, fc=fc, bank=bank: h.tensor_tensor(
                        out=vf(arB, B_YPRE + fc * 512, 512), in0=bank[:, :],
                        in1=vf(arB, B_XRES + fc * 512, 512), op=ALU.add), inc=1)
                    mypre = p.mark("vector")
                    p.wait("scalar", mypre)
                    if sq_guard[0]:
                        p.wait("scalar", sq_guard[0])
                    p.emit("scalar", lambda h, s_, fc=fc: h.activation(
                        vf(arB, B_SQ, 512), vf(arB, B_YPRE + fc * 512, 512),
                        AF.Square), inc=1)
                    msq = p.mark("scalar")
                    p.wait("tensor", mypre)
                    p.emit("tensor", lambda h, s_, fc=fc: h.matmul(
                        psb[6][0:1, 0:512], onescol[:, :],
                        vf(arB, B_YPRE + fc * 512, 512),
                        start=(fc == 0), stop=(fc == NEC - 1)))
                    p.wait("tensor", msq)
                    p.emit("tensor", lambda h, s_, fc=fc: h.matmul(
                        psb[7][0:1, 0:512], onescol[:, :], vf(arB, B_SQ, 512),
                        start=(fc == 0), stop=(fc == NEC - 1)), inc=1)
                    sq_guard[0] = p.mark("tensor")
                m_stats = p.mark("tensor")

                p.wait("vector", m_stats)
                p.emit("vector", lambda h, s_: h.tensor_scalar(
                    out=mrow[:, :], in0=psb[6][0:1, 0:512], scalar1=1.0 / 1024.0,
                    scalar2=None, op0=ALU.mult), inc=1)
                mm1 = p.mark("vector")
                p.wait("scalar", mm1)
                p.emit("scalar", lambda h, s_: h.activation(
                    rrow[:, :], mrow[:, :], AF.Square), inc=1)
                mm2 = p.mark("scalar")
                p.wait("vector", mm2)
                p.emit("vector", lambda h, s_: h.tensor_scalar(
                    out=vrow[:, :], in0=psb[7][0:1, 0:512], scalar1=1.0 / 1024.0,
                    scalar2=None, op0=ALU.mult), inc=1)
                p.wait("vector", p.mark("vector"))
                p.emit("vector", lambda h, s_: h.tensor_tensor(
                    out=vrow[:, :], in0=vrow[:, :], in1=rrow[:, :],
                    op=ALU.subtract), inc=1)
                mm3 = p.mark("vector")
                p.wait("scalar", mm3)
                p.emit("scalar", lambda h, s_: h.activation(
                    rrow[:, :], vrow[:, :], AF.Ln, bias=eps1[:, :]), inc=1)
                p.wait("scalar", p.mark("scalar"))
                p.emit("scalar", lambda h, s_: h.activation(
                    rrow[:, :], rrow[:, :], AF.Exp, scale=-0.5), inc=1)
                mm4 = p.mark("scalar")
                p.wait("vector", mm4)
                p.wait("vector", p.mark("vector"))
                p.emit("vector", lambda h, s_: h.tensor_tensor(
                    out=mrrow[:, :], in0=mrow[:, :], in1=rrow[:, :],
                    op=ALU.mult), inc=1)
                mm5 = p.mark("vector")
                p.wait("tensor", mm5)
                p.emit("tensor", lambda h, s_: h.matmul(
                    psb[4][:, :], onesrow[:, :], rrow[:, :], start=True, stop=True))
                p.emit("tensor", lambda h, s_: h.matmul(
                    psb[5][:, :], onesrow[:, :], mrrow[:, :], start=True, stop=True),
                       inc=1)
                m_bc = p.mark("tensor")

                for fc in range(NEC):
                    yb = fc % 2
                    p.wait("vector", m_bc)
                    if ych_free[yb]:
                        p.wait("vector", ych_free[yb])
                    p.emit("vector", lambda h, s_, fc=fc: h.tensor_tensor(
                        out=vf(arB, B_TMP, 512), in0=vf(arB, B_YPRE + fc * 512, 512),
                        in1=psb[4][:, :], op=ALU.mult))
                    p.emit("vector", lambda h, s_, fc=fc, yb=yb: h.tensor_tensor(
                        out=vf(arB, B_YCH[yb], 512), in0=vf(arB, B_TMP, 512),
                        in1=psb[5][:, :], op=ALU.subtract))
                    p.emit("vector", lambda h, s_, yb=yb: h.tensor_reduce(
                        out=tiny[:, 5:6], in_=vf(arB, B_YCH[yb], 512), axis=AX.X,
                        op=ALU.max, apply_absolute_value=True), inc=1)
                    p.wait("vector", p.mark("vector"))
                    if n == 0 and fc == 0:
                        p.emit("vector", lambda h, s_: h.tensor_copy(
                            rcol[:, 0:1], tiny[:, 5:6]), inc=1)
                    else:
                        p.emit("vector", lambda h, s_: h.tensor_tensor(
                            out=rcol[:, 0:1], in0=rcol[:, 0:1],
                            in1=tiny[:, 5:6], op=ALU.max), inc=1)
                    myv = p.mark("vector")
                    p.wait("gpsimd", myv)
                    ych_free[yb] = p.dma("gpsimd", lambda h, s_, fc=fc, nsl=nsl,
                                         yb=yb: h.dma_start(
                                             out=yscr[ts(fc), nsl],
                                             in_=vf(arB, B_YCH[yb], 512)))
                    m_last_y = ych_free[yb]
                if odbg is not None:
                    modbg = ("dma_gpsimd", p.dma_counts["gpsimd"])
                    p.wait("vector", modbg)
                m_blk_v = p.mark("vector")
            return (p.mark("vector"), p.mark("tensor"), p.mark("scalar"),
                    p.mark("gpsimd"), m_last_y)

        m_a2_v, m_a2_t, m_a2_s, m_a2_g, m_a2_y = a2()

        CB_SH = [0, 2048]
        CB_SL = [4096, 6144]
        CB_ST = [8192, 9216, 10240, 11264]
        CB_E = [12288 + i * 512 for i in range(8)]
        PB_SLOT = [[0, 1024, 2048, 3072, 8192, 9216],
                   [4096, 5120, 6144, 7168, 10240, 12288]]
        CB_OH = 0
        CB_OL = 12288
        CB_Y = [12288, 12800, 13312]
        P11_ST = [12800, 13312, 13824, 14336, 14848, 15360, 15872]

        def phase_barrier(eng_list=("sync", "tensor", "vector", "scalar", "gpsimd")):
            ms = {e: p.mark(e) for e in ENGINES}
            dm = {e: ("dma_" + e, p.dma_counts[e]) for e in ENGINES}
            for e in eng_list:
                for e2 in ENGINES:
                    if e2 != e:
                        p.wait(e, ms[e2])
                    p.wait(e, dm[e2])

        def collective(slot_marker, k):
            p.wait(G, slot_marker)
            mi = p.dma(G, lambda h, s_: h.dma_start(out=ccin[:, :], in_=ccs[:, :]))
            p.wait(G, mi)
            p.emit(G, lambda h, s_: h.collective_compute(
                "AllReduce", ALU.max, replica_groups=[list(range(ncores))],
                ins=[ccin[:, :]], outs=[ccout[:, :]]), inc=1)
            mc = p.mark(G)
            p.wait(G, mc)
            mo = p.dma(G, lambda h, s_: h.dma_start(out=ccs[:, :], in_=ccout[:, :]))
            return mo

        def make_scale(k, src_col, extra_pow2=1.0):
            p.emit("vector", lambda h, s_, k=k, c=src_col: h.tensor_scalar(
                out=ccs[:, c:c + 1], in0=ccs[:, c:c + 1], scalar1=r327[0:1, :],
                scalar2=None, op0=ALU.mult), inc=1)
            p.wait("vector", p.mark("vector"))
            if extra_pow2 != 1.0:
                p.emit("vector", lambda h, s_, k=k, c=src_col, e=extra_pow2:
                       h.tensor_scalar(out=ccs[:, c:c + 1], in0=ccs[:, c:c + 1],
                                       scalar1=e, scalar2=None, op0=ALU.mult),
                       inc=1)
                p.wait("vector", p.mark("vector"))
            p.emit("vector", lambda h, s_, c=src_col: h.tensor_scalar(
                out=ccs[:, c:c + 1], in0=ccs[:, c:c + 1], scalar1=1e-12,
                scalar2=None, op0=ALU.max), inc=1)
            mv = p.mark("vector")
            p.wait("tensor", mv)
            p.emit("tensor", lambda h, s_, c=src_col: h.matmul(
                psb[6][:, 384 + c:385 + c], onesrow[:, :], ccs[:, c:c + 1],
                start=True, stop=True), inc=1)
            mt = p.mark("tensor")
            p.wait("vector", mt)
            p.emit("vector", lambda h, s_, k=k, c=src_col: h.tensor_copy(
                sccol[:, k:k + 1], psb[6][:, 384 + c:385 + c]), inc=1)
            p.wait("vector", p.mark("vector"))
            p.emit("vector", lambda h, s_, k=k: h.reciprocal(
                sicol[:, k:k + 1], sccol[:, k:k + 1]), inc=1)
            p.wait("vector", p.mark("vector"))
            return p.mark("vector")

        def phase_b():
            phase_barrier()
            p.emit("vector", lambda h, s_: h.memset(ccs[:, :], 0.0))
            p.emit("vector", lambda h, s_: h.memset(r327[:, :], 32767.0), inc=1)
            p.wait("vector", p.mark("vector"))
            p.emit("vector", lambda h, s_: h.reciprocal(r327[:, :], r327[:, :]),
                   inc=1)
            p.wait("vector", p.mark("vector"))
            p.emit("vector", lambda h, s_: h.tensor_copy(
                tiny[:, 6:7], rcol[:, 0:1]), inc=1)
            mty = p.mark("vector")
            p.wait("tensor", mty)
            p.emit("tensor", lambda h, s_: h.transpose(
                psb[7][0:1, 0:128], tiny[:, 6:7], eyef[:, :]), inc=1)
            mtt = p.mark("tensor")
            p.wait("vector", mtt)
            p.emit("vector", lambda h, s_: h.tensor_reduce(
                out=ccs[:, 0:1], in_=psb[7][0:1, 0:128], axis=AX.X, op=ALU.max),
                   inc=1)
            mred = p.mark("vector")
            mcc = collective(mred, 0)
            p.wait("vector", mcc)
            m_sx = make_scale(0, 0)

            p.wait("sync", m_sx)
            set_guard4 = [None, None]
            for it in range(32):
                ec, th = divmod(it, 4)
                st_ = it % 2
                L, T_ = PB_SLOT[st_][0], PB_SLOT[st_][1]
                if set_guard4[st_]:
                    p.wait("sync", set_guard4[st_])
                mld = p.dma("sync", lambda h, s_, ec=ec, th=th, L=L: h.dma_start(
                    out=vf(arC, L, 1024),
                    in_=yscr[ts(ec), th * 1024:(th + 1) * 1024]))
                p.wait("scalar", mld)
                p.emit("scalar", lambda h, s_, L=L, T_=T_: h.activation(
                    vf(arC, T_, 1024), vf(arC, L, 1024), AF.Identity,
                    scale=sicol[:, 0:1], bias=mcol[:, :]), inc=1)
                m1 = p.mark("scalar")
                p.wait("vector", m1)
                p.emit("vector", lambda h, s_, T_=T_: h.tensor_scalar(
                    out=vf(arC, T_, 1024), in0=vf(arC, T_, 1024),
                    scalar1=-MAGIC, scalar2=sccol[:, 0:1], op0=ALU.add,
                    op1=ALU.mult), inc=1)
                m2 = p.mark("vector")
                p.wait("scalar", m2)
                p.emit("scalar", lambda h, s_, ec=ec, th=th, T_=T_: h.copy(
                    vb(arA, ec * 4096 + th * 1024, 1024), vf(arC, T_, 1024)),
                       inc=1)
                mhi = p.mark("scalar")
                p.wait("vector", mhi)
                p.emit("vector", lambda h, s_, ec=ec, th=th, T_=T_: h.tensor_tensor(
                    out=vb(arB, ec * 4096 + th * 1024, 1024),
                    in0=vf(arC, T_, 1024),
                    in1=vb(arA, ec * 4096 + th * 1024, 1024), op=ALU.subtract),
                       inc=1)
                set_guard4[st_] = p.mark("vector")
            m_xq = set_guard4[1]

            p.wait("tensor", m_xq)
            ev_guard = [None] * 4
            bank_guard = [None, None]
            st_guard = [m_xq, m_xq]
            rix2 = [0]
            for wi, (hscr, lscr, oscr_) in enumerate(
                    ((upwh, upwl, upscr), (gawh, gawl, gascr))):
                for fg in range(NFC):
                    buf = fg % 2
                    if st_guard[buf]:
                        p.wait("sync", st_guard[buf])
                    mwh = p.dma("sync", lambda h, s_, hscr=hscr, fg=fg, buf=buf:
                                h.dma_start(out=vb(arC, CB_SH[buf], 1024),
                                            in_=hscr[fg, :, :]))
                    mwl = p.dma("sync", lambda h, s_, lscr=lscr, fg=fg, buf=buf:
                                h.dma_start(out=vb(arC, CB_SL[buf], 1024),
                                            in_=lscr[fg, :, :]))
                    p.wait("tensor", mwh)
                    p.wait("tensor", mwl)
                    for tc in range(NTC):
                        bank = psb[tc % 2]
                        if bank_guard[tc % 2]:
                            p.wait("tensor", bank_guard[tc % 2])
                        for pa, (woff, xar) in enumerate(
                                ((CB_SH[buf], arA), (CB_SH[buf], arB),
                                 (CB_SL[buf], arA))):
                            for ec in range(NEC):
                                st_ = (pa == 0 and ec == 0)
                                sp_ = (pa == 2 and ec == NEC - 1)
                                p.emit("tensor", lambda h, s_, woff=woff, xar=xar,
                                       ec=ec, tc=tc, bank=bank, st=st_, sp=sp_:
                                       h.matmul(bank[:, :],
                                                vb(arC, woff + ec * 128, 128),
                                                vb(xar, ec * 4096 + tc * 512, 512),
                                                start=st, stop=sp),
                                       inc=(1 if sp_ else 0))
                        mmm = p.mark("tensor")
                        sti = (fg * NTC + tc) % 4
                        p.wait("scalar", mmm)
                        if ev_guard[sti]:
                            p.wait("scalar", ev_guard[sti])
                        p.emit("scalar", lambda h, s_, bank=bank, sti=sti: h.copy(
                            vf(arC, CB_ST[sti], 512), bank[:, :]), inc=1)
                        mev = p.mark("scalar")
                        bank_guard[tc % 2] = mev
                        p.wait("vector", mev)
                        p.emit("vector", lambda h, s_, sti=sti: h.tensor_reduce(
                            out=tiny[:, 7:8], in_=vf(arC, CB_ST[sti], 512),
                            axis=AX.X, op=ALU.max, apply_absolute_value=True),
                               inc=1)
                        p.wait("vector", p.mark("vector"))
                        if rix2[0] == 0 or (wi == 1 and fg == 0 and tc == 0):
                            p.emit("vector", lambda h, s_, wi=wi: h.tensor_copy(
                                rcol[:, 1 + wi:2 + wi], tiny[:, 7:8]), inc=1)
                        else:
                            p.emit("vector", lambda h, s_, wi=wi: h.tensor_tensor(
                                out=rcol[:, 1 + wi:2 + wi],
                                in0=rcol[:, 1 + wi:2 + wi], in1=tiny[:, 7:8],
                                op=ALU.max), inc=1)
                        rix2[0] += 1
                        mrc = p.mark("vector")
                        p.wait("gpsimd", mrc)
                        mst = p.dma("gpsimd", lambda h, s_, oscr_=oscr_, fg=fg,
                                    tc=tc, sti=sti: h.dma_start(
                                        out=oscr_[ts(fg), tc * 512:(tc + 1) * 512],
                                        in_=vf(arC, CB_ST[sti], 512)))
                        ev_guard[sti] = mst
                    st_guard[buf] = p.mark("tensor")
            m_p5_v = p.mark("vector")

            p.wait("vector", m_p5_v)
            for wi in range(2):
                p.emit("vector", lambda h, s_, wi=wi: h.tensor_copy(
                    tiny[:, 6:7], rcol[:, 1 + wi:2 + wi]), inc=1)
                p.wait("tensor", p.mark("vector"))
                p.emit("tensor", lambda h, s_, wi=wi: h.transpose(
                    psb[7][0:1, 128 * wi:128 * wi + 128], tiny[:, 6:7],
                    eyef[:, :]), inc=1)
                mq = p.mark("tensor")
                p.wait("vector", mq)
                p.emit("vector", lambda h, s_, wi=wi: h.tensor_reduce(
                    out=ccs[:, 1 + wi:2 + wi],
                    in_=psb[7][0:1, 128 * wi:128 * wi + 128], axis=AX.X,
                    op=ALU.max), inc=1)
            mred2 = p.mark("vector")
            mcc2 = collective(mred2, 1)
            p.wait("vector", mcc2)
            make_scale(1, 1)
            m_sg = make_scale(2, 2)

            p.wait("sync", m_sg)
            for m_ in ev_guard:
                p.wait("scalar", m_)
                p.wait("vector", m_)
            set_g7 = [None, None]
            first7 = True
            for it in range(NFC * 4):
                fg, th = divmod(it, 4)
                st_ = it % 2
                S_ = PB_SLOT[st_]
                if set_g7[st_]:
                    p.wait("sync", set_g7[st_])
                mld = p.dma("sync", lambda h, s_, fg=fg, th=th, S_=S_: h.dma_start(
                    out=vf(arC, S_[1], 1024),
                    in_=gascr[ts(fg), th * 1024:(th + 1) * 1024]))
                p.wait("scalar", mld)
                p.emit("scalar", lambda h, s_, S_=S_: h.activation(
                    vf(arC, S_[3], 1024), vf(arC, S_[1], 1024), AF.Identity,
                    scale=sicol[:, 2:3], bias=mcol[:, :]), inc=1)
                m1 = p.mark("scalar")
                p.wait("vector", m1)
                p.emit("vector", lambda h, s_, S_=S_: h.tensor_scalar(
                    out=vf(arC, S_[3], 1024), in0=vf(arC, S_[3], 1024),
                    scalar1=-MAGIC, scalar2=sccol[:, 2:3], op0=ALU.add,
                    op1=ALU.mult), inc=1)
                mg1 = p.mark("vector")
                p.wait("scalar", mg1)
                p.emit("scalar", lambda h, s_, S_=S_: h.activation(
                    vf(arC, S_[4], 1024), vf(arC, S_[3], 1024), AF.Sigmoid), inc=1)
                p.wait("scalar", p.mark("scalar"))
                p.emit("scalar", lambda h, s_, S_=S_: h.activation(
                    vf(arC, S_[4], 1024), vf(arC, S_[4], 1024), AF.Identity,
                    scale=65536.0, bias=mcol[:, :]), inc=1)
                m3 = p.mark("scalar")
                p.wait("vector", m3)
                p.emit("vector", lambda h, s_, S_=S_: h.tensor_scalar(
                    out=vf(arC, S_[4], 1024), in0=vf(arC, S_[4], 1024),
                    scalar1=-MAGIC, scalar2=UMAX, op0=ALU.add, op1=ALU.min),
                       inc=1)
                p.wait("vector", p.mark("vector"))
                p.emit("vector", lambda h, s_, S_=S_: h.tensor_tensor(
                    out=vf(arC, S_[5], 1024), in0=vf(arC, S_[3], 1024),
                    in1=vf(arC, S_[4], 1024), op=ALU.mult), inc=1)
                p.wait("vector", p.mark("vector"))
                p.emit("vector", lambda h, s_, S_=S_: h.tensor_reduce(
                    out=tiny[:, 7:8], in_=vf(arC, S_[5], 1024), axis=AX.X,
                    op=ALU.max, apply_absolute_value=True), inc=1)
                p.wait("vector", p.mark("vector"))
                if first7:
                    p.emit("vector", lambda h, s_: h.tensor_copy(
                        rcol[:, 3:4], tiny[:, 7:8]), inc=1)
                    first7 = False
                else:
                    p.emit("vector", lambda h, s_: h.tensor_tensor(
                        out=rcol[:, 3:4], in0=rcol[:, 3:4],
                        in1=tiny[:, 7:8], op=ALU.max), inc=1)
                set_g7[st_] = p.mark("vector")
            prev7 = set_g7[1]

            p.wait("vector", prev7)
            p.emit("vector", lambda h, s_: h.tensor_copy(
                tiny[:, 6:7], rcol[:, 3:4]), inc=1)
            p.wait("tensor", p.mark("vector"))
            p.emit("tensor", lambda h, s_: h.transpose(
                psb[7][0:1, 0:128], tiny[:, 6:7], eyef[:, :]), inc=1)
            mq3 = p.mark("tensor")
            p.wait("vector", mq3)
            p.emit("vector", lambda h, s_: h.tensor_reduce(
                out=ccs[:, 3:4], in_=psb[7][0:1, 0:128], axis=AX.X, op=ALU.max),
                   inc=1)
            mred3 = p.mark("vector")
            mcc3 = collective(mred3, 3)
            p.wait("vector", mcc3)
            m_sg2 = make_scale(3, 3, extra_pow2=float(2.0 ** -16))
            p.wait("vector", m_sg2)
            p.emit("vector", lambda h, s_: h.tensor_scalar(
                out=sicol[:, 6:7], in0=sicol[:, 3:4], scalar1=float(2.0 ** -16),
                scalar2=None, op0=ALU.mult), inc=1)
            p.wait("vector", p.mark("vector"))
            m_sg2b = p.mark("vector")

            p.wait("sync", m_sg2b)
            set_g9 = [None, None]
            first9 = True
            for it in range(NFC * 4):
                fg, th = divmod(it, 4)
                st_ = it % 2
                S_ = PB_SLOT[st_]
                if set_g9[st_]:
                    p.wait("sync", set_g9[st_])
                mldu = p.dma("sync", lambda h, s_, fg=fg, th=th, S_=S_: h.dma_start(
                    out=vf(arC, S_[0], 1024),
                    in_=upscr[ts(fg), th * 1024:(th + 1) * 1024]))
                mldg = p.dma("sync", lambda h, s_, fg=fg, th=th, S_=S_: h.dma_start(
                    out=vf(arC, S_[1], 1024),
                    in_=gascr[ts(fg), th * 1024:(th + 1) * 1024]))
                p.wait("scalar", mldu)
                p.wait("scalar", mldg)
                p.emit("scalar", lambda h, s_, S_=S_: h.activation(
                    vf(arC, S_[2], 1024), vf(arC, S_[0], 1024), AF.Identity,
                    scale=sicol[:, 1:2], bias=mcol[:, :]), inc=1)
                m1 = p.mark("scalar")
                p.wait("vector", m1)
                p.emit("vector", lambda h, s_, S_=S_: h.tensor_scalar(
                    out=vf(arC, S_[2], 1024), in0=vf(arC, S_[2], 1024),
                    scalar1=-MAGIC, scalar2=sccol[:, 1:2], op0=ALU.add,
                    op1=ALU.mult), inc=1)
                p.emit("scalar", lambda h, s_, S_=S_: h.activation(
                    vf(arC, S_[3], 1024), vf(arC, S_[1], 1024), AF.Identity,
                    scale=sicol[:, 2:3], bias=mcol[:, :]), inc=1)
                m2 = p.mark("scalar")
                p.wait("vector", m2)
                p.emit("vector", lambda h, s_, S_=S_: h.tensor_scalar(
                    out=vf(arC, S_[3], 1024), in0=vf(arC, S_[3], 1024),
                    scalar1=-MAGIC, scalar2=sccol[:, 2:3], op0=ALU.add,
                    op1=ALU.mult), inc=1)
                mg1 = p.mark("vector")
                p.wait("scalar", mg1)
                p.emit("scalar", lambda h, s_, S_=S_: h.activation(
                    vf(arC, S_[4], 1024), vf(arC, S_[3], 1024), AF.Sigmoid), inc=1)
                p.wait("scalar", p.mark("scalar"))
                p.emit("scalar", lambda h, s_, S_=S_: h.activation(
                    vf(arC, S_[4], 1024), vf(arC, S_[4], 1024), AF.Identity,
                    scale=65536.0, bias=mcol[:, :]), inc=1)
                m3 = p.mark("scalar")
                p.wait("vector", m3)
                p.emit("vector", lambda h, s_, S_=S_: h.tensor_scalar(
                    out=vf(arC, S_[4], 1024), in0=vf(arC, S_[4], 1024),
                    scalar1=-MAGIC, scalar2=UMAX, op0=ALU.add, op1=ALU.min),
                       inc=1)
                p.wait("vector", p.mark("vector"))
                p.emit("vector", lambda h, s_, S_=S_: h.tensor_tensor(
                    out=vf(arC, S_[5], 1024), in0=vf(arC, S_[3], 1024),
                    in1=vf(arC, S_[4], 1024), op=ALU.mult), inc=1)
                mu_ = p.mark("vector")
                p.wait("scalar", mu_)
                p.emit("scalar", lambda h, s_, S_=S_: h.activation(
                    vf(arC, S_[5], 1024), vf(arC, S_[5], 1024), AF.Identity,
                    scale=sicol[:, 6:7], bias=mcol[:, :]), inc=1)
                m4 = p.mark("scalar")
                p.wait("vector", m4)
                p.emit("vector", lambda h, s_, S_=S_: h.tensor_scalar(
                    out=vf(arC, S_[5], 1024), in0=vf(arC, S_[5], 1024),
                    scalar1=-MAGIC, scalar2=sccol[:, 3:4], op0=ALU.add,
                    op1=ALU.mult), inc=1)
                p.wait("vector", p.mark("vector"))
                p.emit("vector", lambda h, s_, S_=S_: h.tensor_tensor(
                    out=vf(arC, S_[5], 1024), in0=vf(arC, S_[5], 1024),
                    in1=vf(arC, S_[2], 1024), op=ALU.mult), inc=1)
                p.wait("vector", p.mark("vector"))
                p.emit("vector", lambda h, s_, S_=S_: h.tensor_reduce(
                    out=tiny[:, 7:8], in_=vf(arC, S_[5], 1024), axis=AX.X,
                    op=ALU.max, apply_absolute_value=True), inc=1)
                p.wait("vector", p.mark("vector"))
                if first9:
                    p.emit("vector", lambda h, s_: h.tensor_copy(
                        rcol[:, 4:5], tiny[:, 7:8]), inc=1)
                    first9 = False
                else:
                    p.emit("vector", lambda h, s_: h.tensor_tensor(
                        out=rcol[:, 4:5], in0=rcol[:, 4:5],
                        in1=tiny[:, 7:8], op=ALU.max), inc=1)
                mo9 = p.mark("vector")
                p.wait("gpsimd", mo9)
                set_g9[st_] = p.dma("gpsimd", lambda h, s_, fg=fg, th=th, S_=S_:
                                    h.dma_start(
                    out=oscr[ts(fg), th * 1024:(th + 1) * 1024],
                    in_=vf(arC, S_[5], 1024)))
            prev9 = set_g9[1]

            p.wait("vector", prev9)
            p.emit("vector", lambda h, s_: h.tensor_copy(
                tiny[:, 6:7], rcol[:, 4:5]), inc=1)
            p.wait("tensor", p.mark("vector"))
            p.emit("tensor", lambda h, s_: h.transpose(
                psb[7][0:1, 0:128], tiny[:, 6:7], eyef[:, :]), inc=1)
            mq4 = p.mark("tensor")
            p.wait("vector", mq4)
            p.emit("vector", lambda h, s_: h.tensor_reduce(
                out=ccs[:, 4:5], in_=psb[7][0:1, 0:128], axis=AX.X, op=ALU.max),
                   inc=1)
            mred4 = p.mark("vector")
            mcc4 = collective(mred4, 4)
            p.wait("vector", mcc4)
            m_so = make_scale(4, 4)

            phase_barrier()
            mdn1 = p.dma("sync", lambda h, s_: h.dma_start(
                out=arA.ap()[:, 0:24576].rearrange("p (a b c) -> p a b c",
                                                   a=8, b=3),
                in_=dnwh[:, :, :, :].transpose((2, 0, 1, 3))))
            mdn2 = p.dma("sync", lambda h, s_: h.dma_start(
                out=arB.ap()[:, 0:24576].rearrange("p (a b c) -> p a b c",
                                                   a=8, b=3),
                in_=dnwl[:, :, :, :].transpose((2, 0, 1, 3))))
            oq_guard = [None] * 24
            bank_guard2 = [None, None]
            out_guard = [None, None]
            m_out_last = None
            for tc in range(NTC):
                for ic in range(NFC):
                    c = ic & 1
                    ar_, s0, s1, s2 = ((arC, P11_ST[0], P11_ST[1], P11_ST[2]),
                                       (arA, 13312, 13824, 14336))[c]
                    if oq_guard[ic]:
                        p.wait("sync", oq_guard[ic])
                    mld = p.dma("sync", lambda h, s_, ic=ic, tc=tc, ar_=ar_,
                                s0=s0: h.dma_start(
                        out=vf(ar_, s0, 512),
                        in_=oscr[ts(ic), tc * 512:(tc + 1) * 512]))
                    p.wait("scalar", mld)
                    p.emit("scalar", lambda h, s_, ar_=ar_, s0=s0, s1=s1:
                           h.activation(
                        vf(ar_, s1, 512), vf(ar_, s0, 512),
                        AF.Identity, scale=sicol[:, 4:5], bias=mcol[:, :]), inc=1)
                    mt1 = p.mark("scalar")
                    p.wait("vector", mt1)
                    p.wait("sync", mt1)
                    p.emit("vector", lambda h, s_, ar_=ar_, s1=s1, s2=s2:
                           h.tensor_scalar(
                        out=vf(ar_, s2, 512), in0=vf(ar_, s1, 512),
                        scalar1=-MAGIC, scalar2=sccol[:, 4:5], op0=ALU.add,
                        op1=ALU.mult), inc=1)
                    mt2 = p.mark("vector")
                    p.wait("scalar", mt2)
                    p.emit("scalar", lambda h, s_, ic=ic, ar_=ar_, s2=s2: h.copy(
                        vb(arC, CB_OH + ic * 512, 512), vf(ar_, s2, 512)),
                           inc=1)
                    mhi = p.mark("scalar")
                    p.wait("vector", mhi)
                    p.emit("vector", lambda h, s_, ic=ic, ar_=ar_, s2=s2:
                           h.tensor_tensor(
                        out=vb(arC, CB_OL + ic * 512, 512),
                        in0=vf(ar_, s2, 512),
                        in1=vb(arC, CB_OH + ic * 512, 512), op=ALU.subtract),
                           inc=1)
                    oq_guard[ic] = p.mark("vector")
                m_pool = p.mark("vector")
                p.wait("tensor", m_pool)
                if tc == 0:
                    p.wait("tensor", mdn1)
                    p.wait("tensor", mdn2)
                for hc in range(NEC):
                    bank = psb[4 + hc % 2]
                    if bank_guard2[hc % 2]:
                        p.wait("tensor", bank_guard2[hc % 2])
                    for pa, (war, ooff) in enumerate(
                            ((arA, CB_OH), (arA, CB_OL), (arB, CB_OH))):
                        for ic in range(NFC):
                            st_ = (pa == 0 and ic == 0)
                            sp_ = (pa == 2 and ic == NFC - 1)
                            p.emit("tensor", lambda h, s_, war=war, ooff=ooff,
                                   ic=ic, hc=hc, bank=bank, st=st_, sp=sp_:
                                   h.matmul(bank[:, :],
                                            vb(war, (hc * 24 + ic) * 128, 128),
                                            vb(arC, ooff + ic * 512, 512),
                                            start=st, stop=sp),
                                   inc=(1 if sp_ else 0))
                    mmm = p.mark("tensor")
                    myld = p.dma("sync", lambda h, s_, hc=hc, tc=tc: h.dma_start(
                        out=vf(arC, P11_ST[3], 512),
                        in_=yscr[ts(hc), tc * 512:(tc + 1) * 512]))
                    p.wait("vector", mmm)
                    p.wait("vector", myld)
                    p.emit("vector", lambda h, s_, hc=hc, bank=bank: h.tensor_tensor(
                        out=vf(arC, P11_ST[4] + 0, 512) if False else
                        vf(arB, 12288 + hc * 512, 512), in0=bank[:, :],
                        in1=vf(arC, P11_ST[3], 512), op=ALU.add), inc=1)
                    mypre = p.mark("vector")
                    p.wait("sync", mypre)
                    bank_guard2[hc % 2] = mypre
                    p.wait("scalar", mypre)
                    if sq_guard2[0]:
                        p.wait("scalar", sq_guard2[0])
                    p.emit("scalar", lambda h, s_, hc=hc: h.activation(
                        vf(arC, P11_ST[5], 512), vf(arB, 12288 + hc * 512, 512),
                        AF.Square), inc=1)
                    msq = p.mark("scalar")
                    p.wait("tensor", mypre)
                    p.emit("tensor", lambda h, s_, hc=hc: h.matmul(
                        psb[6][0:1, 0:512], onescol[:, :],
                        vf(arB, 12288 + hc * 512, 512),
                        start=(hc == 0), stop=(hc == NEC - 1)))
                    p.wait("tensor", msq)
                    p.emit("tensor", lambda h, s_, hc=hc: h.matmul(
                        psb[7][0:1, 0:512], onescol[:, :], vf(arC, P11_ST[5], 512),
                        start=(hc == 0), stop=(hc == NEC - 1)), inc=1)
                    sq_guard2[0] = p.mark("tensor")
                m_stats = p.mark("tensor")
                p.wait("vector", m_stats)
                p.emit("vector", lambda h, s_: h.tensor_scalar(
                    out=mrow[:, :], in0=psb[6][0:1, 0:512], scalar1=1.0 / 1024.0,
                    scalar2=None, op0=ALU.mult), inc=1)
                mm1 = p.mark("vector")
                p.wait("scalar", mm1)
                p.emit("scalar", lambda h, s_: h.activation(
                    rrow[:, :], mrow[:, :], AF.Square), inc=1)
                mm2 = p.mark("scalar")
                p.wait("vector", mm2)
                p.emit("vector", lambda h, s_: h.tensor_scalar(
                    out=vrow[:, :], in0=psb[7][0:1, 0:512], scalar1=1.0 / 1024.0,
                    scalar2=None, op0=ALU.mult), inc=1)
                p.wait("vector", p.mark("vector"))
                p.emit("vector", lambda h, s_: h.tensor_tensor(
                    out=vrow[:, :], in0=vrow[:, :], in1=rrow[:, :],
                    op=ALU.subtract), inc=1)
                mm3 = p.mark("vector")
                p.wait("scalar", mm3)
                p.emit("scalar", lambda h, s_: h.activation(
                    rrow[:, :], vrow[:, :], AF.Ln, bias=eps1[:, :]), inc=1)
                p.wait("scalar", p.mark("scalar"))
                p.emit("scalar", lambda h, s_: h.activation(
                    rrow[:, :], rrow[:, :], AF.Exp, scale=-0.5), inc=1)
                mm4 = p.mark("scalar")
                p.wait("vector", mm4)
                p.wait("vector", p.mark("vector"))
                p.emit("vector", lambda h, s_: h.tensor_tensor(
                    out=mrrow[:, :], in0=mrow[:, :], in1=rrow[:, :],
                    op=ALU.mult), inc=1)
                mm5 = p.mark("vector")
                p.wait("tensor", mm5)
                p.emit("tensor", lambda h, s_: h.matmul(
                    psb[2][:, :], onesrow[:, :], rrow[:, :], start=True, stop=True))
                p.emit("tensor", lambda h, s_: h.matmul(
                    psb[3][:, :], onesrow[:, :], mrrow[:, :], start=True,
                    stop=True), inc=1)
                m_bc = p.mark("tensor")
                for hc in range(NEC):
                    ob = hc % 2
                    p.wait("vector", m_bc)
                    if out_guard[ob]:
                        p.wait("vector", out_guard[ob])
                    p.emit("vector", lambda h, s_, hc=hc: h.tensor_tensor(
                        out=vf(arC, P11_ST[6], 512),
                        in0=vf(arB, 12288 + hc * 512, 512), in1=psb[2][:, :],
                        op=ALU.mult), inc=1)
                    p.wait("vector", p.mark("vector"))
                    p.emit("vector", lambda h, s_, hc=hc, ob=ob: h.tensor_tensor(
                        out=vf(arA, 12288 + ob * 512, 512),
                        in0=vf(arC, P11_ST[6], 512), in1=psb[3][:, :],
                        op=ALU.subtract), inc=1)
                    mych = p.mark("vector")
                    p.wait("gpsimd", mych)
                    out_guard[ob] = p.dma("gpsimd", lambda h, s_, hc=hc, tc=tc,
                                          ob=ob: h.dma_start(
                        out=out[ts(hc), tc * 512:(tc + 1) * 512],
                        in_=vf(arA, 12288 + ob * 512, 512)))
                    m_out_last = out_guard[ob]
                p.wait("tensor", p.mark("vector"))
            return m_out_last

        sq_guard2 = [None]
        m_b_done = phase_b()

        def dump_mdbg():
            if mdbg is None:
                return
            p.emit("vector", lambda h, s_: h.tensor_copy(
                vf(arB, B_TMP, 512), psb[2][:, :]))
            p.emit("vector", lambda h, s_: h.tensor_copy(
                vf(arB, B_YCH[0], 512), psb[3][:, :]), inc=1)
            mpv = p.mark("vector")
            p.wait(G, mpv)
            p.dma(G, lambda h, s_: h.dma_start(out=mdbg["pv2"], in_=vf(arB, B_TMP, 512)))
            p.dma(G, lambda h, s_: h.dma_start(out=mdbg["pv3"], in_=vf(arB, B_YCH[0], 512)))
            p.dma(G, lambda h, s_: h.dma_start(out=mdbg["lnx"], in_=lnx[:, :]))
            p.dma(G, lambda h, s_: h.dma_start(out=mdbg["rex"], in_=rex[:, :]))
            p.dma(G, lambda h, s_: h.dma_start(out=mdbg["ccs"], in_=ccs[:, :]))
            p.dma(G, lambda h, s_: h.dma_start(out=mdbg["v2"], in_=vf(arA, 0, 8320)))
            p.dma(G, lambda h, s_: h.dma_start(out=mdbg["k2p0"], in_=k2v(0)))
            p.dma(G, lambda h, s_: h.dma_start(out=mdbg["qp0"], in_=qv(0)))

        def finalize(extra_markers):
            for m in extra_markers:
                p.wait(G, m)
            dump_mdbg()
            for name, (odbg, src) in dbg.items():
                rows, cols = src.shape
                for r0 in range(0, rows, 128):
                    p.dma(G, lambda h, s_, odbg=odbg, src=src, r0=r0: h.dma_start(
                        out=odbg[r0:r0 + 128, :], in_=src[r0:r0 + 128, :]))
            tgt = p.dma_counts[G]
            if tgt:
                p.emit(G, lambda h, s_, t=tgt: h.wait_ge(s_["dma_gpsimd"], t))
            for e in ("sync",):
                v = p.dma_counts[e]
                if v:
                    p.emit(G, lambda h, s_, e=e, v=v: h.wait_ge(s_["dma_" + e], v))

        finalize([m_b_done])

        with nc.Block() as block:
            @block.sync
            def _(eng):
                for fn in p.ops["sync"]:
                    fn(eng, sems)

            @block.tensor
            def _(eng):
                for fn in p.ops["tensor"]:
                    fn(eng, sems)

            @block.vector
            def _(eng):
                for fn in p.ops["vector"]:
                    fn(eng, sems)

            @block.scalar
            def _(eng):
                for fn in p.ops["scalar"]:
                    fn(eng, sems)

            @block.gpsimd
            def _(eng):
                for fn in p.ops["gpsimd"]:
                    fn(eng, sems)

    return nc



def _host_inputs(L0_attn_x, L0_attn_w_in, L0_attn_w_out,
                 L0_ffn_w_gate, L0_ffn_w_up, L0_ffn_w_down):
    win_t = np.ascontiguousarray(
        L0_attn_w_in.T.reshape(8, 128, 24, 128).transpose(2, 1, 0, 3))
    wout_t = np.ascontiguousarray(
        L0_attn_w_out.T.reshape(8, 128, 8, 128).transpose(2, 1, 0, 3))
    maps = []
    for c in range(NCORES):
        maps.append(dict(
            x_t=np.ascontiguousarray(L0_attn_x[c].T),
            w_in_t=win_t, w_out_t=wout_t,
            w_gate=L0_ffn_w_gate, w_up=L0_ffn_w_up, w_down=L0_ffn_w_down,
        ))
    return maps


def kernel(L0_attn_x, L0_attn_w_in, L0_attn_w_out,
           L0_ffn_w_gate, L0_ffn_w_up, L0_ffn_w_down):
    nc = build_program()
    maps = _host_inputs(L0_attn_x, L0_attn_w_in, L0_attn_w_out,
                        L0_ffn_w_gate, L0_ffn_w_up, L0_ffn_w_down)
    res = run_bass_kernel_spmd(nc, maps, core_ids=list(range(NCORES)))
    outs = [np.ascontiguousarray(res.results[c]["out"].T) for c in range(NCORES)]
    return np.stack(outs, axis=0)


if __name__ == "__main__":
    print("megakernel builder")


# version 10
# speedup vs baseline: 1.0074x; 1.0074x over previous
import contextlib
import numpy as np
import ml_dtypes
import concourse.bass as bass
import concourse.mybir as mybir
from concourse.bass_utils import run_bass_kernel_spmd

F32 = mybir.dt.float32
BF16 = mybir.dt.bfloat16
AF = mybir.ActivationFunctionType
ALU = mybir.AluOpType
AX = mybir.AxisListType
import bass_rust as _br
RED = _br.ReduceOp

NCORES = 8
S = 4096
E = 1024
H = 16
D = 64
WIN = 512
NB = S // WIN
FF = 3072
NEC = E // 128
NFC = FF // 128
NTC = S // 512
MAGIC = 12582912.0
NEG_BIG = -3.0e38
QMAX = 32767.0
UMAX = 65535.0
EPS_LN = 1e-5
SLOPES = [float(np.float32(np.exp2(np.float64(-(h + 1) * 8.0 / H)))) for h in range(H)]

ENGINES = ("sync", "tensor", "vector", "scalar", "gpsimd")


class Prog:
    def __init__(self):
        self.ops = {e: [] for e in ENGINES}
        self.counts = {e: 0 for e in ENGINES}
        self.dma_counts = {e: 0 for e in ENGINES}

    def emit(self, eng, fn, inc=0):
        if inc:
            self.counts[eng] += inc

            def wrapped(h, sems, fn=fn, eng=eng, n=inc):
                fn(h, sems).then_inc(sems[eng], n)
            self.ops[eng].append(wrapped)
        else:
            self.ops[eng].append(fn)
        return (eng, self.counts[eng])

    def dma(self, eng, fn):
        self.dma_counts[eng] += 16
        tgt = self.dma_counts[eng]

        def wrapped(h, sems, fn=fn, eng=eng):
            fn(h, sems).then_inc(sems["dma_" + eng], 16)
        self.ops[eng].append(wrapped)
        return ("dma_" + eng, tgt)

    def mark(self, eng):
        key = eng if eng in self.counts else None
        return (eng, self.counts[eng])

    def wait(self, eng, marker):
        if marker is None:
            return
        sname, val = marker
        if val <= 0:
            return
        self.ops[eng].append(lambda h, sems, s=sname, v=val: h.wait_ge(sems[s], v))


def ts(i, n=128):
    return slice(i * n, (i + 1) * n)


def band_chunks(n):
    out = []
    for c in range(8):
        if n == 0 and c < 4:
            continue
        if n == 0:
            qlo, qhi = 128 * (c - 4), 512
        else:
            qlo, qhi = max(0, 128 * (c - 4)), min(512, 128 * (c + 1))
        out.append((c, qlo, qhi))
    return out


def build_program(debug=None, a2stop=None, ncores=NCORES):
    nc = bass.Bass("TRN2", target_bir_lowering=False)
    p = Prog()

    x_t = nc.dram_tensor("x_t", [E, S], F32, kind="ExternalInput").ap()
    w_in_t = nc.dram_tensor("w_in_t", [24, 128, 8, 128], F32, kind="ExternalInput").ap()
    w_out_t = nc.dram_tensor("w_out_t", [8, 128, 8, 128], F32, kind="ExternalInput").ap()
    w_gate = nc.dram_tensor("w_gate", [FF, E], F32, kind="ExternalInput").ap()
    w_up = nc.dram_tensor("w_up", [FF, E], F32, kind="ExternalInput").ap()
    w_down = nc.dram_tensor("w_down", [E, FF], F32, kind="ExternalInput").ap()
    out = nc.dram_tensor("out", [E, S], F32, kind="ExternalOutput").ap()

    kscr = nc.dram_tensor("kscr", [E, S], F32, kind="Internal").ap()
    qscr = nc.dram_tensor("qscr", [E, S], F32, kind="Internal").ap()
    vscr = nc.dram_tensor("vscr", [E, S], F32, kind="Internal").ap()
    yscr = nc.dram_tensor("yscr", [E, S], F32, kind="Internal").ap()
    btab = nc.dram_tensor("btab", [H, 2 * WIN], F32, kind="Internal").ap()
    wkind = "ExternalOutput" if (debug and "wdbg" in debug) else "Internal"
    upwh = nc.dram_tensor("upwh", [NFC, 128, 1024], BF16, kind=wkind).ap()
    upwl = nc.dram_tensor("upwl", [NFC, 128, 1024], BF16, kind=wkind).ap()
    gawh = nc.dram_tensor("gawh", [NFC, 128, 1024], BF16, kind="Internal").ap()
    gawl = nc.dram_tensor("gawl", [NFC, 128, 1024], BF16, kind="Internal").ap()
    dnwh = nc.dram_tensor("dnwh", [NEC, 3, 128, 1024], BF16, kind="Internal").ap()
    dnwl = nc.dram_tensor("dnwl", [NEC, 3, 128, 1024], BF16, kind="Internal").ap()
    upscr = nc.dram_tensor("upscr", [FF, S], F32, kind="Internal").ap()
    gascr = nc.dram_tensor("gascr", [FF, S], F32, kind="Internal").ap()
    oscr = nc.dram_tensor("oscr", [FF, S], F32, kind="Internal").ap()
    ccin = nc.dram_tensor("ccin", [1, 8], F32, kind="Internal").ap()
    ccout = nc.dram_tensor("ccout", [1, 8], F32, kind="Internal").ap()

    dbg_names = (debug or "").split(",") if debug else []
    mdbg = None
    if "mdbg" in dbg_names:
        mdbg = {
            "negb4": nc.dram_tensor("dbg_negb4", [128, 1], F32, kind="ExternalOutput").ap(),
            "ccs": nc.dram_tensor("dbg_ccs", [1, 8], F32, kind="ExternalOutput").ap(),
            "v2": nc.dram_tensor("dbg_v2", [128, 8320], F32, kind="ExternalOutput").ap(),
            "k2p0": nc.dram_tensor("dbg_k2p0", [66, 1024], F32, kind="ExternalOutput").ap(),
            "qp0": nc.dram_tensor("dbg_qp0", [66, 512], F32, kind="ExternalOutput").ap(),
            "ccs2": nc.dram_tensor("dbg_ccs2", [1, 8], F32, kind="ExternalOutput").ap(),
            "tiny": nc.dram_tensor("dbg_tiny", [128, 8], F32, kind="ExternalOutput").ap(),
            "rcol": nc.dram_tensor("dbg_rcol", [128, 144], F32, kind="ExternalOutput").ap(),
            "pv2": nc.dram_tensor("dbg_pv2", [128, 512], F32, kind="ExternalOutput").ap(),
            "pv3": nc.dram_tensor("dbg_pv3", [128, 512], F32, kind="ExternalOutput").ap(),
            "lnx": nc.dram_tensor("dbg_lnx", [128, 8], F32, kind="ExternalOutput").ap(),
            "rex": nc.dram_tensor("dbg_rex", [128, 8], F32, kind="ExternalOutput").ap(),
            "negb4h": nc.dram_tensor("dbg_negb4h", [128, 16], F32, kind="ExternalOutput").ap(),
            "mqf": nc.dram_tensor("dbg_mqf", [128, 8], F32, kind="ExternalOutput").ap(),
            "mkf": nc.dram_tensor("dbg_mkf", [128, 8], F32, kind="ExternalOutput").ap(),
        }
        dbg_names.remove("mdbg")
    odbg = None
    if "odbg" in dbg_names:
        odbg = nc.dram_tensor("dbg_odbg", [S, E], F32, kind="ExternalOutput").ap()
        dbg_names.remove("odbg")
    if "wdbg" in dbg_names:
        dbg_names.remove("wdbg")
    dbg = {}
    for name in dbg_names:
        src = {"kscr": kscr, "qscr": qscr, "vscr": vscr, "yscr": yscr,
               "upscr": upscr, "gascr": gascr, "oscr": oscr}[name]
        dbg[name] = (nc.dram_tensor("dbg_" + name, list(src.shape), F32,
                                    kind="ExternalOutput").ap(), src)

    est = contextlib.ExitStack()
    with est:
        def sbt(name, shape, dtype):
            return est.enter_context(nc.sbuf_tensor(name, shape, dtype))

        arA = sbt("arA", [128, 32768], BF16)
        arB = sbt("arB", [128, 32768], BF16)
        arC = sbt("arC", [128, 33536], BF16)

        def vf(ar, off, n, p0=0, p1=128):
            return ar.ap().bitcast(F32)[p0:p1, off:off + n]

        def vb(ar, off, n, p0=0, p1=128):
            return ar.ap()[p0:p1, off:off + n]

        qramp = sbt("qramp", [1, 512], F32)
        mrow = sbt("mrow", [1, 512], F32)
        vrow = sbt("vrow", [1, 512], F32)
        rrow = sbt("rrow", [1, 512], F32)
        mrrow = sbt("mrrow", [1, 512], F32)
        slopecol = sbt("slopecol", [16, 1], F32)
        s512col = sbt("s512col", [16, 1], F32)
        u20col = sbt("u20col", [128, 1], F32)
        onesrow = sbt("onesrow", [1, 128], F32)
        onescol = sbt("onescol", [128, 1], F32)
        eyebf = sbt("eyebf", [128, 128], BF16)
        eyef = sbt("eyef", [128, 128], F32)
        maskA = sbt("maskA", [128, 128], BF16)
        maskB = sbt("maskB", [128, 128], BF16)
        onetile = sbt("onetile", [128, 128], F32)
        scol = sbt("scol", [128, 64], F32)
        invcol = sbt("invcol", [128, 64], F32)
        sctp = vf(arC, 15360, 512, 0, 32)
        rcol = sbt("rcol", [128, 8], F32)
        lncol = sbt("lncol", [128, 8], F32)
        sccol = sbt("sccol", [128, 8], F32)
        sicol = sbt("sicol", [128, 8], F32)
        tiny = sbt("tiny", [128, 8], F32)
        lnx = sbt("lnx", [128, 8], F32)
        rmcol = sbt("rmcol", [128, 12], F32)
        eps1 = sbt("eps1", [1, 1], F32)
        mcol = sbt("mcol", [128, 1], F32)
        rex = sbt("rex", [128, 8], F32)
        ccs = sbt("ccs", [1, 8], F32)
        r327 = sbt("r327", [1, 1], F32)

        psb = [est.enter_context(nc.psum_tensor(f"ps{i}", [128, 512], F32))
               for i in range(8)]

        sems = {}
        for e in ENGINES:
            sems[e] = est.enter_context(nc.semaphore("sem_" + e))
            sems["dma_" + e] = est.enter_context(nc.semaphore("dsem_" + e))

        G = "gpsimd"
        p.emit(G, lambda h, s_: h.memset(onesrow[:, :], 1.0))
        p.emit(G, lambda h, s_: h.memset(onescol[:, :], 1.0))
        p.emit(G, lambda h, s_: h.memset(onetile[:, :], 1.0))
        p.emit(G, lambda h, s_: h.memset(eps1[:, :], EPS_LN))
        p.emit(G, lambda h, s_: h.memset(mcol[:, :], MAGIC))
        p.emit(G, lambda h, s_: h.memset(u20col[:, :], 20.0))
        p.emit(G, lambda h, s_: h.affine_select(
            out=eyef[:, :], in_=onetile[:, :], pattern=[[-1, 128]], base=0,
            channel_multiplier=1, compare_op=ALU.is_equal, fill=0.0))
        p.emit(G, lambda h, s_: h.affine_select(
            out=eyebf[:, :], in_=onetile[:, :], pattern=[[-1, 128]], base=0,
            channel_multiplier=1, compare_op=ALU.is_equal, fill=0.0))
        p.emit(G, lambda h, s_: h.memset(maskA[:, :], NEG_BIG))
        p.emit(G, lambda h, s_: h.affine_select(
            out=maskA[:, :], in_=maskA[:, :], pattern=[[-1, 128]], base=-1,
            channel_multiplier=1, compare_op=ALU.is_ge, fill=0.0))
        p.emit(G, lambda h, s_: h.memset(maskB[:, :], NEG_BIG))
        p.emit(G, lambda h, s_: h.affine_select(
            out=maskB[:, :], in_=maskB[:, :], pattern=[[1, 128]], base=-1,
            channel_multiplier=-1, compare_op=ALU.is_ge, fill=0.0))
        p.emit(G, lambda h, s_: h.iota(
            out=qramp[:, :], pattern=[[1, 512]], base=0, channel_multiplier=0, allow_small_or_imprecise_dtypes=True))
        p.emit(G, lambda h, s_: h.iota(
            out=vf(arC, 0, 1024, 0, 16), pattern=[[1, 1024]], base=0, channel_multiplier=0, allow_small_or_imprecise_dtypes=True))
        p.emit(G, lambda h, s_: h.iota(
            out=slopecol[:, :], pattern=[[1, 1]], base=1, channel_multiplier=1, allow_small_or_imprecise_dtypes=True),
               inc=1)
        m_g0 = p.mark(G)
        p.wait("scalar", m_g0)
        p.emit("scalar", lambda h, s_: h.activation(
            slopecol[:, :], slopecol[:, :], AF.Exp,
            scale=float(-np.log(2.0) / 2.0)), inc=1)
        m_s0 = p.mark("scalar")
        p.wait("vector", m_s0)
        p.emit("vector", lambda h, s_: h.tensor_scalar(
            out=slopecol[:, :], in0=slopecol[:, :], scalar1=-1.0, scalar2=None,
            op0=ALU.mult), inc=1)
        p.wait("vector", p.mark("vector"))
        p.emit("vector", lambda h, s_: h.tensor_scalar(
            out=s512col[:, :], in0=slopecol[:, :], scalar1=-512.0, scalar2=None,
            op0=ALU.mult), inc=1)
        p.wait("vector", p.mark("vector"))
        p.emit("vector", lambda h, s_: h.tensor_scalar(
            out=vf(arC, 0, 1024, 0, 16), in0=vf(arC, 0, 1024, 0, 16),
            scalar1=slopecol[:, :], scalar2=s512col[:, :],
            op0=ALU.mult, op1=ALU.add), inc=1)
        m_v0 = p.mark("vector")
        p.wait("sync", m_v0)
        m_btab = p.dma("sync", lambda h, s_: h.dma_start(out=btab[:, :], in_=vf(arC, 0, 1024, 0, 16)))
        consts_done = (m_g0, m_v0)

        XSTG = lambda: vf(arC, 0, 4096)
        A1_WSTG = 4096
        DQ_WSTG = 5120
        DQ_TMP = 6144
        DQ_DQ = 7168
        DQ_SREP = 8192
        DQ_IREP = 9216
        B_A1WH = [2 * 10240, 2 * 10240 + 1024]
        B_A1WL = [2 * 10240 + 2048, 2 * 10240 + 3072]
        B_DQH = 2 * 10240 + 4096
        B_DQL = 2 * 10240 + 5120
        B_TSTG = 2 * 10240 + 6144
        B_TSTG2 = 31488
        ST0 = 13696
        STG = [13696, 14208, 14720, 15232]
        wohscr = nc.dram_tensor("wohscr", [8, 128, 8, 128], BF16, kind="Internal").ap()
        wolscr = nc.dram_tensor("wolscr", [8, 128, 8, 128], BF16, kind="Internal").ap()

        last_v_use = {}
        prev_st = [None]
        e32_built = [False]

        def a1_x_split():
            p.wait("sync", m_btab)
            prev = None
            for ec in range(NEC):
                mld = p.dma("sync", lambda h, s_, ec=ec: h.dma_start(
                    out=XSTG(), in_=x_t[ts(ec), :]))
                p.wait("scalar", mld)
                if prev:
                    p.wait("scalar", prev)
                p.emit("scalar", lambda h, s_, ec=ec: h.copy(
                    vb(arA, ec * 4096, 4096), XSTG()), inc=1)
                mhi = p.mark("scalar")
                p.wait("vector", mhi)
                p.emit("vector", lambda h, s_, ec=ec: h.tensor_tensor(
                    out=vb(arB, ec * 4096, 4096), in0=XSTG(),
                    in1=vb(arA, ec * 4096, 4096), op=ALU.subtract), inc=1)
                prev = p.mark("vector")
                p.wait("sync", prev)
            return prev

        m_xsplit = a1_x_split()

        def a1_qkv():
            mm_done = None
            ev_prev = [None] * 4
            wh_free = [None, None]
            a1_bank_guard = [None, None]
            rix = [0]
            for fg in range(24):
                buf = fg % 2
                mld = p.dma("sync", lambda h, s_, fg=fg: h.dma_start(
                    out=vf(arC, A1_WSTG, 1024),
                    in_=w_in_t[fg, :, :, :].rearrange("p a b -> p (a b)")))
                p.wait("scalar", mld)
                if wh_free[buf]:
                    p.wait("scalar", wh_free[buf])
                p.emit("scalar", lambda h, s_, buf=buf: h.copy(
                    vb(arC, B_A1WH[buf], 1024), vf(arC, A1_WSTG, 1024)), inc=1)
                mhi = p.mark("scalar")
                p.wait("vector", mhi)
                p.emit("vector", lambda h, s_, buf=buf: h.tensor_tensor(
                    out=vb(arC, B_A1WL[buf], 1024), in0=vf(arC, A1_WSTG, 1024),
                    in1=vb(arC, B_A1WH[buf], 1024), op=ALU.subtract), inc=1)
                mlo = p.mark("vector")
                p.wait("sync", mlo)
                p.wait("tensor", mlo)
                if fg == 0:
                    p.wait("tensor", m_xsplit)
                for tc in range(NTC):
                    bank = psb[tc % 2]
                    if a1_bank_guard[tc % 2]:
                        p.wait("tensor", a1_bank_guard[tc % 2])
                    for pa, (wof, xar) in enumerate(
                            ((B_A1WH[buf], arA), (B_A1WH[buf], arB),
                             (B_A1WL[buf], arA))):
                        for ec in range(NEC):
                            st = (pa == 0 and ec == 0)
                            sp = (pa == 2 and ec == NEC - 1)
                            p.emit("tensor", lambda h, s_, wof=wof, xar=xar, ec=ec,
                                   tc=tc, bank=bank, st=st, sp=sp:
                                   h.matmul(bank[:, :],
                                            vb(arC, wof + ec * 128, 128),
                                            vb(xar, ec * 4096 + tc * 512, 512),
                                            start=st, stop=sp),
                                   inc=(1 if sp else 0))
                    mmm = p.mark("tensor")
                    sti = (fg * NTC + tc) % 4
                    p.wait("scalar", mmm)
                    if ev_prev[sti]:
                        p.wait("scalar", ev_prev[sti])
                    if fg < 8:
                        p.emit("scalar", lambda h, s_, bank=bank, sti=sti: h.activation(
                            vf(arC, STG[sti], 512), bank[:, :], AF.Identity,
                            scale=0.125), inc=1)
                    else:
                        p.emit("scalar", lambda h, s_, bank=bank, sti=sti: h.copy(
                            vf(arC, STG[sti], 512), bank[:, :]), inc=1)
                    mev = p.mark("scalar")
                    a1_bank_guard[tc % 2] = mev
                    p.wait("gpsimd", mev)
                    scr = qscr if fg < 8 else (kscr if fg < 16 else vscr)
                    row = (fg % 8) * 128
                    mst = p.dma("gpsimd", lambda h, s_, scr=scr, row=row, tc=tc, sti=sti:
                                h.dma_start(out=scr[row:row + 128, tc * 512:(tc + 1) * 512],
                                            in_=vf(arC, STG[sti], 512)))
                    ev_prev[sti] = mst
                wh_free[buf] = p.mark("tensor")
            return p.mark("tensor"), p.mark("sync"), ev_prev, rix[0]

        m_qkv_t, m_qkv_dma, a1_ev_prev, n_rcol_qk = a1_qkv()

        def wout_split():
            prevs = None
            p.wait("scalar", m_qkv_t)
            for fg in range(8):
                mld = p.dma("sync", lambda h, s_, fg=fg: h.dma_start(
                    out=vf(arC, A1_WSTG, 1024),
                    in_=w_out_t[fg, :, :, :].rearrange("p a b -> p (a b)")))
                p.wait("scalar", mld)
                if prevs:
                    p.wait("scalar", prevs)
                p.emit("scalar", lambda h, s_: h.copy(
                    vb(arC, B_A1WH[0], 1024), vf(arC, A1_WSTG, 1024)), inc=1)
                mhi = p.mark("scalar")
                p.wait("vector", mhi)
                p.emit("vector", lambda h, s_: h.tensor_tensor(
                    out=vb(arC, B_A1WL[0], 1024), in0=vf(arC, A1_WSTG, 1024),
                    in1=vb(arC, B_A1WH[0], 1024), op=ALU.subtract), inc=1)
                mlo = p.mark("vector")
                p.wait("sync", mlo)
                m1 = p.dma("sync", lambda h, s_, fg=fg: h.dma_start(
                    out=wohscr[fg, :, :, :].rearrange("p a b -> p (a b)"),
                    in_=vb(arC, B_A1WH[0], 1024)))
                m2 = p.dma("sync", lambda h, s_, fg=fg: h.dma_start(
                    out=wolscr[fg, :, :, :].rearrange("p a b -> p (a b)"),
                    in_=vb(arC, B_A1WL[0], 1024)))
                prevs = m2
                p.wait("scalar", m2)
            return m2

        m_wout = wout_split()

        def dequant_all():
            E32 = vf(arC, 0, 1024)
            p.wait("gpsimd", m_qkv_t)
            p.emit(G, lambda h, s_: h.memset(E32, 1.0))
            p.emit(G, lambda h, s_: h.affine_select(
                out=E32, in_=E32, pattern=[[1, 1024]], base=0,
                channel_multiplier=-32, compare_op=ALU.is_ge, fill=0.0))
            p.emit(G, lambda h, s_: h.affine_select(
                out=E32, in_=E32, pattern=[[-1, 1024]], base=31,
                channel_multiplier=32, compare_op=ALU.is_ge, fill=0.0),
                   inc=1)
            p.wait("tensor", p.mark("gpsimd"))
            for m_ in a1_ev_prev:
                p.wait("tensor", m_)

            def SW(st):
                return 1024 + st * 3584

            def ST(st):
                return SW(st) + 1024

            def SHB(st):
                return 2 * (SW(st) + 2048)

            def SLB(st):
                return SHB(st) + 1024

            def SSTG(st):
                return SHB(st) + 2048

            w_free = [None] * 4
            hl_free = [None] * 4
            stg_free = [None] * 4
            sc_free = [None, None]
            sctp_free = [None, None]
            tr_free = [None, None]
            it = 0
            for w_ap, hscr, lscr, nog, nic in (
                    (w_up, upwh, upwl, 24, 8), (w_gate, gawh, gawl, 24, 8),
                    (w_down, dnwh, dnwl, 8, 24)):
                for og in range(nog):
                    for ip in range(nic // 8):
                        st = it % 4
                        pr = it % 2
                        b0 = 4 * pr
                        it += 1
                        i0_ = ip * 1024
                        if w_free[st]:
                            p.wait("sync", w_free[st])
                        mld = p.dma("sync", lambda h, s_, w_ap=w_ap, og=og,
                                    i0_=i0_, st=st: h.dma_start(
                            out=vf(arC, SW(st), 1024),
                            in_=w_ap[ts(og), i0_:i0_ + 1024]))
                        p.wait("vector", mld)
                        if sc_free[pr]:
                            p.wait("vector", sc_free[pr])
                        p.emit("vector", lambda h, s_, st=st, pr=pr:
                               h.tensor_reduce(
                            out=scol[:, 32 * pr:32 * pr + 32],
                            in_=vf(arC, SW(st), 1024).rearrange(
                                "p (b w) -> p b w", w=32),
                            axis=AX.X, op=ALU.max, apply_absolute_value=True),
                               inc=1)
                        p.wait("vector", p.mark("vector"))
                        p.emit("vector", lambda h, s_, pr=pr: h.tensor_scalar(
                            out=scol[:, 32 * pr:32 * pr + 32],
                            in0=scol[:, 32 * pr:32 * pr + 32],
                            scalar1=float(np.float32(1.0) / np.float32(7.0)),
                            scalar2=1e-12, op0=ALU.mult, op1=ALU.max), inc=1)
                        p.wait("vector", p.mark("vector"))
                        p.emit("vector", lambda h, s_, pr=pr: h.reciprocal(
                            invcol[:, 32 * pr:32 * pr + 32],
                            scol[:, 32 * pr:32 * pr + 32]), inc=1)
                        msc = p.mark("vector")
                        p.wait("tensor", msc)
                        if tr_free[pr]:
                            p.wait("tensor", tr_free[pr])
                        p.emit("tensor", lambda h, s_, pr=pr, b0=b0: h.transpose(
                            psb[b0][0:32, 0:128], scol[:, 32 * pr:32 * pr + 32],
                            eyef[:, :]))
                        p.emit("tensor", lambda h, s_, pr=pr, b0=b0: h.transpose(
                            psb[b0][0:32, 128:256],
                            invcol[:, 32 * pr:32 * pr + 32], eyef[:, :]), inc=1)
                        mtr0 = p.mark("tensor")
                        sc_free[pr] = mtr0
                        p.wait("scalar", mtr0)
                        if sctp_free[pr]:
                            p.wait("scalar", sctp_free[pr])
                        p.emit("scalar", lambda h, s_, pr=pr, b0=b0: h.copy(
                            sctp[0:32, 256 * pr:256 * pr + 256],
                            psb[b0][0:32, 0:256]), inc=1)
                        mevs = p.mark("scalar")
                        p.wait("tensor", mevs)
                        for hlf in range(2):
                            p.emit("tensor", lambda h, s_, hlf=hlf, pr=pr, b0=b0,
                                   E32=E32: h.matmul(
                                psb[b0 + hlf][:, :],
                                sctp[0:32, 256 * pr + 128:256 * pr + 256],
                                E32[0:32, 512 * hlf:512 * hlf + 512],
                                start=True, stop=True))
                            p.emit("tensor", lambda h, s_, hlf=hlf, pr=pr, b0=b0,
                                   E32=E32: h.matmul(
                                psb[b0 + 2 + hlf][:, :],
                                sctp[0:32, 256 * pr:256 * pr + 128],
                                E32[0:32, 512 * hlf:512 * hlf + 512],
                                start=True, stop=True),
                                   inc=(1 if hlf == 1 else 0))
                        mrep = p.mark("tensor")
                        sctp_free[pr] = mrep
                        p.wait("vector", mrep)
                        for hlf in range(2):
                            p.emit("vector", lambda h, s_, st=st, hlf=hlf, b0=b0:
                                   h.tensor_tensor(
                                out=vf(arC, ST(st) + 512 * hlf, 512),
                                in0=vf(arC, SW(st) + 512 * hlf, 512),
                                in1=psb[b0 + hlf][:, :], op=ALU.mult),
                                   inc=(1 if hlf == 1 else 0))
                        w_free[st] = p.mark("vector")
                        p.emit("vector", lambda h, s_, st=st: h.tensor_scalar(
                            out=vf(arC, ST(st), 1024), in0=vf(arC, ST(st), 1024),
                            scalar1=MAGIC, scalar2=None, op0=ALU.add), inc=1)
                        p.wait("vector", p.mark("vector"))
                        for hlf in range(2):
                            p.emit("vector", lambda h, s_, st=st, hlf=hlf, b0=b0:
                                   h.scalar_tensor_tensor(
                                out=vf(arC, ST(st) + 512 * hlf, 512),
                                in0=vf(arC, ST(st) + 512 * hlf, 512),
                                scalar=-MAGIC, in1=psb[b0 + 2 + hlf][:, :],
                                op0=ALU.add, op1=ALU.mult),
                                   inc=(1 if hlf == 1 else 0))
                        mdq = p.mark("vector")
                        p.wait("scalar", mdq)
                        if hl_free[st]:
                            p.wait("scalar", hl_free[st])
                        p.emit("scalar", lambda h, s_, st=st: h.copy(
                            vb(arC, SHB(st), 1024), vf(arC, ST(st), 1024)),
                               inc=1)
                        mhi = p.mark("scalar")
                        p.wait("vector", mhi)
                        p.emit("vector", lambda h, s_, st=st: h.tensor_tensor(
                            out=vb(arC, SLB(st), 1024), in0=vf(arC, ST(st), 1024),
                            in1=vb(arC, SHB(st), 1024), op=ALU.subtract), inc=1)
                        mlo = p.mark("vector")
                        bview = psb[b0].ap().bitcast(BF16)
                        p.wait("tensor", mlo)
                        mevp = None
                        for part, boff in enumerate((SHB(st), SLB(st))):
                            if part == 1:
                                p.wait("tensor", mevp)
                            for jj in range(8):
                                p.emit("tensor", lambda h, s_, boff=boff, jj=jj,
                                       bview=bview: h.transpose(
                                    bview[:, jj * 128:jj * 128 + 128],
                                    vb(arC, boff + jj * 128, 128),
                                    eyebf[:, :]), inc=(1 if jj == 7 else 0))
                            mtrp = p.mark("tensor")
                            p.wait("scalar", mtrp)
                            if part == 0 and stg_free[st]:
                                p.wait("scalar", stg_free[st])
                            if part == 1:
                                p.wait("scalar", mdh)
                            p.emit("scalar", lambda h, s_, st=st, bview=bview:
                                   h.copy(vb(arC, SSTG(st), 1024),
                                          bview[:, 0:1024]), inc=1)
                            mevp = p.mark("scalar")
                            if part == 1:
                                tr_free[pr] = mevp
                                hl_free[st] = mtrp
                            p.wait("sync", mevp)
                            dscr = hscr if part == 0 else lscr
                            md = p.dma("sync", lambda h, s_, dscr=dscr,
                                       og=og, i0_=i0_, st=st, nic=nic:
                                       h.dma_start(
                                out=(dscr[og, :, :] if nic == 8
                                     else dscr[og, i0_ // 1024, :, :]),
                                in_=vb(arC, SSTG(st), 1024)))
                            if part == 0:
                                mdh = md
                            else:
                                stg_free[st] = md
            return ("dma_sync", p.dma_counts["sync"])

        m_dq_dn = dequant_all()

        A_V2, A_OT = 0, 8320
        B_P1, B_XRES, B_YPRE = 0, 5120, 9216
        B_SQ, B_TMP = 13312, 13824
        B_YCH = [14336, 14848]
        C_K2 = [0, 1024, 2048, 3072]
        C_QP = [4096, 4608, 5120, 5632]
        C_OTH, C_OTL = 12288, 20480
        C_WOH = [28672, 29696]
        C_WOL = [30720, 31744]
        PW = [128, 256, 384, 512, 512, 384, 256, 128]
        POFF = [0]
        for w_ in PW:
            POFF.append(POFF[-1] + w_)

        def k2v(i):
            return vf(arC, C_K2[i], 1024, 0, 66)

        def qv(i):
            return vf(arC, C_QP[i], 512, 0, 66)

        def p1v(par, c):
            return vf(arB, B_P1 + par * 2560 + POFF[c], PW[c])

        def a2(a2stop=a2stop):
            p.wait("gpsimd", m_qkv_t)
            p.wait("gpsimd", m_dq_dn)
            p.wait("gpsimd", m_wout)
            ones_ap = vf(arA, A_V2, 8320).rearrange(
                "p (c h o) -> p c h o", h=16, o=65)[:, :, :, 64:65]
            p.emit("gpsimd", lambda h, s_: h.memset(ones_ap, 1.0), inc=1)
            m_v2ones = p.mark("gpsimd")
            p.wait("vector", m_qkv_t)
            p.wait("vector", m_dq_dn)
            p.wait("vector", m_wout)
            for i in range(4):
                p.emit("vector", lambda h, s_, i=i: h.memset(k2v(i)[0:1, :], 1.0))
                p.emit("vector", lambda h, s_, i=i: h.memset(qv(i)[0:2, :], 1.0),
                       inc=(1 if i == 3 else 0))
            m_v2ones_v = p.mark("vector")
            p.wait("sync", m_v2ones)
            p.wait("sync", m_v2ones_v)

            pool_free = [None] * 4
            p1_free = [None, None]
            sc_bank_free = [None, None]
            pv_bank_free = [None, None]
            pv_evac_done = [None, None]
            ot_free = None
            wo_free = [None, None]
            ych_free = [None, None]
            ymax_count = [0]
            m_last_y = None
            m_blk_v = None
            sq_guard = [None]
            mx_guard = [None]
            rmx_guard = [None]

            for n in range(NB):
                nsl = slice(n * 512, (n + 1) * 512)
                half = n % 2
                if ot_free:
                    p.wait("sync", ot_free)
                if n == 0:
                    p.wait("sync", m_dq_dn)
                mvl = p.dma("sync", lambda h, s_, nsl=nsl: h.dma_start(
                    out=vf(arA, A_OT, 4096).rearrange("p (a b) -> p a b", a=8),
                    in_=vscr[:, nsl].rearrange("(a p) b -> p a b", p=128)))
                p.wait("tensor", mvl)
                if n == 0:
                    p.wait("tensor", m_v2ones)
                    p.wait("tensor", m_v2ones_v)
                if m_blk_v:
                    p.wait("tensor", m_blk_v)
                for ec in range(NEC):
                    for tsub in range(4):
                        bank = psb[4 + tsub % 2]
                        phys = half * 4 + tsub
                        p.emit("tensor", lambda h, s_, ec=ec, tsub=tsub, bank=bank:
                               h.transpose(bank[:, 0:128],
                                           vf(arA, A_OT + ec * 512 + tsub * 128, 128),
                                           eyef[:, :]), inc=1)
                        mtr = p.mark("tensor")
                        p.wait("scalar", mtr)
                        dst = vf(arA, A_V2 + phys * 1040 + 2 * ec * 65, 130).rearrange(
                            "p (a b) -> p a b", a=2)[:, :, 0:64]
                        p.emit("scalar", lambda h, s_, bank=bank, dst=dst: h.copy(
                            dst, bank[:, 0:128].rearrange("p (a b) -> p a b", a=2)),
                               inc=1)
                        mev = p.mark("scalar")
                        p.wait("tensor", mev)
                m_v2 = p.mark("scalar")
                if a2stop == "v2":
                    return p.mark("vector"), None

                chunks = band_chunks(n)
                pv_state = [None]

                def emit_pv(stv):
                    (hh, gi, slot, par, exp_last) = stv
                    half = n % 2
                    pvb = psb[2 + gi % 2]
                    p.wait("tensor", exp_last)
                    if pv_evac_done[gi % 2]:
                        p.wait("tensor", pv_evac_done[gi % 2])
                    for j in range(4):
                        cs = [c for (c, ql, qh) in chunks
                              if ql <= 128 * j and 128 * j + 128 <= qh]
                        for k_, c in enumerate(cs):
                            qlo = [ql for (cc, ql, qh) in chunks if cc == c][0]
                            phys = (half * 4 + (c - 4)) if c >= 4 else \
                                   (((n - 1) % 2) * 4 + c)
                            p.emit("tensor", lambda h, s_, par=par, c=c, j=j,
                                   qlo=qlo, phys=phys, hh=hh, pvb=pvb, k_=k_,
                                   last=(k_ == len(cs) - 1):
                                   h.matmul(pvb[:, 65 * j:65 * j + 65],
                                            p1v(par, c)[:, 128 * j - qlo:
                                                        128 * j - qlo + 128],
                                            vf(arA, A_V2 + phys * 1040 + 65 * hh, 65),
                                            start=(k_ == 0), stop=last),
                                   inc=(1 if (j == 3 and k_ == len(cs) - 1) else 0))
                    m_pv = p.mark("tensor")
                    p1_free[par] = m_pv

                    p.wait("scalar", m_pv)
                    sums = pvb.ap()[:, 64:324].rearrange(
                        "p (a b) -> p a b", b=65)[:, :, 0]
                    p.emit("scalar", lambda h, s_, sums=sums, par=par: h.activation(
                        lnx[:, 4 * par:4 * par + 4], sums, AF.Ln), inc=1)
                    p.wait("scalar", p.mark("scalar"))
                    p.emit("scalar", lambda h, s_, par=par: h.activation(
                        rex[:, 4 * par:4 * par + 4], lnx[:, 4 * par:4 * par + 4],
                        AF.Exp, scale=-1.0), inc=1)
                    m_rex = p.mark("scalar")
                    p.wait("vector", m_rex)
                    for j in range(4):
                        p.emit("vector", lambda h, s_, j=j, hh=hh, pvb=pvb, par=par:
                               h.tensor_scalar(
                                   out=vf(arA, A_OT + j * 1024 + 64 * hh, 64),
                                   in0=pvb[:, 65 * j:65 * j + 64],
                                   scalar1=rex[:, 4 * par + j:4 * par + j + 1],
                                   scalar2=None,
                                   op0=ALU.mult), inc=(1 if j == 3 else 0))
                    pv_evac_done[gi % 2] = p.mark("vector")
                    pool_free[slot] = m_pv


                for ph in range(H + 1):
                    if ph < H:
                        hh = ph
                        gi = n * H + hh
                        slot = gi % 4
                        par = gi % 2
                        if pool_free[slot]:
                            p.wait("sync", pool_free[slot])
                        if n == 0:
                            mk = p.dma("sync", lambda h, s_, hh=hh, slot=slot:
                                       h.dma_start(out=k2v(slot)[2:66, 512:1024],
                                                   in_=kscr[64 * hh:64 * hh + 64, 0:512]))
                        else:
                            mk = p.dma("sync", lambda h, s_, hh=hh, n=n, slot=slot:
                                       h.dma_start(out=k2v(slot)[2:66, :],
                                                   in_=kscr[64 * hh:64 * hh + 64,
                                                            (n - 1) * 512:(n + 1) * 512]))
                        mb = p.dma("sync", lambda h, s_, hh=hh, slot=slot: h.dma_start(
                            out=k2v(slot)[1:2, :], in_=btab[hh:hh + 1, :]))
                        mq = p.dma("sync", lambda h, s_, hh=hh, nsl=nsl, slot=slot:
                                   h.dma_start(out=qv(slot)[2:66, :],
                                               in_=qscr[64 * hh:64 * hh + 64, nsl]))
                        p.wait("scalar", mq)
                        p.emit("scalar", lambda h, s_, hh=hh, slot=slot: h.activation(
                            qv(slot)[0:1, :], qramp[:, :], AF.Identity,
                            scale=float(SLOPES[hh])), inc=1)
                        m_arow = p.mark("scalar")

                        p.wait("tensor", mk)
                        p.wait("tensor", mb)
                        p.wait("tensor", mq)
                        p.wait("tensor", m_arow)
                        if gi == 0:
                            p.wait("tensor", m_v2)
                        if mx_guard[0]:
                            p.wait("tensor", mx_guard[0])
                        red_marks = []
                        for j in range(4):
                            bA = psb[6 if j % 2 == 0 else 4]
                            bB = psb[7 if j % 2 == 0 else 5]
                            if j >= 2:
                                p.wait("tensor", red_marks[j - 2])
                            elif mx_guard[0]:
                                p.wait("tensor", mx_guard[0])
                            if n == 0:
                                wA = 128 * (j + 1)
                                p.emit("tensor", lambda h, s_, slot=slot, j=j, wA=wA,
                                       bA=bA:
                                       h.matmul(bA[:, 0:wA],
                                                qv(slot)[0:66, 128 * j:128 * j + 128],
                                                k2v(slot)[0:66, 512:512 + wA],
                                                start=True, stop=False))
                                p.emit("tensor", lambda h, s_, j=j, bA=bA: h.matmul(
                                    bA[:, 128 * j:128 * j + 128], maskA[:, :],
                                    eyebf[:, :], start=False, stop=True), inc=1)
                                mT = p.mark("tensor")
                                p.wait("vector", mT)
                                p.emit("vector", lambda h, s_, j=j, wA=wA, bA=bA:
                                       h.tensor_reduce(out=rmcol[:, 8 + j:9 + j],
                                                       in_=bA[:, 0:wA], axis=AX.X,
                                                       op=ALU.max), inc=1)
                                red_marks.append(p.mark("vector"))
                            else:
                                p.emit("tensor", lambda h, s_, slot=slot, j=j, bA=bA:
                                       h.matmul(bA[:, 0:512],
                                                qv(slot)[0:66, 128 * j:128 * j + 128],
                                                k2v(slot)[0:66, 128 * j:128 * j + 512],
                                                start=True, stop=False))
                                p.emit("tensor", lambda h, s_, bA=bA: h.matmul(
                                    bA[:, 0:128], maskB[:, :], eyebf[:, :],
                                    start=False, stop=True))
                                p.emit("tensor", lambda h, s_, slot=slot, j=j, bB=bB:
                                       h.matmul(bB[:, 0:128],
                                                qv(slot)[0:66, 128 * j:128 * j + 128],
                                                k2v(slot)[0:66, 128 * j + 512:
                                                          128 * j + 640],
                                                start=True, stop=False))
                                p.emit("tensor", lambda h, s_, bB=bB: h.matmul(
                                    bB[:, 0:128], maskA[:, :], eyebf[:, :],
                                    start=False, stop=True), inc=1)
                                mT = p.mark("tensor")
                                p.wait("vector", mT)
                                p.emit("vector", lambda h, s_, j=j, bA=bA:
                                       h.tensor_reduce(
                                           out=rmcol[:, j:j + 1], in_=bA[:, 0:512],
                                           axis=AX.X, op=ALU.max))
                                p.emit("vector", lambda h, s_, j=j, bB=bB:
                                       h.tensor_reduce(
                                           out=rmcol[:, 4 + j:5 + j], in_=bB[:, 0:128],
                                           axis=AX.X, op=ALU.max), inc=1)
                                red_marks.append(p.mark("vector"))
                        if n > 0:
                            p.wait("vector", red_marks[-1])
                            p.wait("vector", p.mark("vector"))
                            for j in range(4):
                                p.emit("vector", lambda h, s_, j=j: h.tensor_tensor(
                                    out=rmcol[:, 8 + j:9 + j], in0=rmcol[:, j:j + 1],
                                    in1=rmcol[:, 4 + j:5 + j], op=ALU.max),
                                       inc=(1 if j == 3 else 0))

                    if ph > 0:
                        emit_pv(pv_state[0])
                    if ph < H:
                        m_comb = p.mark("vector")
                        p.wait("tensor", m_comb)
                        if rmx_guard[0]:
                            p.wait("tensor", rmx_guard[0])
                        for j in range(4):
                            p.emit("tensor", lambda h, s_, j=j: h.transpose(
                                psb[7][0:1, 128 * j:128 * j + 128],
                                rmcol[:, 8 + j:9 + j], eyef[:, :]),
                                   inc=(1 if j == 3 else 0))
                        m_rmt = p.mark("tensor")
                        p.wait("vector", m_rmt)
                        p.emit("vector", lambda h, s_, hh=hh, slot=slot:
                               h.scalar_tensor_tensor(
                                   out=qv(slot)[0:1, :], in0=qramp[:, :],
                                   scalar=float(SLOPES[hh]), in1=psb[7][0:1, 0:512],
                                   op0=ALU.mult, op1=ALU.subtract), inc=1)
                        m_row0 = p.mark("vector")
                        rmx_guard[0] = m_row0
                        mx_guard[0] = m_row0

                        p.wait("tensor", m_row0)
                        exp_last = None
                        for ci, (c, qlo, qhi) in enumerate(chunks):
                            bank = psb[ci % 2]
                            W = qhi - qlo
                            if sc_bank_free[ci % 2]:
                                p.wait("tensor", sc_bank_free[ci % 2])
                            diag = (n > 0 and c <= 3)
                            p.emit("tensor", lambda h, s_, slot=slot, c=c, qlo=qlo,
                                   qhi=qhi, bank=bank, W=W:
                                   h.matmul(bank[:, 0:W], k2v(slot)[0:66, ts(c)],
                                            qv(slot)[0:66, qlo:qhi],
                                            start=True, stop=False))
                            mcol = 128 * c if diag else 0
                            msk = maskA if diag else maskB
                            p.emit("tensor", lambda h, s_, bank=bank, mcol=mcol, msk=msk:
                                   h.matmul(bank[:, mcol:mcol + 128], msk[:, :],
                                            eyebf[:, :], start=False, stop=True),
                                   inc=1)
                            msc = p.mark("tensor")
                            p.wait("scalar", msc)
                            if p1_free[par]:
                                p.wait("scalar", p1_free[par])
                            p.emit("scalar", lambda h, s_, par=par, c=c, bank=bank,
                                   W=W, hh=hh:
                                   h.activation(p1v(par, c)[:, 0:W], bank[:, 0:W],
                                                AF.Exp, bias=u20col[:, :]), inc=1)
                            sc_bank_free[ci % 2] = p.mark("scalar")
                            exp_last = sc_bank_free[ci % 2]
                        p1_free[par] = None


                        pv_state[0] = (hh, gi, slot, par, exp_last)
                m_heads_v = p.mark("vector")
                if odbg is not None:
                    p.wait("gpsimd", m_heads_v)
                    for j in range(4):
                        p.dma("gpsimd", lambda h, s_, n=n, j=j: h.dma_start(
                            out=odbg[n * 512 + j * 128:n * 512 + j * 128 + 128, :],
                            in_=vf(arA, A_OT + j * 1024, 1024)))

                p.wait("tensor", m_heads_v)
                for ec in range(NEC):
                    bank = psb[4 + ec % 2]
                    for j in range(4):
                        p.emit("tensor", lambda h, s_, ec=ec, j=j, bank=bank:
                               h.transpose(bank[:, ts(j)],
                                           vf(arA, A_OT + j * 1024 + ts(ec).start, 128),
                                           eyef[:, :]), inc=(1 if j == 3 else 0))
                    mtr = p.mark("tensor")
                    p.wait("scalar", mtr)
                    p.emit("scalar", lambda h, s_, ec=ec, bank=bank: h.copy(
                        vb(arC, C_OTH + ec * 512, 512), bank[:, 0:512]), inc=1)
                    mhi = p.mark("scalar")
                    p.wait("vector", mhi)
                    p.emit("vector", lambda h, s_, ec=ec, bank=bank: h.tensor_tensor(
                        out=vb(arC, C_OTL + ec * 512, 512), in0=bank[:, 0:512],
                        in1=vb(arC, C_OTH + ec * 512, 512), op=ALU.subtract), inc=1)
                    mlo = p.mark("vector")
                    p.wait("tensor", mlo)
                ot_free = p.mark("tensor")
                m_osplit = p.mark("vector")
                if a2stop == "osplit":
                    return p.mark("vector"), None

                if m_blk_v:
                    p.wait("sync", m_blk_v)
                mxr = p.dma("sync", lambda h, s_, nsl=nsl: h.dma_start(
                    out=vf(arB, B_XRES, 4096).rearrange("p (a b) -> p a b", a=8),
                    in_=x_t[:, nsl].rearrange("(a p) b -> p a b", p=128)))

                for fc in range(NEC):
                    wbuf = fc % 2
                    if wo_free[wbuf]:
                        p.wait("sync", wo_free[wbuf])
                    mwh = p.dma("sync", lambda h, s_, fc=fc, wbuf=wbuf: h.dma_start(
                        out=vb(arC, C_WOH[wbuf], 1024),
                        in_=wohscr[fc, :, :, :].rearrange("p a b -> p (a b)")))
                    mwl = p.dma("sync", lambda h, s_, fc=fc, wbuf=wbuf: h.dma_start(
                        out=vb(arC, C_WOL[wbuf], 1024),
                        in_=wolscr[fc, :, :, :].rearrange("p a b -> p (a b)")))
                    bank = psb[4 + fc % 2]
                    p.wait("tensor", mwh)
                    p.wait("tensor", mwl)
                    if fc == 0:
                        p.wait("tensor", m_osplit)
                        p.wait("tensor", m_wout)
                    for pa, (woff, ooff) in enumerate(
                            ((C_WOH[wbuf], C_OTH), (C_WOH[wbuf], C_OTL),
                             (C_WOL[wbuf], C_OTH))):
                        for ec in range(NEC):
                            st_ = (pa == 0 and ec == 0)
                            sp_ = (pa == 2 and ec == NEC - 1)
                            p.emit("tensor", lambda h, s_, woff=woff, ooff=ooff,
                                   ec=ec, bank=bank, st=st_, sp=sp_:
                                   h.matmul(bank[:, :],
                                            vb(arC, woff + ec * 128, 128),
                                            vb(arC, ooff + ec * 512, 512),
                                            start=st, stop=sp),
                                   inc=(1 if sp_ else 0))
                    mmm = p.mark("tensor")
                    wo_free[wbuf] = mmm
                    p.wait("vector", mmm)
                    p.wait("vector", mxr)
                    p.emit("vector", lambda h, s_, fc=fc, bank=bank: h.tensor_tensor(
                        out=vf(arB, B_YPRE + fc * 512, 512), in0=bank[:, :],
                        in1=vf(arB, B_XRES + fc * 512, 512), op=ALU.add), inc=1)
                    mypre = p.mark("vector")
                    p.wait("scalar", mypre)
                    if sq_guard[0]:
                        p.wait("scalar", sq_guard[0])
                    p.emit("scalar", lambda h, s_, fc=fc: h.activation(
                        vf(arB, B_SQ, 512), vf(arB, B_YPRE + fc * 512, 512),
                        AF.Square), inc=1)
                    msq = p.mark("scalar")
                    p.wait("tensor", mypre)
                    p.emit("tensor", lambda h, s_, fc=fc: h.matmul(
                        psb[6][0:1, 0:512], onescol[:, :],
                        vf(arB, B_YPRE + fc * 512, 512),
                        start=(fc == 0), stop=(fc == NEC - 1)))
                    p.wait("tensor", msq)
                    p.emit("tensor", lambda h, s_, fc=fc: h.matmul(
                        psb[7][0:1, 0:512], onescol[:, :], vf(arB, B_SQ, 512),
                        start=(fc == 0), stop=(fc == NEC - 1)), inc=1)
                    sq_guard[0] = p.mark("tensor")
                m_stats = p.mark("tensor")

                p.wait("vector", m_stats)
                p.emit("vector", lambda h, s_: h.tensor_scalar(
                    out=mrow[:, :], in0=psb[6][0:1, 0:512], scalar1=1.0 / 1024.0,
                    scalar2=None, op0=ALU.mult), inc=1)
                mm1 = p.mark("vector")
                p.wait("scalar", mm1)
                p.emit("scalar", lambda h, s_: h.activation(
                    rrow[:, :], mrow[:, :], AF.Square), inc=1)
                mm2 = p.mark("scalar")
                p.wait("vector", mm2)
                p.emit("vector", lambda h, s_: h.tensor_scalar(
                    out=vrow[:, :], in0=psb[7][0:1, 0:512], scalar1=1.0 / 1024.0,
                    scalar2=None, op0=ALU.mult), inc=1)
                p.wait("vector", p.mark("vector"))
                p.emit("vector", lambda h, s_: h.tensor_tensor(
                    out=vrow[:, :], in0=vrow[:, :], in1=rrow[:, :],
                    op=ALU.subtract), inc=1)
                mm3 = p.mark("vector")
                p.wait("scalar", mm3)
                p.emit("scalar", lambda h, s_: h.activation(
                    rrow[:, :], vrow[:, :], AF.Ln, bias=eps1[:, :]), inc=1)
                p.wait("scalar", p.mark("scalar"))
                p.emit("scalar", lambda h, s_: h.activation(
                    rrow[:, :], rrow[:, :], AF.Exp, scale=-0.5), inc=1)
                mm4 = p.mark("scalar")
                p.wait("vector", mm4)
                p.wait("vector", p.mark("vector"))
                p.emit("vector", lambda h, s_: h.tensor_tensor(
                    out=mrrow[:, :], in0=mrow[:, :], in1=rrow[:, :],
                    op=ALU.mult), inc=1)
                mm5 = p.mark("vector")
                p.wait("tensor", mm5)
                p.emit("tensor", lambda h, s_: h.matmul(
                    psb[4][:, :], onesrow[:, :], rrow[:, :], start=True, stop=True))
                p.emit("tensor", lambda h, s_: h.matmul(
                    psb[5][:, :], onesrow[:, :], mrrow[:, :], start=True, stop=True),
                       inc=1)
                m_bc = p.mark("tensor")

                for fc in range(NEC):
                    yb = fc % 2
                    p.wait("vector", m_bc)
                    if ych_free[yb]:
                        p.wait("vector", ych_free[yb])
                    p.emit("vector", lambda h, s_, fc=fc: h.tensor_tensor(
                        out=vf(arB, B_TMP, 512), in0=vf(arB, B_YPRE + fc * 512, 512),
                        in1=psb[4][:, :], op=ALU.mult))
                    p.emit("vector", lambda h, s_, fc=fc, yb=yb: h.tensor_tensor(
                        out=vf(arB, B_YCH[yb], 512), in0=vf(arB, B_TMP, 512),
                        in1=psb[5][:, :], op=ALU.subtract))
                    p.emit("vector", lambda h, s_, yb=yb: h.tensor_reduce(
                        out=tiny[:, 5:6], in_=vf(arB, B_YCH[yb], 512), axis=AX.X,
                        op=ALU.max, apply_absolute_value=True), inc=1)
                    p.wait("vector", p.mark("vector"))
                    if n == 0 and fc == 0:
                        p.emit("vector", lambda h, s_: h.tensor_copy(
                            rcol[:, 0:1], tiny[:, 5:6]), inc=1)
                    else:
                        p.emit("vector", lambda h, s_: h.tensor_tensor(
                            out=rcol[:, 0:1], in0=rcol[:, 0:1],
                            in1=tiny[:, 5:6], op=ALU.max), inc=1)
                    myv = p.mark("vector")
                    p.wait("gpsimd", myv)
                    ych_free[yb] = p.dma("gpsimd", lambda h, s_, fc=fc, nsl=nsl,
                                         yb=yb: h.dma_start(
                                             out=yscr[ts(fc), nsl],
                                             in_=vf(arB, B_YCH[yb], 512)))
                    m_last_y = ych_free[yb]
                if odbg is not None:
                    modbg = ("dma_gpsimd", p.dma_counts["gpsimd"])
                    p.wait("vector", modbg)
                m_blk_v = p.mark("vector")
            return (p.mark("vector"), p.mark("tensor"), p.mark("scalar"),
                    p.mark("gpsimd"), m_last_y)

        m_a2_v, m_a2_t, m_a2_s, m_a2_g, m_a2_y = a2()

        CB_SH = [0, 2048]
        CB_SL = [4096, 6144]
        CB_ST = [8192, 9216, 10240, 11264]
        CB_E = [12288 + i * 512 for i in range(8)]
        PB_SLOT = [[0, 1024, 2048, 3072, 8192, 9216],
                   [4096, 5120, 6144, 7168, 10240, 12288]]
        P79 = ([(arC, o) for o in PB_SLOT[0]],
               [(arC, o) for o in PB_SLOT[1]],
               [(arA, o) for o in (0, 1024, 2048, 3072, 4096, 5120)],
               [(arB, o) for o in (0, 1024, 2048, 3072, 4096, 5120)])

        def sv(S_, k):
            return vf(S_[k][0], S_[k][1], 1024)
        CB_OH = 0
        CB_OL = 12288
        CB_Y = [12288, 12800, 13312]
        P11_ST = [12800, 13312, 13824, 14336, 14848, 15360, 15872]

        def phase_barrier(eng_list=("sync", "tensor", "vector", "scalar", "gpsimd")):
            ms = {e: p.mark(e) for e in ENGINES}
            dm = {e: ("dma_" + e, p.dma_counts[e]) for e in ENGINES}
            for e in eng_list:
                for e2 in ENGINES:
                    if e2 != e:
                        p.wait(e, ms[e2])
                    p.wait(e, dm[e2])

        def collective(slot_marker, k):
            p.wait(G, slot_marker)
            mi = p.dma(G, lambda h, s_: h.dma_start(out=ccin[:, :], in_=ccs[:, :]))
            p.wait(G, mi)
            p.emit(G, lambda h, s_: h.collective_compute(
                "AllReduce", ALU.max, replica_groups=[list(range(ncores))],
                ins=[ccin[:, :]], outs=[ccout[:, :]]), inc=1)
            mc = p.mark(G)
            p.wait(G, mc)
            mo = p.dma(G, lambda h, s_: h.dma_start(out=ccs[:, :], in_=ccout[:, :]))
            return mo

        def make_scale(k, src_col, extra_pow2=1.0):
            p.emit("vector", lambda h, s_, k=k, c=src_col: h.tensor_scalar(
                out=ccs[:, c:c + 1], in0=ccs[:, c:c + 1], scalar1=r327[0:1, :],
                scalar2=None, op0=ALU.mult), inc=1)
            p.wait("vector", p.mark("vector"))
            if extra_pow2 != 1.0:
                p.emit("vector", lambda h, s_, k=k, c=src_col, e=extra_pow2:
                       h.tensor_scalar(out=ccs[:, c:c + 1], in0=ccs[:, c:c + 1],
                                       scalar1=e, scalar2=None, op0=ALU.mult),
                       inc=1)
                p.wait("vector", p.mark("vector"))
            p.emit("vector", lambda h, s_, c=src_col: h.tensor_scalar(
                out=ccs[:, c:c + 1], in0=ccs[:, c:c + 1], scalar1=1e-12,
                scalar2=None, op0=ALU.max), inc=1)
            mv = p.mark("vector")
            p.wait("tensor", mv)
            p.emit("tensor", lambda h, s_, c=src_col: h.matmul(
                psb[6][:, 384 + c:385 + c], onesrow[:, :], ccs[:, c:c + 1],
                start=True, stop=True), inc=1)
            mt = p.mark("tensor")
            p.wait("vector", mt)
            p.emit("vector", lambda h, s_, k=k, c=src_col: h.tensor_copy(
                sccol[:, k:k + 1], psb[6][:, 384 + c:385 + c]), inc=1)
            p.wait("vector", p.mark("vector"))
            p.emit("vector", lambda h, s_, k=k: h.reciprocal(
                sicol[:, k:k + 1], sccol[:, k:k + 1]), inc=1)
            p.wait("vector", p.mark("vector"))
            return p.mark("vector")

        def phase_b():
            phase_barrier()
            p.emit("vector", lambda h, s_: h.memset(ccs[:, :], 0.0))
            p.emit("vector", lambda h, s_: h.memset(r327[:, :], 32767.0), inc=1)
            p.wait("vector", p.mark("vector"))
            p.emit("vector", lambda h, s_: h.reciprocal(r327[:, :], r327[:, :]),
                   inc=1)
            p.wait("vector", p.mark("vector"))
            p.emit("vector", lambda h, s_: h.tensor_copy(
                tiny[:, 6:7], rcol[:, 0:1]), inc=1)
            mty = p.mark("vector")
            p.wait("tensor", mty)
            p.emit("tensor", lambda h, s_: h.transpose(
                psb[7][0:1, 0:128], tiny[:, 6:7], eyef[:, :]), inc=1)
            mtt = p.mark("tensor")
            p.wait("vector", mtt)
            p.emit("vector", lambda h, s_: h.tensor_reduce(
                out=ccs[:, 0:1], in_=psb[7][0:1, 0:128], axis=AX.X, op=ALU.max),
                   inc=1)
            mred = p.mark("vector")
            mcc = collective(mred, 0)
            p.wait("vector", mcc)
            m_sx = make_scale(0, 0)

            p.wait("sync", m_sx)
            set_guard4 = [None, None]
            for it in range(32):
                ec, th = divmod(it, 4)
                st_ = it % 2
                L, T_ = PB_SLOT[st_][0], PB_SLOT[st_][1]
                if set_guard4[st_]:
                    p.wait("sync", set_guard4[st_])
                mld = p.dma("sync", lambda h, s_, ec=ec, th=th, L=L: h.dma_start(
                    out=vf(arC, L, 1024),
                    in_=yscr[ts(ec), th * 1024:(th + 1) * 1024]))
                p.wait("scalar", mld)
                p.emit("scalar", lambda h, s_, L=L, T_=T_: h.activation(
                    vf(arC, T_, 1024), vf(arC, L, 1024), AF.Identity,
                    scale=sicol[:, 0:1], bias=mcol[:, :]), inc=1)
                m1 = p.mark("scalar")
                p.wait("vector", m1)
                p.emit("vector", lambda h, s_, T_=T_: h.tensor_scalar(
                    out=vf(arC, T_, 1024), in0=vf(arC, T_, 1024),
                    scalar1=-MAGIC, scalar2=sccol[:, 0:1], op0=ALU.add,
                    op1=ALU.mult), inc=1)
                m2 = p.mark("vector")
                p.wait("scalar", m2)
                p.emit("scalar", lambda h, s_, ec=ec, th=th, T_=T_: h.copy(
                    vb(arA, ec * 4096 + th * 1024, 1024), vf(arC, T_, 1024)),
                       inc=1)
                mhi = p.mark("scalar")
                p.wait("vector", mhi)
                p.emit("vector", lambda h, s_, ec=ec, th=th, T_=T_: h.tensor_tensor(
                    out=vb(arB, ec * 4096 + th * 1024, 1024),
                    in0=vf(arC, T_, 1024),
                    in1=vb(arA, ec * 4096 + th * 1024, 1024), op=ALU.subtract),
                       inc=1)
                set_guard4[st_] = p.mark("vector")
            m_xq = set_guard4[1]

            p.wait("tensor", m_xq)
            ev_guard = [None] * 4
            bank_guard = [None, None]
            st_guard = [m_xq, m_xq]
            rix2 = [0]
            for wi, (hscr, lscr, oscr_) in enumerate(
                    ((upwh, upwl, upscr), (gawh, gawl, gascr))):
                for fg in range(NFC):
                    buf = fg % 2
                    if st_guard[buf]:
                        p.wait("sync", st_guard[buf])
                    mwh = p.dma("sync", lambda h, s_, hscr=hscr, fg=fg, buf=buf:
                                h.dma_start(out=vb(arC, CB_SH[buf], 1024),
                                            in_=hscr[fg, :, :]))
                    mwl = p.dma("sync", lambda h, s_, lscr=lscr, fg=fg, buf=buf:
                                h.dma_start(out=vb(arC, CB_SL[buf], 1024),
                                            in_=lscr[fg, :, :]))
                    p.wait("tensor", mwh)
                    p.wait("tensor", mwl)
                    for tc in range(NTC):
                        bank = psb[tc % 2]
                        if bank_guard[tc % 2]:
                            p.wait("tensor", bank_guard[tc % 2])
                        for pa, (woff, xar) in enumerate(
                                ((CB_SH[buf], arA), (CB_SH[buf], arB),
                                 (CB_SL[buf], arA))):
                            for ec in range(NEC):
                                st_ = (pa == 0 and ec == 0)
                                sp_ = (pa == 2 and ec == NEC - 1)
                                p.emit("tensor", lambda h, s_, woff=woff, xar=xar,
                                       ec=ec, tc=tc, bank=bank, st=st_, sp=sp_:
                                       h.matmul(bank[:, :],
                                                vb(arC, woff + ec * 128, 128),
                                                vb(xar, ec * 4096 + tc * 512, 512),
                                                start=st, stop=sp),
                                       inc=(1 if sp_ else 0))
                        mmm = p.mark("tensor")
                        sti = (fg * NTC + tc) % 4
                        p.wait("scalar", mmm)
                        if ev_guard[sti]:
                            p.wait("scalar", ev_guard[sti])
                        p.emit("scalar", lambda h, s_, bank=bank, sti=sti: h.copy(
                            vf(arC, CB_ST[sti], 512), bank[:, :]), inc=1)
                        mev = p.mark("scalar")
                        bank_guard[tc % 2] = mev
                        p.wait("vector", mev)
                        p.emit("vector", lambda h, s_, sti=sti: h.tensor_reduce(
                            out=tiny[:, 7:8], in_=vf(arC, CB_ST[sti], 512),
                            axis=AX.X, op=ALU.max, apply_absolute_value=True),
                               inc=1)
                        p.wait("vector", p.mark("vector"))
                        if rix2[0] == 0 or (wi == 1 and fg == 0 and tc == 0):
                            p.emit("vector", lambda h, s_, wi=wi: h.tensor_copy(
                                rcol[:, 1 + wi:2 + wi], tiny[:, 7:8]), inc=1)
                        else:
                            p.emit("vector", lambda h, s_, wi=wi: h.tensor_tensor(
                                out=rcol[:, 1 + wi:2 + wi],
                                in0=rcol[:, 1 + wi:2 + wi], in1=tiny[:, 7:8],
                                op=ALU.max), inc=1)
                        rix2[0] += 1
                        mrc = p.mark("vector")
                        p.wait("gpsimd", mrc)
                        mst = p.dma("gpsimd", lambda h, s_, oscr_=oscr_, fg=fg,
                                    tc=tc, sti=sti: h.dma_start(
                                        out=oscr_[ts(fg), tc * 512:(tc + 1) * 512],
                                        in_=vf(arC, CB_ST[sti], 512)))
                        ev_guard[sti] = mst
                    st_guard[buf] = p.mark("tensor")
            m_p5_v = p.mark("vector")

            p.wait("vector", m_p5_v)
            for wi in range(2):
                p.emit("vector", lambda h, s_, wi=wi: h.tensor_copy(
                    tiny[:, 6:7], rcol[:, 1 + wi:2 + wi]), inc=1)
                p.wait("tensor", p.mark("vector"))
                p.emit("tensor", lambda h, s_, wi=wi: h.transpose(
                    psb[7][0:1, 128 * wi:128 * wi + 128], tiny[:, 6:7],
                    eyef[:, :]), inc=1)
                mq = p.mark("tensor")
                p.wait("vector", mq)
                p.emit("vector", lambda h, s_, wi=wi: h.tensor_reduce(
                    out=ccs[:, 1 + wi:2 + wi],
                    in_=psb[7][0:1, 128 * wi:128 * wi + 128], axis=AX.X,
                    op=ALU.max), inc=1)
            mred2 = p.mark("vector")
            mcc2 = collective(mred2, 1)
            p.wait("vector", mcc2)
            make_scale(1, 1)
            m_sg = make_scale(2, 2)

            p.wait("sync", m_sg)
            for m_ in ev_guard:
                p.wait("scalar", m_)
                p.wait("vector", m_)
            set_g7 = [None] * 4
            first7 = True
            for it in range(NFC * 4):
                fg, th = divmod(it, 4)
                st_ = it % 4
                S_ = P79[st_]
                if set_g7[st_]:
                    p.wait("sync", set_g7[st_])
                mld = p.dma("sync", lambda h, s_, fg=fg, th=th, S_=S_: h.dma_start(
                    out=sv(S_, 1),
                    in_=gascr[ts(fg), th * 1024:(th + 1) * 1024]))
                p.wait("scalar", mld)
                p.emit("scalar", lambda h, s_, S_=S_: h.activation(
                    sv(S_, 3), sv(S_, 1), AF.Identity,
                    scale=sicol[:, 2:3], bias=mcol[:, :]), inc=1)
                m1 = p.mark("scalar")
                p.wait("vector", m1)
                p.emit("vector", lambda h, s_, S_=S_: h.tensor_scalar(
                    out=sv(S_, 3), in0=sv(S_, 3),
                    scalar1=-MAGIC, scalar2=sccol[:, 2:3], op0=ALU.add,
                    op1=ALU.mult), inc=1)
                mg1 = p.mark("vector")
                p.wait("scalar", mg1)
                p.emit("scalar", lambda h, s_, S_=S_: h.activation(
                    sv(S_, 4), sv(S_, 3), AF.Sigmoid), inc=1)
                p.wait("scalar", p.mark("scalar"))
                p.emit("scalar", lambda h, s_, S_=S_: h.activation(
                    sv(S_, 4), sv(S_, 4), AF.Identity,
                    scale=65536.0, bias=mcol[:, :]), inc=1)
                m3 = p.mark("scalar")
                p.wait("vector", m3)
                p.emit("vector", lambda h, s_, S_=S_: h.tensor_scalar(
                    out=sv(S_, 4), in0=sv(S_, 4),
                    scalar1=-MAGIC, scalar2=UMAX, op0=ALU.add, op1=ALU.min),
                       inc=1)
                p.wait("vector", p.mark("vector"))
                p.emit("vector", lambda h, s_, S_=S_: h.tensor_tensor(
                    out=sv(S_, 5), in0=sv(S_, 3),
                    in1=sv(S_, 4), op=ALU.mult), inc=1)
                p.wait("vector", p.mark("vector"))
                p.emit("vector", lambda h, s_, S_=S_: h.tensor_reduce(
                    out=tiny[:, 7:8], in_=sv(S_, 5), axis=AX.X,
                    op=ALU.max, apply_absolute_value=True), inc=1)
                p.wait("vector", p.mark("vector"))
                if first7:
                    p.emit("vector", lambda h, s_: h.tensor_copy(
                        rcol[:, 3:4], tiny[:, 7:8]), inc=1)
                    first7 = False
                else:
                    p.emit("vector", lambda h, s_: h.tensor_tensor(
                        out=rcol[:, 3:4], in0=rcol[:, 3:4],
                        in1=tiny[:, 7:8], op=ALU.max), inc=1)
                set_g7[st_] = p.mark("vector")
            prev7 = p.mark("vector")

            p.wait("vector", prev7)
            p.emit("vector", lambda h, s_: h.tensor_copy(
                tiny[:, 6:7], rcol[:, 3:4]), inc=1)
            p.wait("tensor", p.mark("vector"))
            p.emit("tensor", lambda h, s_: h.transpose(
                psb[7][0:1, 0:128], tiny[:, 6:7], eyef[:, :]), inc=1)
            mq3 = p.mark("tensor")
            p.wait("vector", mq3)
            p.emit("vector", lambda h, s_: h.tensor_reduce(
                out=ccs[:, 3:4], in_=psb[7][0:1, 0:128], axis=AX.X, op=ALU.max),
                   inc=1)
            mred3 = p.mark("vector")
            mcc3 = collective(mred3, 3)
            p.wait("vector", mcc3)
            m_sg2 = make_scale(3, 3, extra_pow2=float(2.0 ** -16))
            p.wait("vector", m_sg2)
            p.emit("vector", lambda h, s_: h.tensor_scalar(
                out=sicol[:, 6:7], in0=sicol[:, 3:4], scalar1=float(2.0 ** -16),
                scalar2=None, op0=ALU.mult), inc=1)
            p.wait("vector", p.mark("vector"))
            m_sg2b = p.mark("vector")

            p.wait("sync", m_sg2b)
            set_g9 = [None] * 4
            first9 = True
            for it in range(NFC * 4):
                fg, th = divmod(it, 4)
                st_ = it % 4
                S_ = P79[st_]
                if set_g9[st_]:
                    p.wait("sync", set_g9[st_])
                mldu = p.dma("sync", lambda h, s_, fg=fg, th=th, S_=S_: h.dma_start(
                    out=sv(S_, 0),
                    in_=upscr[ts(fg), th * 1024:(th + 1) * 1024]))
                mldg = p.dma("sync", lambda h, s_, fg=fg, th=th, S_=S_: h.dma_start(
                    out=sv(S_, 1),
                    in_=gascr[ts(fg), th * 1024:(th + 1) * 1024]))
                p.wait("scalar", mldu)
                p.wait("scalar", mldg)
                p.emit("scalar", lambda h, s_, S_=S_: h.activation(
                    sv(S_, 2), sv(S_, 0), AF.Identity,
                    scale=sicol[:, 1:2], bias=mcol[:, :]), inc=1)
                m1 = p.mark("scalar")
                p.wait("vector", m1)
                p.emit("vector", lambda h, s_, S_=S_: h.tensor_scalar(
                    out=sv(S_, 2), in0=sv(S_, 2),
                    scalar1=-MAGIC, scalar2=sccol[:, 1:2], op0=ALU.add,
                    op1=ALU.mult), inc=1)
                p.emit("scalar", lambda h, s_, S_=S_: h.activation(
                    sv(S_, 3), sv(S_, 1), AF.Identity,
                    scale=sicol[:, 2:3], bias=mcol[:, :]), inc=1)
                m2 = p.mark("scalar")
                p.wait("vector", m2)
                p.emit("vector", lambda h, s_, S_=S_: h.tensor_scalar(
                    out=sv(S_, 3), in0=sv(S_, 3),
                    scalar1=-MAGIC, scalar2=sccol[:, 2:3], op0=ALU.add,
                    op1=ALU.mult), inc=1)
                mg1 = p.mark("vector")
                p.wait("scalar", mg1)
                p.emit("scalar", lambda h, s_, S_=S_: h.activation(
                    sv(S_, 4), sv(S_, 3), AF.Sigmoid), inc=1)
                p.wait("scalar", p.mark("scalar"))
                p.emit("scalar", lambda h, s_, S_=S_: h.activation(
                    sv(S_, 4), sv(S_, 4), AF.Identity,
                    scale=65536.0, bias=mcol[:, :]), inc=1)
                m3 = p.mark("scalar")
                p.wait("vector", m3)
                p.emit("vector", lambda h, s_, S_=S_: h.tensor_scalar(
                    out=sv(S_, 4), in0=sv(S_, 4),
                    scalar1=-MAGIC, scalar2=UMAX, op0=ALU.add, op1=ALU.min),
                       inc=1)
                p.wait("vector", p.mark("vector"))
                p.emit("vector", lambda h, s_, S_=S_: h.tensor_tensor(
                    out=sv(S_, 5), in0=sv(S_, 3),
                    in1=sv(S_, 4), op=ALU.mult), inc=1)
                mu_ = p.mark("vector")
                p.wait("scalar", mu_)
                p.emit("scalar", lambda h, s_, S_=S_: h.activation(
                    sv(S_, 5), sv(S_, 5), AF.Identity,
                    scale=sicol[:, 6:7], bias=mcol[:, :]), inc=1)
                m4 = p.mark("scalar")
                p.wait("vector", m4)
                p.emit("vector", lambda h, s_, S_=S_: h.tensor_scalar(
                    out=sv(S_, 5), in0=sv(S_, 5),
                    scalar1=-MAGIC, scalar2=sccol[:, 3:4], op0=ALU.add,
                    op1=ALU.mult), inc=1)
                p.wait("vector", p.mark("vector"))
                p.emit("vector", lambda h, s_, S_=S_: h.tensor_tensor(
                    out=sv(S_, 5), in0=sv(S_, 5),
                    in1=sv(S_, 2), op=ALU.mult), inc=1)
                p.wait("vector", p.mark("vector"))
                p.emit("vector", lambda h, s_, S_=S_: h.tensor_reduce(
                    out=tiny[:, 7:8], in_=sv(S_, 5), axis=AX.X,
                    op=ALU.max, apply_absolute_value=True), inc=1)
                p.wait("vector", p.mark("vector"))
                if first9:
                    p.emit("vector", lambda h, s_: h.tensor_copy(
                        rcol[:, 4:5], tiny[:, 7:8]), inc=1)
                    first9 = False
                else:
                    p.emit("vector", lambda h, s_: h.tensor_tensor(
                        out=rcol[:, 4:5], in0=rcol[:, 4:5],
                        in1=tiny[:, 7:8], op=ALU.max), inc=1)
                mo9 = p.mark("vector")
                p.wait("gpsimd", mo9)
                set_g9[st_] = p.dma("gpsimd", lambda h, s_, fg=fg, th=th, S_=S_:
                                    h.dma_start(
                    out=oscr[ts(fg), th * 1024:(th + 1) * 1024],
                    in_=sv(S_, 5)))
            prev9 = ("dma_gpsimd", p.dma_counts["gpsimd"])

            p.wait("vector", prev9)
            p.emit("vector", lambda h, s_: h.tensor_copy(
                tiny[:, 6:7], rcol[:, 4:5]), inc=1)
            p.wait("tensor", p.mark("vector"))
            p.emit("tensor", lambda h, s_: h.transpose(
                psb[7][0:1, 0:128], tiny[:, 6:7], eyef[:, :]), inc=1)
            mq4 = p.mark("tensor")
            p.wait("vector", mq4)
            p.emit("vector", lambda h, s_: h.tensor_reduce(
                out=ccs[:, 4:5], in_=psb[7][0:1, 0:128], axis=AX.X, op=ALU.max),
                   inc=1)
            mred4 = p.mark("vector")
            mcc4 = collective(mred4, 4)
            p.wait("vector", mcc4)
            m_so = make_scale(4, 4)

            phase_barrier()
            mdn1 = p.dma("sync", lambda h, s_: h.dma_start(
                out=arA.ap()[:, 0:24576].rearrange("p (a b c) -> p a b c",
                                                   a=8, b=3),
                in_=dnwh[:, :, :, :].transpose((2, 0, 1, 3))))
            mdn2 = p.dma("sync", lambda h, s_: h.dma_start(
                out=arB.ap()[:, 0:24576].rearrange("p (a b c) -> p a b c",
                                                   a=8, b=3),
                in_=dnwl[:, :, :, :].transpose((2, 0, 1, 3))))
            oq_guard = [None] * 24
            bank_guard2 = [None, None]
            out_guard = [None, None]
            m_out_last = None
            for tc in range(NTC):
                for ic in range(NFC):
                    if oq_guard[ic]:
                        p.wait("sync", oq_guard[ic])
                    mld = p.dma("sync", lambda h, s_, ic=ic, tc=tc: h.dma_start(
                        out=vf(arC, P11_ST[0], 512),
                        in_=oscr[ts(ic), tc * 512:(tc + 1) * 512]))
                    p.wait("scalar", mld)
                    p.emit("scalar", lambda h, s_: h.activation(
                        vf(arC, P11_ST[1], 512), vf(arC, P11_ST[0], 512),
                        AF.Identity, scale=sicol[:, 4:5], bias=mcol[:, :]), inc=1)
                    mt1 = p.mark("scalar")
                    p.wait("vector", mt1)
                    p.wait("sync", mt1)
                    p.emit("vector", lambda h, s_: h.tensor_scalar(
                        out=vf(arC, P11_ST[2], 512), in0=vf(arC, P11_ST[1], 512),
                        scalar1=-MAGIC, scalar2=sccol[:, 4:5], op0=ALU.add,
                        op1=ALU.mult), inc=1)
                    mt2 = p.mark("vector")
                    p.wait("scalar", mt2)
                    p.emit("scalar", lambda h, s_, ic=ic: h.copy(
                        vb(arC, CB_OH + ic * 512, 512), vf(arC, P11_ST[2], 512)),
                           inc=1)
                    mhi = p.mark("scalar")
                    p.wait("vector", mhi)
                    p.emit("vector", lambda h, s_, ic=ic: h.tensor_tensor(
                        out=vb(arC, CB_OL + ic * 512, 512),
                        in0=vf(arC, P11_ST[2], 512),
                        in1=vb(arC, CB_OH + ic * 512, 512), op=ALU.subtract),
                           inc=1)
                    oq_guard[ic] = p.mark("vector")
                m_pool = p.mark("vector")
                p.wait("tensor", m_pool)
                if tc == 0:
                    p.wait("tensor", mdn1)
                    p.wait("tensor", mdn2)
                for hc in range(NEC):
                    bank = psb[4 + hc % 2]
                    if bank_guard2[hc % 2]:
                        p.wait("tensor", bank_guard2[hc % 2])
                    for pa, (war, ooff) in enumerate(
                            ((arA, CB_OH), (arA, CB_OL), (arB, CB_OH))):
                        for ic in range(NFC):
                            st_ = (pa == 0 and ic == 0)
                            sp_ = (pa == 2 and ic == NFC - 1)
                            p.emit("tensor", lambda h, s_, war=war, ooff=ooff,
                                   ic=ic, hc=hc, bank=bank, st=st_, sp=sp_:
                                   h.matmul(bank[:, :],
                                            vb(war, (hc * 24 + ic) * 128, 128),
                                            vb(arC, ooff + ic * 512, 512),
                                            start=st, stop=sp),
                                   inc=(1 if sp_ else 0))
                    mmm = p.mark("tensor")
                    myld = p.dma("sync", lambda h, s_, hc=hc, tc=tc: h.dma_start(
                        out=vf(arC, P11_ST[3], 512),
                        in_=yscr[ts(hc), tc * 512:(tc + 1) * 512]))
                    p.wait("vector", mmm)
                    p.wait("vector", myld)
                    p.emit("vector", lambda h, s_, hc=hc, bank=bank: h.tensor_tensor(
                        out=vf(arC, P11_ST[4] + 0, 512) if False else
                        vf(arB, 12288 + hc * 512, 512), in0=bank[:, :],
                        in1=vf(arC, P11_ST[3], 512), op=ALU.add), inc=1)
                    mypre = p.mark("vector")
                    p.wait("sync", mypre)
                    bank_guard2[hc % 2] = mypre
                    p.wait("scalar", mypre)
                    if sq_guard2[0]:
                        p.wait("scalar", sq_guard2[0])
                    p.emit("scalar", lambda h, s_, hc=hc: h.activation(
                        vf(arC, P11_ST[5], 512), vf(arB, 12288 + hc * 512, 512),
                        AF.Square), inc=1)
                    msq = p.mark("scalar")
                    p.wait("tensor", mypre)
                    p.emit("tensor", lambda h, s_, hc=hc: h.matmul(
                        psb[6][0:1, 0:512], onescol[:, :],
                        vf(arB, 12288 + hc * 512, 512),
                        start=(hc == 0), stop=(hc == NEC - 1)))
                    p.wait("tensor", msq)
                    p.emit("tensor", lambda h, s_, hc=hc: h.matmul(
                        psb[7][0:1, 0:512], onescol[:, :], vf(arC, P11_ST[5], 512),
                        start=(hc == 0), stop=(hc == NEC - 1)), inc=1)
                    sq_guard2[0] = p.mark("tensor")
                m_stats = p.mark("tensor")
                p.wait("vector", m_stats)
                p.emit("vector", lambda h, s_: h.tensor_scalar(
                    out=mrow[:, :], in0=psb[6][0:1, 0:512], scalar1=1.0 / 1024.0,
                    scalar2=None, op0=ALU.mult), inc=1)
                mm1 = p.mark("vector")
                p.wait("scalar", mm1)
                p.emit("scalar", lambda h, s_: h.activation(
                    rrow[:, :], mrow[:, :], AF.Square), inc=1)
                mm2 = p.mark("scalar")
                p.wait("vector", mm2)
                p.emit("vector", lambda h, s_: h.tensor_scalar(
                    out=vrow[:, :], in0=psb[7][0:1, 0:512], scalar1=1.0 / 1024.0,
                    scalar2=None, op0=ALU.mult), inc=1)
                p.wait("vector", p.mark("vector"))
                p.emit("vector", lambda h, s_: h.tensor_tensor(
                    out=vrow[:, :], in0=vrow[:, :], in1=rrow[:, :],
                    op=ALU.subtract), inc=1)
                mm3 = p.mark("vector")
                p.wait("scalar", mm3)
                p.emit("scalar", lambda h, s_: h.activation(
                    rrow[:, :], vrow[:, :], AF.Ln, bias=eps1[:, :]), inc=1)
                p.wait("scalar", p.mark("scalar"))
                p.emit("scalar", lambda h, s_: h.activation(
                    rrow[:, :], rrow[:, :], AF.Exp, scale=-0.5), inc=1)
                mm4 = p.mark("scalar")
                p.wait("vector", mm4)
                p.wait("vector", p.mark("vector"))
                p.emit("vector", lambda h, s_: h.tensor_tensor(
                    out=mrrow[:, :], in0=mrow[:, :], in1=rrow[:, :],
                    op=ALU.mult), inc=1)
                mm5 = p.mark("vector")
                p.wait("tensor", mm5)
                p.emit("tensor", lambda h, s_: h.matmul(
                    psb[2][:, :], onesrow[:, :], rrow[:, :], start=True, stop=True))
                p.emit("tensor", lambda h, s_: h.matmul(
                    psb[3][:, :], onesrow[:, :], mrrow[:, :], start=True,
                    stop=True), inc=1)
                m_bc = p.mark("tensor")
                for hc in range(NEC):
                    ob = hc % 2
                    p.wait("vector", m_bc)
                    if out_guard[ob]:
                        p.wait("vector", out_guard[ob])
                    p.emit("vector", lambda h, s_, hc=hc: h.tensor_tensor(
                        out=vf(arC, P11_ST[6], 512),
                        in0=vf(arB, 12288 + hc * 512, 512), in1=psb[2][:, :],
                        op=ALU.mult), inc=1)
                    p.wait("vector", p.mark("vector"))
                    p.emit("vector", lambda h, s_, hc=hc, ob=ob: h.tensor_tensor(
                        out=vf(arA, 12288 + ob * 512, 512),
                        in0=vf(arC, P11_ST[6], 512), in1=psb[3][:, :],
                        op=ALU.subtract), inc=1)
                    mych = p.mark("vector")
                    p.wait("gpsimd", mych)
                    out_guard[ob] = p.dma("gpsimd", lambda h, s_, hc=hc, tc=tc,
                                          ob=ob: h.dma_start(
                        out=out[ts(hc), tc * 512:(tc + 1) * 512],
                        in_=vf(arA, 12288 + ob * 512, 512)))
                    m_out_last = out_guard[ob]
                p.wait("tensor", p.mark("vector"))
            return m_out_last

        sq_guard2 = [None]
        m_b_done = phase_b()

        def dump_mdbg():
            if mdbg is None:
                return
            p.emit("vector", lambda h, s_: h.tensor_copy(
                vf(arB, B_TMP, 512), psb[2][:, :]))
            p.emit("vector", lambda h, s_: h.tensor_copy(
                vf(arB, B_YCH[0], 512), psb[3][:, :]), inc=1)
            mpv = p.mark("vector")
            p.wait(G, mpv)
            p.dma(G, lambda h, s_: h.dma_start(out=mdbg["pv2"], in_=vf(arB, B_TMP, 512)))
            p.dma(G, lambda h, s_: h.dma_start(out=mdbg["pv3"], in_=vf(arB, B_YCH[0], 512)))
            p.dma(G, lambda h, s_: h.dma_start(out=mdbg["lnx"], in_=lnx[:, :]))
            p.dma(G, lambda h, s_: h.dma_start(out=mdbg["rex"], in_=rex[:, :]))
            p.dma(G, lambda h, s_: h.dma_start(out=mdbg["ccs"], in_=ccs[:, :]))
            p.dma(G, lambda h, s_: h.dma_start(out=mdbg["v2"], in_=vf(arA, 0, 8320)))
            p.dma(G, lambda h, s_: h.dma_start(out=mdbg["k2p0"], in_=k2v(0)))
            p.dma(G, lambda h, s_: h.dma_start(out=mdbg["qp0"], in_=qv(0)))

        def finalize(extra_markers):
            for m in extra_markers:
                p.wait(G, m)
            dump_mdbg()
            for name, (odbg, src) in dbg.items():
                rows, cols = src.shape
                for r0 in range(0, rows, 128):
                    p.dma(G, lambda h, s_, odbg=odbg, src=src, r0=r0: h.dma_start(
                        out=odbg[r0:r0 + 128, :], in_=src[r0:r0 + 128, :]))
            tgt = p.dma_counts[G]
            if tgt:
                p.emit(G, lambda h, s_, t=tgt: h.wait_ge(s_["dma_gpsimd"], t))
            for e in ("sync",):
                v = p.dma_counts[e]
                if v:
                    p.emit(G, lambda h, s_, e=e, v=v: h.wait_ge(s_["dma_" + e], v))

        finalize([m_b_done])

        with nc.Block() as block:
            @block.sync
            def _(eng):
                for fn in p.ops["sync"]:
                    fn(eng, sems)

            @block.tensor
            def _(eng):
                for fn in p.ops["tensor"]:
                    fn(eng, sems)

            @block.vector
            def _(eng):
                for fn in p.ops["vector"]:
                    fn(eng, sems)

            @block.scalar
            def _(eng):
                for fn in p.ops["scalar"]:
                    fn(eng, sems)

            @block.gpsimd
            def _(eng):
                for fn in p.ops["gpsimd"]:
                    fn(eng, sems)

    return nc



def _host_inputs(L0_attn_x, L0_attn_w_in, L0_attn_w_out,
                 L0_ffn_w_gate, L0_ffn_w_up, L0_ffn_w_down):
    win_t = np.ascontiguousarray(
        L0_attn_w_in.T.reshape(8, 128, 24, 128).transpose(2, 1, 0, 3))
    wout_t = np.ascontiguousarray(
        L0_attn_w_out.T.reshape(8, 128, 8, 128).transpose(2, 1, 0, 3))
    maps = []
    for c in range(NCORES):
        maps.append(dict(
            x_t=np.ascontiguousarray(L0_attn_x[c].T),
            w_in_t=win_t, w_out_t=wout_t,
            w_gate=L0_ffn_w_gate, w_up=L0_ffn_w_up, w_down=L0_ffn_w_down,
        ))
    return maps


def kernel(L0_attn_x, L0_attn_w_in, L0_attn_w_out,
           L0_ffn_w_gate, L0_ffn_w_up, L0_ffn_w_down):
    nc = build_program()
    maps = _host_inputs(L0_attn_x, L0_attn_w_in, L0_attn_w_out,
                        L0_ffn_w_gate, L0_ffn_w_up, L0_ffn_w_down)
    res = run_bass_kernel_spmd(nc, maps, core_ids=list(range(NCORES)))
    outs = [np.ascontiguousarray(res.results[c]["out"].T) for c in range(NCORES)]
    return np.stack(outs, axis=0)


if __name__ == "__main__":
    print("megakernel builder")


# version 11
# speedup vs baseline: 1.0092x; 1.0018x over previous
import contextlib
import numpy as np
import ml_dtypes
import concourse.bass as bass
import concourse.mybir as mybir
from concourse.bass_utils import run_bass_kernel_spmd

F32 = mybir.dt.float32
BF16 = mybir.dt.bfloat16
AF = mybir.ActivationFunctionType
ALU = mybir.AluOpType
AX = mybir.AxisListType
import bass_rust as _br
RED = _br.ReduceOp

NCORES = 8
S = 4096
E = 1024
H = 16
D = 64
WIN = 512
NB = S // WIN
FF = 3072
NEC = E // 128
NFC = FF // 128
NTC = S // 512
MAGIC = 12582912.0
NEG_BIG = -3.0e38
QMAX = 32767.0
UMAX = 65535.0
EPS_LN = 1e-5
SLOPES = [float(np.float32(np.exp2(np.float64(-(h + 1) * 8.0 / H)))) for h in range(H)]

ENGINES = ("sync", "tensor", "vector", "scalar", "gpsimd")


class Prog:
    def __init__(self):
        self.ops = {e: [] for e in ENGINES}
        self.counts = {e: 0 for e in ENGINES}
        self.dma_counts = {e: 0 for e in ENGINES}

    def emit(self, eng, fn, inc=0):
        if inc:
            self.counts[eng] += inc

            def wrapped(h, sems, fn=fn, eng=eng, n=inc):
                fn(h, sems).then_inc(sems[eng], n)
            self.ops[eng].append(wrapped)
        else:
            self.ops[eng].append(fn)
        return (eng, self.counts[eng])

    def dma(self, eng, fn):
        self.dma_counts[eng] += 16
        tgt = self.dma_counts[eng]

        def wrapped(h, sems, fn=fn, eng=eng):
            fn(h, sems).then_inc(sems["dma_" + eng], 16)
        self.ops[eng].append(wrapped)
        return ("dma_" + eng, tgt)

    def mark(self, eng):
        key = eng if eng in self.counts else None
        return (eng, self.counts[eng])

    def wait(self, eng, marker):
        if marker is None:
            return
        sname, val = marker
        if val <= 0:
            return
        self.ops[eng].append(lambda h, sems, s=sname, v=val: h.wait_ge(sems[s], v))


def ts(i, n=128):
    return slice(i * n, (i + 1) * n)


def band_chunks(n):
    out = []
    for c in range(8):
        if n == 0 and c < 4:
            continue
        if n == 0:
            qlo, qhi = 128 * (c - 4), 512
        else:
            qlo, qhi = max(0, 128 * (c - 4)), min(512, 128 * (c + 1))
        out.append((c, qlo, qhi))
    return out


def build_program(debug=None, a2stop=None, ncores=NCORES):
    nc = bass.Bass("TRN2", target_bir_lowering=False)
    p = Prog()

    x_t = nc.dram_tensor("x_t", [E, S], F32, kind="ExternalInput").ap()
    w_in_t = nc.dram_tensor("w_in_t", [24, 128, 8, 128], F32, kind="ExternalInput").ap()
    w_out_t = nc.dram_tensor("w_out_t", [8, 128, 8, 128], F32, kind="ExternalInput").ap()
    w_gate = nc.dram_tensor("w_gate", [FF, E], F32, kind="ExternalInput").ap()
    w_up = nc.dram_tensor("w_up", [FF, E], F32, kind="ExternalInput").ap()
    w_down = nc.dram_tensor("w_down", [E, FF], F32, kind="ExternalInput").ap()
    out = nc.dram_tensor("out", [E, S], F32, kind="ExternalOutput").ap()

    kscr = nc.dram_tensor("kscr", [E, S], F32, kind="Internal").ap()
    qscr = nc.dram_tensor("qscr", [E, S], F32, kind="Internal").ap()
    vscr = nc.dram_tensor("vscr", [E, S], F32, kind="Internal").ap()
    yscr = nc.dram_tensor("yscr", [E, S], F32, kind="Internal").ap()
    btab = nc.dram_tensor("btab", [H, 2 * WIN], F32, kind="Internal").ap()
    wkind = "ExternalOutput" if (debug and "wdbg" in debug) else "Internal"
    upwh = nc.dram_tensor("upwh", [NFC, 128, 1024], BF16, kind=wkind).ap()
    upwl = nc.dram_tensor("upwl", [NFC, 128, 1024], BF16, kind=wkind).ap()
    gawh = nc.dram_tensor("gawh", [NFC, 128, 1024], BF16, kind="Internal").ap()
    gawl = nc.dram_tensor("gawl", [NFC, 128, 1024], BF16, kind="Internal").ap()
    dnwh = nc.dram_tensor("dnwh", [NEC, 3, 128, 1024], BF16, kind="Internal").ap()
    dnwl = nc.dram_tensor("dnwl", [NEC, 3, 128, 1024], BF16, kind="Internal").ap()
    upscr = nc.dram_tensor("upscr", [FF, S], F32, kind="Internal").ap()
    gascr = nc.dram_tensor("gascr", [FF, S], F32, kind="Internal").ap()
    oscr = nc.dram_tensor("oscr", [FF, S], F32, kind="Internal").ap()
    ccin = nc.dram_tensor("ccin", [1, 8], F32, kind="Internal").ap()
    ccout = nc.dram_tensor("ccout", [1, 8], F32, kind="Internal").ap()

    dbg_names = (debug or "").split(",") if debug else []
    mdbg = None
    if "mdbg" in dbg_names:
        mdbg = {
            "negb4": nc.dram_tensor("dbg_negb4", [128, 1], F32, kind="ExternalOutput").ap(),
            "ccs": nc.dram_tensor("dbg_ccs", [1, 8], F32, kind="ExternalOutput").ap(),
            "v2": nc.dram_tensor("dbg_v2", [128, 8320], F32, kind="ExternalOutput").ap(),
            "k2p0": nc.dram_tensor("dbg_k2p0", [66, 1024], F32, kind="ExternalOutput").ap(),
            "qp0": nc.dram_tensor("dbg_qp0", [66, 512], F32, kind="ExternalOutput").ap(),
            "ccs2": nc.dram_tensor("dbg_ccs2", [1, 8], F32, kind="ExternalOutput").ap(),
            "tiny": nc.dram_tensor("dbg_tiny", [128, 8], F32, kind="ExternalOutput").ap(),
            "rcol": nc.dram_tensor("dbg_rcol", [128, 144], F32, kind="ExternalOutput").ap(),
            "pv2": nc.dram_tensor("dbg_pv2", [128, 512], F32, kind="ExternalOutput").ap(),
            "pv3": nc.dram_tensor("dbg_pv3", [128, 512], F32, kind="ExternalOutput").ap(),
            "lnx": nc.dram_tensor("dbg_lnx", [128, 8], F32, kind="ExternalOutput").ap(),
            "rex": nc.dram_tensor("dbg_rex", [128, 8], F32, kind="ExternalOutput").ap(),
            "negb4h": nc.dram_tensor("dbg_negb4h", [128, 16], F32, kind="ExternalOutput").ap(),
            "mqf": nc.dram_tensor("dbg_mqf", [128, 8], F32, kind="ExternalOutput").ap(),
            "mkf": nc.dram_tensor("dbg_mkf", [128, 8], F32, kind="ExternalOutput").ap(),
        }
        dbg_names.remove("mdbg")
    odbg = None
    if "odbg" in dbg_names:
        odbg = nc.dram_tensor("dbg_odbg", [S, E], F32, kind="ExternalOutput").ap()
        dbg_names.remove("odbg")
    if "wdbg" in dbg_names:
        dbg_names.remove("wdbg")
    dbg = {}
    for name in dbg_names:
        src = {"kscr": kscr, "qscr": qscr, "vscr": vscr, "yscr": yscr,
               "upscr": upscr, "gascr": gascr, "oscr": oscr}[name]
        dbg[name] = (nc.dram_tensor("dbg_" + name, list(src.shape), F32,
                                    kind="ExternalOutput").ap(), src)

    est = contextlib.ExitStack()
    with est:
        def sbt(name, shape, dtype):
            return est.enter_context(nc.sbuf_tensor(name, shape, dtype))

        arA = sbt("arA", [128, 32768], BF16)
        arB = sbt("arB", [128, 32768], BF16)
        arC = sbt("arC", [128, 33536], BF16)

        def vf(ar, off, n, p0=0, p1=128):
            return ar.ap().bitcast(F32)[p0:p1, off:off + n]

        def vb(ar, off, n, p0=0, p1=128):
            return ar.ap()[p0:p1, off:off + n]

        qramp = sbt("qramp", [1, 512], F32)
        mrow = sbt("mrow", [1, 512], F32)
        vrow = sbt("vrow", [1, 512], F32)
        rrow = sbt("rrow", [1, 512], F32)
        mrrow = sbt("mrrow", [1, 512], F32)
        slopecol = sbt("slopecol", [16, 1], F32)
        s512col = sbt("s512col", [16, 1], F32)
        u20col = sbt("u20col", [128, 1], F32)
        onesrow = sbt("onesrow", [1, 128], F32)
        onescol = sbt("onescol", [128, 1], F32)
        eyebf = sbt("eyebf", [128, 128], BF16)
        eyef = sbt("eyef", [128, 128], F32)
        maskA = sbt("maskA", [128, 128], BF16)
        maskB = sbt("maskB", [128, 128], BF16)
        onetile = sbt("onetile", [128, 128], F32)
        scol = sbt("scol", [128, 64], F32)
        invcol = sbt("invcol", [128, 64], F32)
        sctp = vf(arC, 15360, 512, 0, 32)
        rcol = sbt("rcol", [128, 8], F32)
        lncol = sbt("lncol", [128, 8], F32)
        sccol = sbt("sccol", [128, 8], F32)
        sicol = sbt("sicol", [128, 8], F32)
        tiny = sbt("tiny", [128, 8], F32)
        lnx = sbt("lnx", [128, 8], F32)
        rmcol = sbt("rmcol", [128, 12], F32)
        eps1 = sbt("eps1", [1, 1], F32)
        mcol = sbt("mcol", [128, 1], F32)
        rex = sbt("rex", [128, 8], F32)
        ccs = sbt("ccs", [1, 8], F32)
        r327 = sbt("r327", [1, 1], F32)

        psb = [est.enter_context(nc.psum_tensor(f"ps{i}", [128, 512], F32))
               for i in range(8)]

        sems = {}
        for e in ENGINES:
            sems[e] = est.enter_context(nc.semaphore("sem_" + e))
            sems["dma_" + e] = est.enter_context(nc.semaphore("dsem_" + e))

        G = "gpsimd"
        p.emit(G, lambda h, s_: h.memset(onesrow[:, :], 1.0))
        p.emit(G, lambda h, s_: h.memset(onescol[:, :], 1.0))
        p.emit(G, lambda h, s_: h.memset(onetile[:, :], 1.0))
        p.emit(G, lambda h, s_: h.memset(eps1[:, :], EPS_LN))
        p.emit(G, lambda h, s_: h.memset(mcol[:, :], MAGIC))
        p.emit(G, lambda h, s_: h.memset(u20col[:, :], 20.0))
        p.emit(G, lambda h, s_: h.affine_select(
            out=eyef[:, :], in_=onetile[:, :], pattern=[[-1, 128]], base=0,
            channel_multiplier=1, compare_op=ALU.is_equal, fill=0.0))
        p.emit(G, lambda h, s_: h.affine_select(
            out=eyebf[:, :], in_=onetile[:, :], pattern=[[-1, 128]], base=0,
            channel_multiplier=1, compare_op=ALU.is_equal, fill=0.0))
        p.emit(G, lambda h, s_: h.memset(maskA[:, :], NEG_BIG))
        p.emit(G, lambda h, s_: h.affine_select(
            out=maskA[:, :], in_=maskA[:, :], pattern=[[-1, 128]], base=-1,
            channel_multiplier=1, compare_op=ALU.is_ge, fill=0.0))
        p.emit(G, lambda h, s_: h.memset(maskB[:, :], NEG_BIG))
        p.emit(G, lambda h, s_: h.affine_select(
            out=maskB[:, :], in_=maskB[:, :], pattern=[[1, 128]], base=-1,
            channel_multiplier=-1, compare_op=ALU.is_ge, fill=0.0))
        p.emit(G, lambda h, s_: h.iota(
            out=qramp[:, :], pattern=[[1, 512]], base=0, channel_multiplier=0, allow_small_or_imprecise_dtypes=True))
        p.emit(G, lambda h, s_: h.iota(
            out=vf(arC, 0, 1024, 0, 16), pattern=[[1, 1024]], base=0, channel_multiplier=0, allow_small_or_imprecise_dtypes=True))
        p.emit(G, lambda h, s_: h.iota(
            out=slopecol[:, :], pattern=[[1, 1]], base=1, channel_multiplier=1, allow_small_or_imprecise_dtypes=True),
               inc=1)
        m_g0 = p.mark(G)
        p.wait("scalar", m_g0)
        p.emit("scalar", lambda h, s_: h.activation(
            slopecol[:, :], slopecol[:, :], AF.Exp,
            scale=float(-np.log(2.0) / 2.0)), inc=1)
        m_s0 = p.mark("scalar")
        p.wait("vector", m_s0)
        p.emit("vector", lambda h, s_: h.tensor_scalar(
            out=slopecol[:, :], in0=slopecol[:, :], scalar1=-1.0, scalar2=None,
            op0=ALU.mult), inc=1)
        p.wait("vector", p.mark("vector"))
        p.emit("vector", lambda h, s_: h.tensor_scalar(
            out=s512col[:, :], in0=slopecol[:, :], scalar1=-512.0, scalar2=None,
            op0=ALU.mult), inc=1)
        p.wait("vector", p.mark("vector"))
        p.emit("vector", lambda h, s_: h.tensor_scalar(
            out=vf(arC, 0, 1024, 0, 16), in0=vf(arC, 0, 1024, 0, 16),
            scalar1=slopecol[:, :], scalar2=s512col[:, :],
            op0=ALU.mult, op1=ALU.add), inc=1)
        m_v0 = p.mark("vector")
        p.wait("sync", m_v0)
        m_btab = p.dma("sync", lambda h, s_: h.dma_start(out=btab[:, :], in_=vf(arC, 0, 1024, 0, 16)))
        consts_done = (m_g0, m_v0)

        XSTG = lambda: vf(arC, 0, 4096)
        A1_WSTG = 4096
        DQ_WSTG = 5120
        DQ_TMP = 6144
        DQ_DQ = 7168
        DQ_SREP = 8192
        DQ_IREP = 9216
        B_A1WH = [2 * 10240, 2 * 10240 + 1024]
        B_A1WL = [2 * 10240 + 2048, 2 * 10240 + 3072]
        B_DQH = 2 * 10240 + 4096
        B_DQL = 2 * 10240 + 5120
        B_TSTG = 2 * 10240 + 6144
        B_TSTG2 = 31488
        ST0 = 13696
        STG = [13696, 14208, 14720, 15232]
        wohscr = nc.dram_tensor("wohscr", [8, 128, 8, 128], BF16, kind="Internal").ap()
        wolscr = nc.dram_tensor("wolscr", [8, 128, 8, 128], BF16, kind="Internal").ap()

        last_v_use = {}
        prev_st = [None]
        e32_built = [False]

        def a1_x_split():
            p.wait("sync", m_btab)
            prev = None
            for ec in range(NEC):
                mld = p.dma("sync", lambda h, s_, ec=ec: h.dma_start(
                    out=XSTG(), in_=x_t[ts(ec), :]))
                p.wait("scalar", mld)
                if prev:
                    p.wait("scalar", prev)
                p.emit("scalar", lambda h, s_, ec=ec: h.copy(
                    vb(arA, ec * 4096, 4096), XSTG()), inc=1)
                mhi = p.mark("scalar")
                p.wait("vector", mhi)
                p.emit("vector", lambda h, s_, ec=ec: h.tensor_tensor(
                    out=vb(arB, ec * 4096, 4096), in0=XSTG(),
                    in1=vb(arA, ec * 4096, 4096), op=ALU.subtract), inc=1)
                prev = p.mark("vector")
                p.wait("sync", prev)
            return prev

        m_xsplit = a1_x_split()

        def a1_qkv():
            mm_done = None
            ev_prev = [None] * 4
            wh_free = [None, None]
            a1_bank_guard = [None, None]
            rix = [0]
            for fg in range(24):
                buf = fg % 2
                mld = p.dma("sync", lambda h, s_, fg=fg: h.dma_start(
                    out=vf(arC, A1_WSTG, 1024),
                    in_=w_in_t[fg, :, :, :].rearrange("p a b -> p (a b)")))
                p.wait("scalar", mld)
                if wh_free[buf]:
                    p.wait("scalar", wh_free[buf])
                p.emit("scalar", lambda h, s_, buf=buf: h.copy(
                    vb(arC, B_A1WH[buf], 1024), vf(arC, A1_WSTG, 1024)), inc=1)
                mhi = p.mark("scalar")
                p.wait("vector", mhi)
                p.emit("vector", lambda h, s_, buf=buf: h.tensor_tensor(
                    out=vb(arC, B_A1WL[buf], 1024), in0=vf(arC, A1_WSTG, 1024),
                    in1=vb(arC, B_A1WH[buf], 1024), op=ALU.subtract), inc=1)
                mlo = p.mark("vector")
                p.wait("sync", mlo)
                p.wait("tensor", mlo)
                if fg == 0:
                    p.wait("tensor", m_xsplit)
                for tc in range(NTC):
                    bank = psb[tc % 2]
                    if a1_bank_guard[tc % 2]:
                        p.wait("tensor", a1_bank_guard[tc % 2])
                    for pa, (wof, xar) in enumerate(
                            ((B_A1WH[buf], arA), (B_A1WH[buf], arB),
                             (B_A1WL[buf], arA))):
                        for ec in range(NEC):
                            st = (pa == 0 and ec == 0)
                            sp = (pa == 2 and ec == NEC - 1)
                            p.emit("tensor", lambda h, s_, wof=wof, xar=xar, ec=ec,
                                   tc=tc, bank=bank, st=st, sp=sp:
                                   h.matmul(bank[:, :],
                                            vb(arC, wof + ec * 128, 128),
                                            vb(xar, ec * 4096 + tc * 512, 512),
                                            start=st, stop=sp),
                                   inc=(1 if sp else 0))
                    mmm = p.mark("tensor")
                    sti = (fg * NTC + tc) % 4
                    p.wait("scalar", mmm)
                    if ev_prev[sti]:
                        p.wait("scalar", ev_prev[sti])
                    if fg < 8:
                        p.emit("scalar", lambda h, s_, bank=bank, sti=sti: h.activation(
                            vf(arC, STG[sti], 512), bank[:, :], AF.Identity,
                            scale=0.125), inc=1)
                    else:
                        p.emit("scalar", lambda h, s_, bank=bank, sti=sti: h.copy(
                            vf(arC, STG[sti], 512), bank[:, :]), inc=1)
                    mev = p.mark("scalar")
                    a1_bank_guard[tc % 2] = mev
                    p.wait("gpsimd", mev)
                    scr = qscr if fg < 8 else (kscr if fg < 16 else vscr)
                    row = (fg % 8) * 128
                    mst = p.dma("gpsimd", lambda h, s_, scr=scr, row=row, tc=tc, sti=sti:
                                h.dma_start(out=scr[row:row + 128, tc * 512:(tc + 1) * 512],
                                            in_=vf(arC, STG[sti], 512)))
                    ev_prev[sti] = mst
                wh_free[buf] = p.mark("tensor")
            return p.mark("tensor"), p.mark("sync"), ev_prev, rix[0]

        m_qkv_t, m_qkv_dma, a1_ev_prev, n_rcol_qk = a1_qkv()

        def wout_split():
            prevs = None
            p.wait("scalar", m_qkv_t)
            for fg in range(8):
                mld = p.dma("sync", lambda h, s_, fg=fg: h.dma_start(
                    out=vf(arC, A1_WSTG, 1024),
                    in_=w_out_t[fg, :, :, :].rearrange("p a b -> p (a b)")))
                p.wait("scalar", mld)
                if prevs:
                    p.wait("scalar", prevs)
                p.emit("scalar", lambda h, s_: h.copy(
                    vb(arC, B_A1WH[0], 1024), vf(arC, A1_WSTG, 1024)), inc=1)
                mhi = p.mark("scalar")
                p.wait("vector", mhi)
                p.emit("vector", lambda h, s_: h.tensor_tensor(
                    out=vb(arC, B_A1WL[0], 1024), in0=vf(arC, A1_WSTG, 1024),
                    in1=vb(arC, B_A1WH[0], 1024), op=ALU.subtract), inc=1)
                mlo = p.mark("vector")
                p.wait("sync", mlo)
                m1 = p.dma("sync", lambda h, s_, fg=fg: h.dma_start(
                    out=wohscr[fg, :, :, :].rearrange("p a b -> p (a b)"),
                    in_=vb(arC, B_A1WH[0], 1024)))
                m2 = p.dma("sync", lambda h, s_, fg=fg: h.dma_start(
                    out=wolscr[fg, :, :, :].rearrange("p a b -> p (a b)"),
                    in_=vb(arC, B_A1WL[0], 1024)))
                prevs = m2
                p.wait("scalar", m2)
            return m2

        m_wout = wout_split()

        def dequant_all():
            E32 = vf(arC, 0, 1024)
            p.wait("gpsimd", m_qkv_t)
            p.emit(G, lambda h, s_: h.memset(E32, 1.0))
            p.emit(G, lambda h, s_: h.affine_select(
                out=E32, in_=E32, pattern=[[1, 1024]], base=0,
                channel_multiplier=-32, compare_op=ALU.is_ge, fill=0.0))
            p.emit(G, lambda h, s_: h.affine_select(
                out=E32, in_=E32, pattern=[[-1, 1024]], base=31,
                channel_multiplier=32, compare_op=ALU.is_ge, fill=0.0),
                   inc=1)
            p.wait("tensor", p.mark("gpsimd"))
            for m_ in a1_ev_prev:
                p.wait("tensor", m_)

            def SW(st):
                return 1024 + st * 3584

            def ST(st):
                return SW(st) + 1024

            def SHB(st):
                return 2 * (SW(st) + 2048)

            def SLB(st):
                return SHB(st) + 1024

            def SSTG(st):
                return SHB(st) + 2048

            w_free = [None] * 4
            hl_free = [None] * 4
            stg_free = [None] * 4
            sc_free = [None, None]
            sctp_free = [None, None]
            tr_free = [None, None]
            it = 0
            for w_ap, hscr, lscr, nog, nic in (
                    (w_up, upwh, upwl, 24, 8), (w_gate, gawh, gawl, 24, 8),
                    (w_down, dnwh, dnwl, 8, 24)):
                for og in range(nog):
                    for ip in range(nic // 8):
                        st = it % 4
                        pr = it % 2
                        b0 = 4 * pr
                        it += 1
                        i0_ = ip * 1024
                        if w_free[st]:
                            p.wait("sync", w_free[st])
                        mld = p.dma("sync", lambda h, s_, w_ap=w_ap, og=og,
                                    i0_=i0_, st=st: h.dma_start(
                            out=vf(arC, SW(st), 1024),
                            in_=w_ap[ts(og), i0_:i0_ + 1024]))
                        p.wait("vector", mld)
                        if sc_free[pr]:
                            p.wait("vector", sc_free[pr])
                        p.emit("vector", lambda h, s_, st=st, pr=pr:
                               h.tensor_reduce(
                            out=scol[:, 32 * pr:32 * pr + 32],
                            in_=vf(arC, SW(st), 1024).rearrange(
                                "p (b w) -> p b w", w=32),
                            axis=AX.X, op=ALU.max, apply_absolute_value=True),
                               inc=1)
                        p.wait("vector", p.mark("vector"))
                        p.emit("vector", lambda h, s_, pr=pr: h.tensor_scalar(
                            out=scol[:, 32 * pr:32 * pr + 32],
                            in0=scol[:, 32 * pr:32 * pr + 32],
                            scalar1=float(np.float32(1.0) / np.float32(7.0)),
                            scalar2=1e-12, op0=ALU.mult, op1=ALU.max), inc=1)
                        p.wait("vector", p.mark("vector"))
                        p.emit("vector", lambda h, s_, pr=pr: h.reciprocal(
                            invcol[:, 32 * pr:32 * pr + 32],
                            scol[:, 32 * pr:32 * pr + 32]), inc=1)
                        msc = p.mark("vector")
                        p.wait("tensor", msc)
                        if tr_free[pr]:
                            p.wait("tensor", tr_free[pr])
                        p.emit("tensor", lambda h, s_, pr=pr, b0=b0: h.transpose(
                            psb[b0][0:32, 0:128], scol[:, 32 * pr:32 * pr + 32],
                            eyef[:, :]))
                        p.emit("tensor", lambda h, s_, pr=pr, b0=b0: h.transpose(
                            psb[b0][0:32, 128:256],
                            invcol[:, 32 * pr:32 * pr + 32], eyef[:, :]), inc=1)
                        mtr0 = p.mark("tensor")
                        sc_free[pr] = mtr0
                        p.wait("scalar", mtr0)
                        if sctp_free[pr]:
                            p.wait("scalar", sctp_free[pr])
                        p.emit("scalar", lambda h, s_, pr=pr, b0=b0: h.copy(
                            sctp[0:32, 256 * pr:256 * pr + 256],
                            psb[b0][0:32, 0:256]), inc=1)
                        mevs = p.mark("scalar")
                        p.wait("tensor", mevs)
                        for hlf in range(2):
                            p.emit("tensor", lambda h, s_, hlf=hlf, pr=pr, b0=b0,
                                   E32=E32: h.matmul(
                                psb[b0 + hlf][:, :],
                                sctp[0:32, 256 * pr + 128:256 * pr + 256],
                                E32[0:32, 512 * hlf:512 * hlf + 512],
                                start=True, stop=True))
                            p.emit("tensor", lambda h, s_, hlf=hlf, pr=pr, b0=b0,
                                   E32=E32: h.matmul(
                                psb[b0 + 2 + hlf][:, :],
                                sctp[0:32, 256 * pr:256 * pr + 128],
                                E32[0:32, 512 * hlf:512 * hlf + 512],
                                start=True, stop=True),
                                   inc=(1 if hlf == 1 else 0))
                        mrep = p.mark("tensor")
                        sctp_free[pr] = mrep
                        p.wait("vector", mrep)
                        for hlf in range(2):
                            p.emit("vector", lambda h, s_, st=st, hlf=hlf, b0=b0:
                                   h.tensor_tensor(
                                out=vf(arC, ST(st) + 512 * hlf, 512),
                                in0=vf(arC, SW(st) + 512 * hlf, 512),
                                in1=psb[b0 + hlf][:, :], op=ALU.mult),
                                   inc=(1 if hlf == 1 else 0))
                        w_free[st] = p.mark("vector")
                        p.emit("vector", lambda h, s_, st=st: h.tensor_scalar(
                            out=vf(arC, ST(st), 1024), in0=vf(arC, ST(st), 1024),
                            scalar1=MAGIC, scalar2=None, op0=ALU.add), inc=1)
                        p.wait("vector", p.mark("vector"))
                        for hlf in range(2):
                            p.emit("vector", lambda h, s_, st=st, hlf=hlf, b0=b0:
                                   h.scalar_tensor_tensor(
                                out=vf(arC, ST(st) + 512 * hlf, 512),
                                in0=vf(arC, ST(st) + 512 * hlf, 512),
                                scalar=-MAGIC, in1=psb[b0 + 2 + hlf][:, :],
                                op0=ALU.add, op1=ALU.mult),
                                   inc=(1 if hlf == 1 else 0))
                        mdq = p.mark("vector")
                        p.wait("scalar", mdq)
                        if hl_free[st]:
                            p.wait("scalar", hl_free[st])
                        p.emit("scalar", lambda h, s_, st=st: h.copy(
                            vb(arC, SHB(st), 1024), vf(arC, ST(st), 1024)),
                               inc=1)
                        mhi = p.mark("scalar")
                        p.wait("vector", mhi)
                        p.emit("vector", lambda h, s_, st=st: h.tensor_tensor(
                            out=vb(arC, SLB(st), 1024), in0=vf(arC, ST(st), 1024),
                            in1=vb(arC, SHB(st), 1024), op=ALU.subtract), inc=1)
                        mlo = p.mark("vector")
                        bview = psb[b0].ap().bitcast(BF16)
                        p.wait("tensor", mlo)
                        mevp = None
                        for part, boff in enumerate((SHB(st), SLB(st))):
                            if part == 1:
                                p.wait("tensor", mevp)
                            for jj in range(8):
                                p.emit("tensor", lambda h, s_, boff=boff, jj=jj,
                                       bview=bview: h.transpose(
                                    bview[:, jj * 128:jj * 128 + 128],
                                    vb(arC, boff + jj * 128, 128),
                                    eyebf[:, :]), inc=(1 if jj == 7 else 0))
                            mtrp = p.mark("tensor")
                            p.wait("scalar", mtrp)
                            if part == 0 and stg_free[st]:
                                p.wait("scalar", stg_free[st])
                            if part == 1:
                                p.wait("scalar", mdh)
                            p.emit("scalar", lambda h, s_, st=st, bview=bview:
                                   h.copy(vb(arC, SSTG(st), 1024),
                                          bview[:, 0:1024]), inc=1)
                            mevp = p.mark("scalar")
                            if part == 1:
                                tr_free[pr] = mevp
                                hl_free[st] = mtrp
                            p.wait("sync", mevp)
                            dscr = hscr if part == 0 else lscr
                            md = p.dma("sync", lambda h, s_, dscr=dscr,
                                       og=og, i0_=i0_, st=st, nic=nic:
                                       h.dma_start(
                                out=(dscr[og, :, :] if nic == 8
                                     else dscr[og, i0_ // 1024, :, :]),
                                in_=vb(arC, SSTG(st), 1024)))
                            if part == 0:
                                mdh = md
                            else:
                                stg_free[st] = md
            return ("dma_sync", p.dma_counts["sync"])

        m_dq_dn = dequant_all()

        A_V2, A_OT = 0, 8320
        B_P1, B_XRES, B_YPRE = 0, 5120, 9216
        B_SQ, B_TMP = 13312, 13824
        B_YCH = [14336, 14848]
        C_K2 = [0, 1024, 2048, 3072]
        C_QP = [4096, 4608, 5120, 5632]
        C_OTH, C_OTL = 12288, 20480
        C_WOH = [28672, 29696]
        C_WOL = [30720, 31744]
        PW = [128, 256, 384, 512, 512, 384, 256, 128]
        POFF = [0]
        for w_ in PW:
            POFF.append(POFF[-1] + w_)

        def k2v(i):
            return vf(arC, C_K2[i], 1024, 0, 66)

        def qv(i):
            return vf(arC, C_QP[i], 512, 0, 66)

        def p1v(par, c):
            return vf(arB, B_P1 + par * 2560 + POFF[c], PW[c])

        def a2(a2stop=a2stop):
            p.wait("gpsimd", m_qkv_t)
            p.wait("gpsimd", m_dq_dn)
            p.wait("gpsimd", m_wout)
            ones_ap = vf(arA, A_V2, 8320).rearrange(
                "p (c h o) -> p c h o", h=16, o=65)[:, :, :, 64:65]
            p.emit("gpsimd", lambda h, s_: h.memset(ones_ap, 1.0), inc=1)
            m_v2ones = p.mark("gpsimd")
            p.wait("vector", m_qkv_t)
            p.wait("vector", m_dq_dn)
            p.wait("vector", m_wout)
            for i in range(4):
                p.emit("vector", lambda h, s_, i=i: h.memset(k2v(i)[0:1, :], 1.0))
                p.emit("vector", lambda h, s_, i=i: h.memset(qv(i)[0:2, :], 1.0),
                       inc=(1 if i == 3 else 0))
            m_v2ones_v = p.mark("vector")
            p.wait("sync", m_v2ones)
            p.wait("sync", m_v2ones_v)

            pool_free = [None] * 4
            p1_free = [None, None]
            sc_bank_free = [None, None]
            pv_bank_free = [None, None]
            pv_evac_done = [None, None]
            ot_free = None
            wo_free = [None, None]
            ych_free = [None, None]
            ymax_count = [0]
            m_last_y = None
            m_blk_v = None
            sq_guard = [None]
            mx_guard = [None]
            rmx_guard = [None]

            for n in range(NB):
                nsl = slice(n * 512, (n + 1) * 512)
                half = n % 2
                if ot_free:
                    p.wait("sync", ot_free)
                if n == 0:
                    p.wait("sync", m_dq_dn)
                mvl = p.dma("sync", lambda h, s_, nsl=nsl: h.dma_start(
                    out=vf(arA, A_OT, 4096).rearrange("p (a b) -> p a b", a=8),
                    in_=vscr[:, nsl].rearrange("(a p) b -> p a b", p=128)))
                p.wait("tensor", mvl)
                if n == 0:
                    p.wait("tensor", m_v2ones)
                    p.wait("tensor", m_v2ones_v)
                if m_blk_v:
                    p.wait("tensor", m_blk_v)
                for ec in range(NEC):
                    for tsub in range(4):
                        bank = psb[4 + tsub % 2]
                        phys = half * 4 + tsub
                        p.emit("tensor", lambda h, s_, ec=ec, tsub=tsub, bank=bank:
                               h.transpose(bank[:, 0:128],
                                           vf(arA, A_OT + ec * 512 + tsub * 128, 128),
                                           eyef[:, :]), inc=1)
                        mtr = p.mark("tensor")
                        p.wait("scalar", mtr)
                        dst = vf(arA, A_V2 + phys * 1040 + 2 * ec * 65, 130).rearrange(
                            "p (a b) -> p a b", a=2)[:, :, 0:64]
                        p.emit("scalar", lambda h, s_, bank=bank, dst=dst: h.copy(
                            dst, bank[:, 0:128].rearrange("p (a b) -> p a b", a=2)),
                               inc=1)
                        mev = p.mark("scalar")
                        p.wait("tensor", mev)
                m_v2 = p.mark("scalar")
                if a2stop == "v2":
                    return p.mark("vector"), None

                chunks = band_chunks(n)
                pv_state = [None]

                def emit_pv(stv):
                    (hh, gi, slot, par, exp_last) = stv
                    half = n % 2
                    pvb = psb[2 + gi % 2]
                    p.wait("tensor", exp_last)
                    if pv_evac_done[gi % 2]:
                        p.wait("tensor", pv_evac_done[gi % 2])
                    for j in range(4):
                        cs = [c for (c, ql, qh) in chunks
                              if ql <= 128 * j and 128 * j + 128 <= qh]
                        for k_, c in enumerate(cs):
                            qlo = [ql for (cc, ql, qh) in chunks if cc == c][0]
                            phys = (half * 4 + (c - 4)) if c >= 4 else \
                                   (((n - 1) % 2) * 4 + c)
                            p.emit("tensor", lambda h, s_, par=par, c=c, j=j,
                                   qlo=qlo, phys=phys, hh=hh, pvb=pvb, k_=k_,
                                   last=(k_ == len(cs) - 1):
                                   h.matmul(pvb[:, 65 * j:65 * j + 65],
                                            p1v(par, c)[:, 128 * j - qlo:
                                                        128 * j - qlo + 128],
                                            vf(arA, A_V2 + phys * 1040 + 65 * hh, 65),
                                            start=(k_ == 0), stop=last),
                                   inc=(1 if (j == 3 and k_ == len(cs) - 1) else 0))
                    m_pv = p.mark("tensor")
                    p1_free[par] = m_pv

                    p.wait("scalar", m_pv)
                    sums = pvb.ap()[:, 64:324].rearrange(
                        "p (a b) -> p a b", b=65)[:, :, 0]
                    p.emit("scalar", lambda h, s_, sums=sums, par=par: h.activation(
                        lnx[:, 4 * par:4 * par + 4], sums, AF.Ln), inc=1)
                    p.wait("scalar", p.mark("scalar"))
                    p.emit("scalar", lambda h, s_, par=par: h.activation(
                        rex[:, 4 * par:4 * par + 4], lnx[:, 4 * par:4 * par + 4],
                        AF.Exp, scale=-1.0), inc=1)
                    m_rex = p.mark("scalar")
                    p.wait("vector", m_rex)
                    for j in range(4):
                        p.emit("vector", lambda h, s_, j=j, hh=hh, pvb=pvb, par=par:
                               h.tensor_scalar(
                                   out=vf(arA, A_OT + j * 1024 + 64 * hh, 64),
                                   in0=pvb[:, 65 * j:65 * j + 64],
                                   scalar1=rex[:, 4 * par + j:4 * par + j + 1],
                                   scalar2=None,
                                   op0=ALU.mult), inc=(1 if j == 3 else 0))
                    pv_evac_done[gi % 2] = p.mark("vector")
                    pool_free[slot] = m_pv


                for ph in range(H + 1):
                    if ph < H:
                        hh = ph
                        gi = n * H + hh
                        slot = gi % 4
                        par = gi % 2
                        if pool_free[slot]:
                            p.wait("sync", pool_free[slot])
                        if n == 0:
                            mk = p.dma("sync", lambda h, s_, hh=hh, slot=slot:
                                       h.dma_start(out=k2v(slot)[2:66, 512:1024],
                                                   in_=kscr[64 * hh:64 * hh + 64, 0:512]))
                        else:
                            mk = p.dma("sync", lambda h, s_, hh=hh, n=n, slot=slot:
                                       h.dma_start(out=k2v(slot)[2:66, :],
                                                   in_=kscr[64 * hh:64 * hh + 64,
                                                            (n - 1) * 512:(n + 1) * 512]))
                        mb = p.dma("sync", lambda h, s_, hh=hh, slot=slot: h.dma_start(
                            out=k2v(slot)[1:2, :], in_=btab[hh:hh + 1, :]))
                        mq = p.dma("sync", lambda h, s_, hh=hh, nsl=nsl, slot=slot:
                                   h.dma_start(out=qv(slot)[2:66, :],
                                               in_=qscr[64 * hh:64 * hh + 64, nsl]))
                        p.wait("scalar", mq)
                        p.emit("scalar", lambda h, s_, hh=hh, slot=slot: h.activation(
                            qv(slot)[0:1, :], qramp[:, :], AF.Identity,
                            scale=float(SLOPES[hh])), inc=1)
                        m_arow = p.mark("scalar")

                        p.wait("tensor", mk)
                        p.wait("tensor", mb)
                        p.wait("tensor", mq)
                        p.wait("tensor", m_arow)
                        if gi == 0:
                            p.wait("tensor", m_v2)
                        if mx_guard[0]:
                            p.wait("tensor", mx_guard[0])
                        red_marks = []
                        for j in range(4):
                            bA = psb[6 if j % 2 == 0 else 4]
                            bB = psb[7 if j % 2 == 0 else 5]
                            if j >= 2:
                                p.wait("tensor", red_marks[j - 2])
                            elif mx_guard[0]:
                                p.wait("tensor", mx_guard[0])
                            if n == 0:
                                wA = 128 * (j + 1)
                                p.emit("tensor", lambda h, s_, slot=slot, j=j, wA=wA,
                                       bA=bA:
                                       h.matmul(bA[:, 0:wA],
                                                qv(slot)[0:66, 128 * j:128 * j + 128],
                                                k2v(slot)[0:66, 512:512 + wA],
                                                start=True, stop=False))
                                p.emit("tensor", lambda h, s_, j=j, bA=bA: h.matmul(
                                    bA[:, 128 * j:128 * j + 128], maskA[:, :],
                                    eyebf[:, :], start=False, stop=True), inc=1)
                                mT = p.mark("tensor")
                                p.wait("vector", mT)
                                p.emit("vector", lambda h, s_, j=j, wA=wA, bA=bA:
                                       h.tensor_reduce(out=rmcol[:, 8 + j:9 + j],
                                                       in_=bA[:, 0:wA], axis=AX.X,
                                                       op=ALU.max), inc=1)
                                red_marks.append(p.mark("vector"))
                            else:
                                p.emit("tensor", lambda h, s_, slot=slot, j=j, bA=bA:
                                       h.matmul(bA[:, 0:512],
                                                qv(slot)[0:66, 128 * j:128 * j + 128],
                                                k2v(slot)[0:66, 128 * j:128 * j + 512],
                                                start=True, stop=False))
                                p.emit("tensor", lambda h, s_, bA=bA: h.matmul(
                                    bA[:, 0:128], maskB[:, :], eyebf[:, :],
                                    start=False, stop=True))
                                p.emit("tensor", lambda h, s_, slot=slot, j=j, bB=bB:
                                       h.matmul(bB[:, 0:128],
                                                qv(slot)[0:66, 128 * j:128 * j + 128],
                                                k2v(slot)[0:66, 128 * j + 512:
                                                          128 * j + 640],
                                                start=True, stop=False))
                                p.emit("tensor", lambda h, s_, bB=bB: h.matmul(
                                    bB[:, 0:128], maskA[:, :], eyebf[:, :],
                                    start=False, stop=True), inc=1)
                                mT = p.mark("tensor")
                                p.wait("vector", mT)
                                p.emit("vector", lambda h, s_, j=j, bA=bA:
                                       h.tensor_reduce(
                                           out=rmcol[:, j:j + 1], in_=bA[:, 0:512],
                                           axis=AX.X, op=ALU.max))
                                p.emit("vector", lambda h, s_, j=j, bB=bB:
                                       h.tensor_reduce(
                                           out=rmcol[:, 4 + j:5 + j], in_=bB[:, 0:128],
                                           axis=AX.X, op=ALU.max), inc=1)
                                red_marks.append(p.mark("vector"))
                        if n > 0:
                            p.wait("vector", red_marks[-1])
                            p.wait("vector", p.mark("vector"))
                            for j in range(4):
                                p.emit("vector", lambda h, s_, j=j: h.tensor_tensor(
                                    out=rmcol[:, 8 + j:9 + j], in0=rmcol[:, j:j + 1],
                                    in1=rmcol[:, 4 + j:5 + j], op=ALU.max),
                                       inc=(1 if j == 3 else 0))

                    if ph > 0:
                        emit_pv(pv_state[0])
                    if ph < H:
                        m_comb = p.mark("vector")
                        p.wait("tensor", m_comb)
                        if rmx_guard[0]:
                            p.wait("tensor", rmx_guard[0])
                        for j in range(4):
                            p.emit("tensor", lambda h, s_, j=j: h.transpose(
                                psb[7][0:1, 128 * j:128 * j + 128],
                                rmcol[:, 8 + j:9 + j], eyef[:, :]),
                                   inc=(1 if j == 3 else 0))
                        m_rmt = p.mark("tensor")
                        p.wait("vector", m_rmt)
                        p.emit("vector", lambda h, s_, hh=hh, slot=slot:
                               h.scalar_tensor_tensor(
                                   out=qv(slot)[0:1, :], in0=qramp[:, :],
                                   scalar=float(SLOPES[hh]), in1=psb[7][0:1, 0:512],
                                   op0=ALU.mult, op1=ALU.subtract), inc=1)
                        m_row0 = p.mark("vector")
                        rmx_guard[0] = m_row0
                        mx_guard[0] = m_row0

                        p.wait("tensor", m_row0)
                        exp_last = None
                        for ci, (c, qlo, qhi) in enumerate(chunks):
                            bank = psb[ci % 2]
                            W = qhi - qlo
                            if sc_bank_free[ci % 2]:
                                p.wait("tensor", sc_bank_free[ci % 2])
                            diag = (n > 0 and c <= 3)
                            p.emit("tensor", lambda h, s_, slot=slot, c=c, qlo=qlo,
                                   qhi=qhi, bank=bank, W=W:
                                   h.matmul(bank[:, 0:W], k2v(slot)[0:66, ts(c)],
                                            qv(slot)[0:66, qlo:qhi],
                                            start=True, stop=False))
                            mcol = 128 * c if diag else 0
                            msk = maskA if diag else maskB
                            p.emit("tensor", lambda h, s_, bank=bank, mcol=mcol, msk=msk:
                                   h.matmul(bank[:, mcol:mcol + 128], msk[:, :],
                                            eyebf[:, :], start=False, stop=True),
                                   inc=1)
                            msc = p.mark("tensor")
                            p.wait("scalar", msc)
                            if p1_free[par]:
                                p.wait("scalar", p1_free[par])
                            p.emit("scalar", lambda h, s_, par=par, c=c, bank=bank,
                                   W=W, hh=hh:
                                   h.activation(p1v(par, c)[:, 0:W], bank[:, 0:W],
                                                AF.Exp, bias=u20col[:, :]), inc=1)
                            sc_bank_free[ci % 2] = p.mark("scalar")
                            exp_last = sc_bank_free[ci % 2]
                        p1_free[par] = None


                        pv_state[0] = (hh, gi, slot, par, exp_last)
                m_heads_v = p.mark("vector")
                if odbg is not None:
                    p.wait("gpsimd", m_heads_v)
                    for j in range(4):
                        p.dma("gpsimd", lambda h, s_, n=n, j=j: h.dma_start(
                            out=odbg[n * 512 + j * 128:n * 512 + j * 128 + 128, :],
                            in_=vf(arA, A_OT + j * 1024, 1024)))

                p.wait("tensor", m_heads_v)
                for ec in range(NEC):
                    bank = psb[4 + ec % 2]
                    for j in range(4):
                        p.emit("tensor", lambda h, s_, ec=ec, j=j, bank=bank:
                               h.transpose(bank[:, ts(j)],
                                           vf(arA, A_OT + j * 1024 + ts(ec).start, 128),
                                           eyef[:, :]), inc=(1 if j == 3 else 0))
                    mtr = p.mark("tensor")
                    p.wait("scalar", mtr)
                    p.emit("scalar", lambda h, s_, ec=ec, bank=bank: h.copy(
                        vb(arC, C_OTH + ec * 512, 512), bank[:, 0:512]), inc=1)
                    mhi = p.mark("scalar")
                    p.wait("vector", mhi)
                    p.emit("vector", lambda h, s_, ec=ec, bank=bank: h.tensor_tensor(
                        out=vb(arC, C_OTL + ec * 512, 512), in0=bank[:, 0:512],
                        in1=vb(arC, C_OTH + ec * 512, 512), op=ALU.subtract), inc=1)
                    mlo = p.mark("vector")
                    p.wait("tensor", mlo)
                ot_free = p.mark("tensor")
                m_osplit = p.mark("vector")
                if a2stop == "osplit":
                    return p.mark("vector"), None

                if m_blk_v:
                    p.wait("sync", m_blk_v)
                mxr = p.dma("sync", lambda h, s_, nsl=nsl: h.dma_start(
                    out=vf(arB, B_XRES, 4096).rearrange("p (a b) -> p a b", a=8),
                    in_=x_t[:, nsl].rearrange("(a p) b -> p a b", p=128)))

                for fc in range(NEC):
                    wbuf = fc % 2
                    if wo_free[wbuf]:
                        p.wait("sync", wo_free[wbuf])
                    mwh = p.dma("sync", lambda h, s_, fc=fc, wbuf=wbuf: h.dma_start(
                        out=vb(arC, C_WOH[wbuf], 1024),
                        in_=wohscr[fc, :, :, :].rearrange("p a b -> p (a b)")))
                    mwl = p.dma("sync", lambda h, s_, fc=fc, wbuf=wbuf: h.dma_start(
                        out=vb(arC, C_WOL[wbuf], 1024),
                        in_=wolscr[fc, :, :, :].rearrange("p a b -> p (a b)")))
                    bank = psb[4 + fc % 2]
                    p.wait("tensor", mwh)
                    p.wait("tensor", mwl)
                    if fc == 0:
                        p.wait("tensor", m_osplit)
                        p.wait("tensor", m_wout)
                    for pa, (woff, ooff) in enumerate(
                            ((C_WOH[wbuf], C_OTH), (C_WOH[wbuf], C_OTL),
                             (C_WOL[wbuf], C_OTH))):
                        for ec in range(NEC):
                            st_ = (pa == 0 and ec == 0)
                            sp_ = (pa == 2 and ec == NEC - 1)
                            p.emit("tensor", lambda h, s_, woff=woff, ooff=ooff,
                                   ec=ec, bank=bank, st=st_, sp=sp_:
                                   h.matmul(bank[:, :],
                                            vb(arC, woff + ec * 128, 128),
                                            vb(arC, ooff + ec * 512, 512),
                                            start=st, stop=sp),
                                   inc=(1 if sp_ else 0))
                    mmm = p.mark("tensor")
                    wo_free[wbuf] = mmm
                    p.wait("vector", mmm)
                    p.wait("vector", mxr)
                    p.emit("vector", lambda h, s_, fc=fc, bank=bank: h.tensor_tensor(
                        out=vf(arB, B_YPRE + fc * 512, 512), in0=bank[:, :],
                        in1=vf(arB, B_XRES + fc * 512, 512), op=ALU.add), inc=1)
                    mypre = p.mark("vector")
                    p.wait("scalar", mypre)
                    if sq_guard[0]:
                        p.wait("scalar", sq_guard[0])
                    p.emit("scalar", lambda h, s_, fc=fc: h.activation(
                        vf(arB, B_SQ, 512), vf(arB, B_YPRE + fc * 512, 512),
                        AF.Square), inc=1)
                    msq = p.mark("scalar")
                    p.wait("tensor", mypre)
                    p.emit("tensor", lambda h, s_, fc=fc: h.matmul(
                        psb[6][0:1, 0:512], onescol[:, :],
                        vf(arB, B_YPRE + fc * 512, 512),
                        start=(fc == 0), stop=(fc == NEC - 1)))
                    p.wait("tensor", msq)
                    p.emit("tensor", lambda h, s_, fc=fc: h.matmul(
                        psb[7][0:1, 0:512], onescol[:, :], vf(arB, B_SQ, 512),
                        start=(fc == 0), stop=(fc == NEC - 1)), inc=1)
                    sq_guard[0] = p.mark("tensor")
                m_stats = p.mark("tensor")

                p.wait("vector", m_stats)
                p.emit("vector", lambda h, s_: h.tensor_scalar(
                    out=mrow[:, :], in0=psb[6][0:1, 0:512], scalar1=1.0 / 1024.0,
                    scalar2=None, op0=ALU.mult), inc=1)
                mm1 = p.mark("vector")
                p.wait("scalar", mm1)
                p.emit("scalar", lambda h, s_: h.activation(
                    rrow[:, :], mrow[:, :], AF.Square), inc=1)
                mm2 = p.mark("scalar")
                p.wait("vector", mm2)
                p.emit("vector", lambda h, s_: h.tensor_scalar(
                    out=vrow[:, :], in0=psb[7][0:1, 0:512], scalar1=1.0 / 1024.0,
                    scalar2=None, op0=ALU.mult), inc=1)
                p.wait("vector", p.mark("vector"))
                p.emit("vector", lambda h, s_: h.tensor_tensor(
                    out=vrow[:, :], in0=vrow[:, :], in1=rrow[:, :],
                    op=ALU.subtract), inc=1)
                mm3 = p.mark("vector")
                p.wait("scalar", mm3)
                p.emit("scalar", lambda h, s_: h.activation(
                    rrow[:, :], vrow[:, :], AF.Ln, bias=eps1[:, :]), inc=1)
                p.wait("scalar", p.mark("scalar"))
                p.emit("scalar", lambda h, s_: h.activation(
                    rrow[:, :], rrow[:, :], AF.Exp, scale=-0.5), inc=1)
                mm4 = p.mark("scalar")
                p.wait("vector", mm4)
                p.wait("vector", p.mark("vector"))
                p.emit("vector", lambda h, s_: h.tensor_tensor(
                    out=mrrow[:, :], in0=mrow[:, :], in1=rrow[:, :],
                    op=ALU.mult), inc=1)
                mm5 = p.mark("vector")
                p.wait("tensor", mm5)
                p.emit("tensor", lambda h, s_: h.matmul(
                    psb[4][:, :], onesrow[:, :], rrow[:, :], start=True, stop=True))
                p.emit("tensor", lambda h, s_: h.matmul(
                    psb[5][:, :], onesrow[:, :], mrrow[:, :], start=True, stop=True),
                       inc=1)
                m_bc = p.mark("tensor")

                for fc in range(NEC):
                    yb = fc % 2
                    p.wait("vector", m_bc)
                    if ych_free[yb]:
                        p.wait("vector", ych_free[yb])
                    p.emit("vector", lambda h, s_, fc=fc: h.tensor_tensor(
                        out=vf(arB, B_TMP, 512), in0=vf(arB, B_YPRE + fc * 512, 512),
                        in1=psb[4][:, :], op=ALU.mult))
                    p.emit("vector", lambda h, s_, fc=fc, yb=yb: h.tensor_tensor(
                        out=vf(arB, B_YCH[yb], 512), in0=vf(arB, B_TMP, 512),
                        in1=psb[5][:, :], op=ALU.subtract))
                    p.emit("vector", lambda h, s_, yb=yb: h.tensor_reduce(
                        out=tiny[:, 5:6], in_=vf(arB, B_YCH[yb], 512), axis=AX.X,
                        op=ALU.max, apply_absolute_value=True), inc=1)
                    p.wait("vector", p.mark("vector"))
                    if n == 0 and fc == 0:
                        p.emit("vector", lambda h, s_: h.tensor_copy(
                            rcol[:, 0:1], tiny[:, 5:6]), inc=1)
                    else:
                        p.emit("vector", lambda h, s_: h.tensor_tensor(
                            out=rcol[:, 0:1], in0=rcol[:, 0:1],
                            in1=tiny[:, 5:6], op=ALU.max), inc=1)
                    myv = p.mark("vector")
                    p.wait("gpsimd", myv)
                    ych_free[yb] = p.dma("gpsimd", lambda h, s_, fc=fc, nsl=nsl,
                                         yb=yb: h.dma_start(
                                             out=yscr[ts(fc), nsl],
                                             in_=vf(arB, B_YCH[yb], 512)))
                    m_last_y = ych_free[yb]
                if odbg is not None:
                    modbg = ("dma_gpsimd", p.dma_counts["gpsimd"])
                    p.wait("vector", modbg)
                m_blk_v = p.mark("vector")
            return (p.mark("vector"), p.mark("tensor"), p.mark("scalar"),
                    p.mark("gpsimd"), m_last_y)

        m_a2_v, m_a2_t, m_a2_s, m_a2_g, m_a2_y = a2()

        CB_SH = [0, 2048]
        CB_SL = [4096, 6144]
        CB_ST = [8192, 9216, 10240, 11264]
        CB_E = [12288 + i * 512 for i in range(8)]
        PB_SLOT = [[0, 1024, 2048, 3072, 8192, 9216],
                   [4096, 5120, 6144, 7168, 10240, 12288]]
        P79 = ([(arC, o) for o in PB_SLOT[0]],
               [(arC, o) for o in PB_SLOT[1]],
               [(arA, o) for o in (0, 1024, 2048, 3072, 4096, 5120)],
               [(arB, o) for o in (0, 1024, 2048, 3072, 4096, 5120)])

        def sv(S_, k):
            return vf(S_[k][0], S_[k][1], 1024)
        CB_OH = 0
        CB_OL = 12288
        CB_Y = [12288, 12800, 13312]
        P11_ST = [12800, 13312, 13824, 14336, 14848, 15360, 15872]

        def phase_barrier(eng_list=("sync", "tensor", "vector", "scalar", "gpsimd")):
            ms = {e: p.mark(e) for e in ENGINES}
            dm = {e: ("dma_" + e, p.dma_counts[e]) for e in ENGINES}
            for e in eng_list:
                for e2 in ENGINES:
                    if e2 != e:
                        p.wait(e, ms[e2])
                    p.wait(e, dm[e2])

        def collective(slot_marker, k):
            p.wait(G, slot_marker)
            mi = p.dma(G, lambda h, s_: h.dma_start(out=ccin[:, :], in_=ccs[:, :]))
            p.wait(G, mi)
            p.emit(G, lambda h, s_: h.collective_compute(
                "AllReduce", ALU.max, replica_groups=[list(range(ncores))],
                ins=[ccin[:, :]], outs=[ccout[:, :]]), inc=1)
            mc = p.mark(G)
            p.wait(G, mc)
            mo = p.dma(G, lambda h, s_: h.dma_start(out=ccs[:, :], in_=ccout[:, :]))
            return mo

        def make_scale(k, src_col, extra_pow2=1.0):
            p.emit("vector", lambda h, s_, k=k, c=src_col: h.tensor_scalar(
                out=ccs[:, c:c + 1], in0=ccs[:, c:c + 1], scalar1=r327[0:1, :],
                scalar2=None, op0=ALU.mult), inc=1)
            p.wait("vector", p.mark("vector"))
            if extra_pow2 != 1.0:
                p.emit("vector", lambda h, s_, k=k, c=src_col, e=extra_pow2:
                       h.tensor_scalar(out=ccs[:, c:c + 1], in0=ccs[:, c:c + 1],
                                       scalar1=e, scalar2=None, op0=ALU.mult),
                       inc=1)
                p.wait("vector", p.mark("vector"))
            p.emit("vector", lambda h, s_, c=src_col: h.tensor_scalar(
                out=ccs[:, c:c + 1], in0=ccs[:, c:c + 1], scalar1=1e-12,
                scalar2=None, op0=ALU.max), inc=1)
            mv = p.mark("vector")
            p.wait("tensor", mv)
            p.emit("tensor", lambda h, s_, c=src_col: h.matmul(
                psb[6][:, 384 + c:385 + c], onesrow[:, :], ccs[:, c:c + 1],
                start=True, stop=True), inc=1)
            mt = p.mark("tensor")
            p.wait("vector", mt)
            p.emit("vector", lambda h, s_, k=k, c=src_col: h.tensor_copy(
                sccol[:, k:k + 1], psb[6][:, 384 + c:385 + c]), inc=1)
            p.wait("vector", p.mark("vector"))
            p.emit("vector", lambda h, s_, k=k: h.reciprocal(
                sicol[:, k:k + 1], sccol[:, k:k + 1]), inc=1)
            p.wait("vector", p.mark("vector"))
            return p.mark("vector")

        def phase_b():
            phase_barrier()
            p.emit("vector", lambda h, s_: h.memset(ccs[:, :], 0.0))
            p.emit("vector", lambda h, s_: h.memset(r327[:, :], 32767.0), inc=1)
            p.wait("vector", p.mark("vector"))
            p.emit("vector", lambda h, s_: h.reciprocal(r327[:, :], r327[:, :]),
                   inc=1)
            p.wait("vector", p.mark("vector"))
            p.emit("vector", lambda h, s_: h.tensor_copy(
                tiny[:, 6:7], rcol[:, 0:1]), inc=1)
            mty = p.mark("vector")
            p.wait("tensor", mty)
            p.emit("tensor", lambda h, s_: h.transpose(
                psb[7][0:1, 0:128], tiny[:, 6:7], eyef[:, :]), inc=1)
            mtt = p.mark("tensor")
            p.wait("vector", mtt)
            p.emit("vector", lambda h, s_: h.tensor_reduce(
                out=ccs[:, 0:1], in_=psb[7][0:1, 0:128], axis=AX.X, op=ALU.max),
                   inc=1)
            mred = p.mark("vector")
            mcc = collective(mred, 0)
            p.wait("vector", mcc)
            m_sx = make_scale(0, 0)

            p.wait("sync", m_sx)
            set_guard4 = [None, None]
            for it in range(32):
                ec, th = divmod(it, 4)
                st_ = it % 2
                L, T_ = PB_SLOT[st_][0], PB_SLOT[st_][1]
                if set_guard4[st_]:
                    p.wait("sync", set_guard4[st_])
                mld = p.dma("sync", lambda h, s_, ec=ec, th=th, L=L: h.dma_start(
                    out=vf(arC, L, 1024),
                    in_=yscr[ts(ec), th * 1024:(th + 1) * 1024]))
                p.wait("scalar", mld)
                p.emit("scalar", lambda h, s_, L=L, T_=T_: h.activation(
                    vf(arC, T_, 1024), vf(arC, L, 1024), AF.Identity,
                    scale=sicol[:, 0:1], bias=mcol[:, :]), inc=1)
                m1 = p.mark("scalar")
                p.wait("vector", m1)
                p.emit("vector", lambda h, s_, T_=T_: h.tensor_scalar(
                    out=vf(arC, T_, 1024), in0=vf(arC, T_, 1024),
                    scalar1=-MAGIC, scalar2=sccol[:, 0:1], op0=ALU.add,
                    op1=ALU.mult), inc=1)
                m2 = p.mark("vector")
                p.wait("scalar", m2)
                p.emit("scalar", lambda h, s_, ec=ec, th=th, T_=T_: h.copy(
                    vb(arA, ec * 4096 + th * 1024, 1024), vf(arC, T_, 1024)),
                       inc=1)
                mhi = p.mark("scalar")
                p.wait("vector", mhi)
                p.emit("vector", lambda h, s_, ec=ec, th=th, T_=T_: h.tensor_tensor(
                    out=vb(arB, ec * 4096 + th * 1024, 1024),
                    in0=vf(arC, T_, 1024),
                    in1=vb(arA, ec * 4096 + th * 1024, 1024), op=ALU.subtract),
                       inc=1)
                set_guard4[st_] = p.mark("vector")
            m_xq = set_guard4[1]

            p.wait("tensor", m_xq)
            ev_guard = [None] * 4
            bank_guard = [None, None]
            st_guard = [m_xq, m_xq]
            rix2 = [0]
            for wi, (hscr, lscr, oscr_) in enumerate(
                    ((upwh, upwl, upscr), (gawh, gawl, gascr))):
                for fg in range(NFC):
                    buf = fg % 2
                    if st_guard[buf]:
                        p.wait("sync", st_guard[buf])
                    mwh = p.dma("sync", lambda h, s_, hscr=hscr, fg=fg, buf=buf:
                                h.dma_start(out=vb(arC, CB_SH[buf], 1024),
                                            in_=hscr[fg, :, :]))
                    mwl = p.dma("sync", lambda h, s_, lscr=lscr, fg=fg, buf=buf:
                                h.dma_start(out=vb(arC, CB_SL[buf], 1024),
                                            in_=lscr[fg, :, :]))
                    p.wait("tensor", mwh)
                    p.wait("tensor", mwl)
                    for tc in range(NTC):
                        bank = psb[tc % 2]
                        if bank_guard[tc % 2]:
                            p.wait("tensor", bank_guard[tc % 2])
                        for pa, (woff, xar) in enumerate(
                                ((CB_SH[buf], arA), (CB_SH[buf], arB),
                                 (CB_SL[buf], arA))):
                            for ec in range(NEC):
                                st_ = (pa == 0 and ec == 0)
                                sp_ = (pa == 2 and ec == NEC - 1)
                                p.emit("tensor", lambda h, s_, woff=woff, xar=xar,
                                       ec=ec, tc=tc, bank=bank, st=st_, sp=sp_:
                                       h.matmul(bank[:, :],
                                                vb(arC, woff + ec * 128, 128),
                                                vb(xar, ec * 4096 + tc * 512, 512),
                                                start=st, stop=sp),
                                       inc=(1 if sp_ else 0))
                        mmm = p.mark("tensor")
                        sti = (fg * NTC + tc) % 4
                        p.wait("scalar", mmm)
                        if ev_guard[sti]:
                            p.wait("scalar", ev_guard[sti])
                        p.emit("scalar", lambda h, s_, bank=bank, sti=sti: h.copy(
                            vf(arC, CB_ST[sti], 512), bank[:, :]), inc=1)
                        mev = p.mark("scalar")
                        bank_guard[tc % 2] = mev
                        p.wait("vector", mev)
                        p.emit("vector", lambda h, s_, sti=sti: h.tensor_reduce(
                            out=tiny[:, 7:8], in_=vf(arC, CB_ST[sti], 512),
                            axis=AX.X, op=ALU.max, apply_absolute_value=True),
                               inc=1)
                        p.wait("vector", p.mark("vector"))
                        if rix2[0] == 0 or (wi == 1 and fg == 0 and tc == 0):
                            p.emit("vector", lambda h, s_, wi=wi: h.tensor_copy(
                                rcol[:, 1 + wi:2 + wi], tiny[:, 7:8]), inc=1)
                        else:
                            p.emit("vector", lambda h, s_, wi=wi: h.tensor_tensor(
                                out=rcol[:, 1 + wi:2 + wi],
                                in0=rcol[:, 1 + wi:2 + wi], in1=tiny[:, 7:8],
                                op=ALU.max), inc=1)
                        rix2[0] += 1
                        mrc = p.mark("vector")
                        p.wait("gpsimd", mrc)
                        mst = p.dma("gpsimd", lambda h, s_, oscr_=oscr_, fg=fg,
                                    tc=tc, sti=sti: h.dma_start(
                                        out=oscr_[ts(fg), tc * 512:(tc + 1) * 512],
                                        in_=vf(arC, CB_ST[sti], 512)))
                        ev_guard[sti] = mst
                    st_guard[buf] = p.mark("tensor")
            m_p5_v = p.mark("vector")

            p.wait("vector", m_p5_v)
            for wi in range(2):
                p.emit("vector", lambda h, s_, wi=wi: h.tensor_copy(
                    tiny[:, 6:7], rcol[:, 1 + wi:2 + wi]), inc=1)
                p.wait("tensor", p.mark("vector"))
                p.emit("tensor", lambda h, s_, wi=wi: h.transpose(
                    psb[7][0:1, 128 * wi:128 * wi + 128], tiny[:, 6:7],
                    eyef[:, :]), inc=1)
                mq = p.mark("tensor")
                p.wait("vector", mq)
                p.emit("vector", lambda h, s_, wi=wi: h.tensor_reduce(
                    out=ccs[:, 1 + wi:2 + wi],
                    in_=psb[7][0:1, 128 * wi:128 * wi + 128], axis=AX.X,
                    op=ALU.max), inc=1)
            mred2 = p.mark("vector")
            mcc2 = collective(mred2, 1)
            p.wait("vector", mcc2)
            make_scale(1, 1)
            m_sg = make_scale(2, 2)

            p.wait("sync", m_sg)
            for m_ in ev_guard:
                p.wait("scalar", m_)
                p.wait("vector", m_)
            set_g7 = [None] * 4
            first7 = [True] * 4
            for it in range(NFC * 4):
                fg, th = divmod(it, 4)
                st_ = it % 4
                S_ = P79[st_]
                if set_g7[st_]:
                    p.wait("sync", set_g7[st_])
                mld = p.dma("sync", lambda h, s_, fg=fg, th=th, S_=S_: h.dma_start(
                    out=sv(S_, 1),
                    in_=gascr[ts(fg), th * 1024:(th + 1) * 1024]))
                p.wait("scalar", mld)
                p.emit("scalar", lambda h, s_, S_=S_: h.activation(
                    sv(S_, 3), sv(S_, 1), AF.Identity,
                    scale=sicol[:, 2:3], bias=mcol[:, :]), inc=1)
                m1 = p.mark("scalar")
                p.wait("vector", m1)
                p.emit("vector", lambda h, s_, S_=S_: h.tensor_scalar(
                    out=sv(S_, 3), in0=sv(S_, 3),
                    scalar1=-MAGIC, scalar2=sccol[:, 2:3], op0=ALU.add,
                    op1=ALU.mult), inc=1)
                mg1 = p.mark("vector")
                p.wait("scalar", mg1)
                p.emit("scalar", lambda h, s_, S_=S_: h.activation(
                    sv(S_, 4), sv(S_, 3), AF.Sigmoid), inc=1)
                p.wait("scalar", p.mark("scalar"))
                p.emit("scalar", lambda h, s_, S_=S_: h.activation(
                    sv(S_, 4), sv(S_, 4), AF.Identity,
                    scale=65536.0, bias=mcol[:, :]), inc=1)
                m3 = p.mark("scalar")
                p.wait("vector", m3)
                p.emit("vector", lambda h, s_, S_=S_: h.tensor_scalar(
                    out=sv(S_, 4), in0=sv(S_, 4),
                    scalar1=-MAGIC, scalar2=UMAX, op0=ALU.add, op1=ALU.min),
                       inc=1)
                p.wait("vector", p.mark("vector"))
                p.emit("vector", lambda h, s_, S_=S_: h.tensor_tensor(
                    out=sv(S_, 5), in0=sv(S_, 3),
                    in1=sv(S_, 4), op=ALU.mult), inc=1)
                p.wait("vector", p.mark("vector"))
                p.emit("vector", lambda h, s_, S_=S_, st_=st_: h.tensor_reduce(
                    out=tiny[:, st_:st_ + 1], in_=sv(S_, 5), axis=AX.X,
                    op=ALU.max, apply_absolute_value=True), inc=1)
                p.wait("vector", p.mark("vector"))
                if first7[st_]:
                    p.emit("vector", lambda h, s_, st_=st_: h.tensor_copy(
                        tiny[:, 4 + st_:5 + st_], tiny[:, st_:st_ + 1]), inc=1)
                    first7[st_] = False
                else:
                    p.emit("vector", lambda h, s_, st_=st_: h.tensor_tensor(
                        out=tiny[:, 4 + st_:5 + st_],
                        in0=tiny[:, 4 + st_:5 + st_],
                        in1=tiny[:, st_:st_ + 1], op=ALU.max), inc=1)
                set_g7[st_] = p.mark("vector")
            p.wait("vector", p.mark("vector"))
            p.emit("vector", lambda h, s_: h.tensor_tensor(
                out=tiny[:, 4:5], in0=tiny[:, 4:5], in1=tiny[:, 5:6],
                op=ALU.max), inc=1)
            p.emit("vector", lambda h, s_: h.tensor_tensor(
                out=tiny[:, 6:7], in0=tiny[:, 6:7], in1=tiny[:, 7:8],
                op=ALU.max), inc=1)
            p.wait("vector", p.mark("vector"))
            p.emit("vector", lambda h, s_: h.tensor_tensor(
                out=rcol[:, 3:4], in0=tiny[:, 4:5], in1=tiny[:, 6:7],
                op=ALU.max), inc=1)
            prev7 = p.mark("vector")

            p.wait("vector", prev7)
            p.emit("vector", lambda h, s_: h.tensor_copy(
                tiny[:, 6:7], rcol[:, 3:4]), inc=1)
            p.wait("tensor", p.mark("vector"))
            p.emit("tensor", lambda h, s_: h.transpose(
                psb[7][0:1, 0:128], tiny[:, 6:7], eyef[:, :]), inc=1)
            mq3 = p.mark("tensor")
            p.wait("vector", mq3)
            p.emit("vector", lambda h, s_: h.tensor_reduce(
                out=ccs[:, 3:4], in_=psb[7][0:1, 0:128], axis=AX.X, op=ALU.max),
                   inc=1)
            mred3 = p.mark("vector")
            mcc3 = collective(mred3, 3)
            p.wait("vector", mcc3)
            m_sg2 = make_scale(3, 3, extra_pow2=float(2.0 ** -16))
            p.wait("vector", m_sg2)
            p.emit("vector", lambda h, s_: h.tensor_scalar(
                out=sicol[:, 6:7], in0=sicol[:, 3:4], scalar1=float(2.0 ** -16),
                scalar2=None, op0=ALU.mult), inc=1)
            p.wait("vector", p.mark("vector"))
            m_sg2b = p.mark("vector")

            p.wait("sync", m_sg2b)
            set_g9 = [None] * 4
            first9 = [True] * 4
            for it in range(NFC * 4):
                fg, th = divmod(it, 4)
                st_ = it % 4
                S_ = P79[st_]
                if set_g9[st_]:
                    p.wait("sync", set_g9[st_])
                mldu = p.dma("sync", lambda h, s_, fg=fg, th=th, S_=S_: h.dma_start(
                    out=sv(S_, 0),
                    in_=upscr[ts(fg), th * 1024:(th + 1) * 1024]))
                mldg = p.dma("sync", lambda h, s_, fg=fg, th=th, S_=S_: h.dma_start(
                    out=sv(S_, 1),
                    in_=gascr[ts(fg), th * 1024:(th + 1) * 1024]))
                p.wait("scalar", mldu)
                p.wait("scalar", mldg)
                p.emit("scalar", lambda h, s_, S_=S_: h.activation(
                    sv(S_, 2), sv(S_, 0), AF.Identity,
                    scale=sicol[:, 1:2], bias=mcol[:, :]), inc=1)
                m1 = p.mark("scalar")
                p.wait("vector", m1)
                p.emit("vector", lambda h, s_, S_=S_: h.tensor_scalar(
                    out=sv(S_, 2), in0=sv(S_, 2),
                    scalar1=-MAGIC, scalar2=sccol[:, 1:2], op0=ALU.add,
                    op1=ALU.mult), inc=1)
                p.emit("scalar", lambda h, s_, S_=S_: h.activation(
                    sv(S_, 3), sv(S_, 1), AF.Identity,
                    scale=sicol[:, 2:3], bias=mcol[:, :]), inc=1)
                m2 = p.mark("scalar")
                p.wait("vector", m2)
                p.emit("vector", lambda h, s_, S_=S_: h.tensor_scalar(
                    out=sv(S_, 3), in0=sv(S_, 3),
                    scalar1=-MAGIC, scalar2=sccol[:, 2:3], op0=ALU.add,
                    op1=ALU.mult), inc=1)
                mg1 = p.mark("vector")
                p.wait("scalar", mg1)
                p.emit("scalar", lambda h, s_, S_=S_: h.activation(
                    sv(S_, 4), sv(S_, 3), AF.Sigmoid), inc=1)
                p.wait("scalar", p.mark("scalar"))
                p.emit("scalar", lambda h, s_, S_=S_: h.activation(
                    sv(S_, 4), sv(S_, 4), AF.Identity,
                    scale=65536.0, bias=mcol[:, :]), inc=1)
                m3 = p.mark("scalar")
                p.wait("vector", m3)
                p.emit("vector", lambda h, s_, S_=S_: h.tensor_scalar(
                    out=sv(S_, 4), in0=sv(S_, 4),
                    scalar1=-MAGIC, scalar2=UMAX, op0=ALU.add, op1=ALU.min),
                       inc=1)
                p.wait("vector", p.mark("vector"))
                p.emit("vector", lambda h, s_, S_=S_: h.tensor_tensor(
                    out=sv(S_, 5), in0=sv(S_, 3),
                    in1=sv(S_, 4), op=ALU.mult), inc=1)
                mu_ = p.mark("vector")
                p.wait("scalar", mu_)
                p.emit("scalar", lambda h, s_, S_=S_: h.activation(
                    sv(S_, 5), sv(S_, 5), AF.Identity,
                    scale=sicol[:, 6:7], bias=mcol[:, :]), inc=1)
                m4 = p.mark("scalar")
                p.wait("vector", m4)
                p.emit("vector", lambda h, s_, S_=S_: h.tensor_scalar(
                    out=sv(S_, 5), in0=sv(S_, 5),
                    scalar1=-MAGIC, scalar2=sccol[:, 3:4], op0=ALU.add,
                    op1=ALU.mult), inc=1)
                p.wait("vector", p.mark("vector"))
                p.emit("vector", lambda h, s_, S_=S_: h.tensor_tensor(
                    out=sv(S_, 5), in0=sv(S_, 5),
                    in1=sv(S_, 2), op=ALU.mult), inc=1)
                p.wait("vector", p.mark("vector"))
                p.emit("vector", lambda h, s_, S_=S_, st_=st_: h.tensor_reduce(
                    out=tiny[:, st_:st_ + 1], in_=sv(S_, 5), axis=AX.X,
                    op=ALU.max, apply_absolute_value=True), inc=1)
                p.wait("vector", p.mark("vector"))
                if first9[st_]:
                    p.emit("vector", lambda h, s_, st_=st_: h.tensor_copy(
                        tiny[:, 4 + st_:5 + st_], tiny[:, st_:st_ + 1]), inc=1)
                    first9[st_] = False
                else:
                    p.emit("vector", lambda h, s_, st_=st_: h.tensor_tensor(
                        out=tiny[:, 4 + st_:5 + st_],
                        in0=tiny[:, 4 + st_:5 + st_],
                        in1=tiny[:, st_:st_ + 1], op=ALU.max), inc=1)
                mo9 = p.mark("vector")
                p.wait("gpsimd", mo9)
                set_g9[st_] = p.dma("gpsimd", lambda h, s_, fg=fg, th=th, S_=S_:
                                    h.dma_start(
                    out=oscr[ts(fg), th * 1024:(th + 1) * 1024],
                    in_=sv(S_, 5)))
            p.wait("vector", p.mark("vector"))
            p.emit("vector", lambda h, s_: h.tensor_tensor(
                out=tiny[:, 4:5], in0=tiny[:, 4:5], in1=tiny[:, 5:6],
                op=ALU.max), inc=1)
            p.emit("vector", lambda h, s_: h.tensor_tensor(
                out=tiny[:, 6:7], in0=tiny[:, 6:7], in1=tiny[:, 7:8],
                op=ALU.max), inc=1)
            p.wait("vector", p.mark("vector"))
            p.emit("vector", lambda h, s_: h.tensor_tensor(
                out=rcol[:, 4:5], in0=tiny[:, 4:5], in1=tiny[:, 6:7],
                op=ALU.max), inc=1)
            p.wait("vector", p.mark("vector"))
            prev9 = ("dma_gpsimd", p.dma_counts["gpsimd"])

            p.wait("vector", prev9)
            p.emit("vector", lambda h, s_: h.tensor_copy(
                tiny[:, 6:7], rcol[:, 4:5]), inc=1)
            p.wait("tensor", p.mark("vector"))
            p.emit("tensor", lambda h, s_: h.transpose(
                psb[7][0:1, 0:128], tiny[:, 6:7], eyef[:, :]), inc=1)
            mq4 = p.mark("tensor")
            p.wait("vector", mq4)
            p.emit("vector", lambda h, s_: h.tensor_reduce(
                out=ccs[:, 4:5], in_=psb[7][0:1, 0:128], axis=AX.X, op=ALU.max),
                   inc=1)
            mred4 = p.mark("vector")
            mcc4 = collective(mred4, 4)
            p.wait("vector", mcc4)
            m_so = make_scale(4, 4)

            phase_barrier()
            mdn1 = p.dma("sync", lambda h, s_: h.dma_start(
                out=arA.ap()[:, 0:24576].rearrange("p (a b c) -> p a b c",
                                                   a=8, b=3),
                in_=dnwh[:, :, :, :].transpose((2, 0, 1, 3))))
            mdn2 = p.dma("sync", lambda h, s_: h.dma_start(
                out=arB.ap()[:, 0:24576].rearrange("p (a b c) -> p a b c",
                                                   a=8, b=3),
                in_=dnwl[:, :, :, :].transpose((2, 0, 1, 3))))
            oq_guard = [None] * 24
            bank_guard2 = [None, None]
            out_guard = [None, None]
            m_out_last = None
            for tc in range(NTC):
                for ic in range(NFC):
                    if oq_guard[ic]:
                        p.wait("sync", oq_guard[ic])
                    mld = p.dma("sync", lambda h, s_, ic=ic, tc=tc: h.dma_start(
                        out=vf(arC, P11_ST[0], 512),
                        in_=oscr[ts(ic), tc * 512:(tc + 1) * 512]))
                    p.wait("scalar", mld)
                    p.emit("scalar", lambda h, s_: h.activation(
                        vf(arC, P11_ST[1], 512), vf(arC, P11_ST[0], 512),
                        AF.Identity, scale=sicol[:, 4:5], bias=mcol[:, :]), inc=1)
                    mt1 = p.mark("scalar")
                    p.wait("vector", mt1)
                    p.wait("sync", mt1)
                    p.emit("vector", lambda h, s_: h.tensor_scalar(
                        out=vf(arC, P11_ST[2], 512), in0=vf(arC, P11_ST[1], 512),
                        scalar1=-MAGIC, scalar2=sccol[:, 4:5], op0=ALU.add,
                        op1=ALU.mult), inc=1)
                    mt2 = p.mark("vector")
                    p.wait("scalar", mt2)
                    p.emit("scalar", lambda h, s_, ic=ic: h.copy(
                        vb(arC, CB_OH + ic * 512, 512), vf(arC, P11_ST[2], 512)),
                           inc=1)
                    mhi = p.mark("scalar")
                    p.wait("vector", mhi)
                    p.emit("vector", lambda h, s_, ic=ic: h.tensor_tensor(
                        out=vb(arC, CB_OL + ic * 512, 512),
                        in0=vf(arC, P11_ST[2], 512),
                        in1=vb(arC, CB_OH + ic * 512, 512), op=ALU.subtract),
                           inc=1)
                    oq_guard[ic] = p.mark("vector")
                m_pool = p.mark("vector")
                p.wait("tensor", m_pool)
                if tc == 0:
                    p.wait("tensor", mdn1)
                    p.wait("tensor", mdn2)
                for hc in range(NEC):
                    bank = psb[4 + hc % 2]
                    if bank_guard2[hc % 2]:
                        p.wait("tensor", bank_guard2[hc % 2])
                    for pa, (war, ooff) in enumerate(
                            ((arA, CB_OH), (arA, CB_OL), (arB, CB_OH))):
                        for ic in range(NFC):
                            st_ = (pa == 0 and ic == 0)
                            sp_ = (pa == 2 and ic == NFC - 1)
                            p.emit("tensor", lambda h, s_, war=war, ooff=ooff,
                                   ic=ic, hc=hc, bank=bank, st=st_, sp=sp_:
                                   h.matmul(bank[:, :],
                                            vb(war, (hc * 24 + ic) * 128, 128),
                                            vb(arC, ooff + ic * 512, 512),
                                            start=st, stop=sp),
                                   inc=(1 if sp_ else 0))
                    mmm = p.mark("tensor")
                    myld = p.dma("sync", lambda h, s_, hc=hc, tc=tc: h.dma_start(
                        out=vf(arC, P11_ST[3], 512),
                        in_=yscr[ts(hc), tc * 512:(tc + 1) * 512]))
                    p.wait("vector", mmm)
                    p.wait("vector", myld)
                    p.emit("vector", lambda h, s_, hc=hc, bank=bank: h.tensor_tensor(
                        out=vf(arC, P11_ST[4] + 0, 512) if False else
                        vf(arB, 12288 + hc * 512, 512), in0=bank[:, :],
                        in1=vf(arC, P11_ST[3], 512), op=ALU.add), inc=1)
                    mypre = p.mark("vector")
                    p.wait("sync", mypre)
                    bank_guard2[hc % 2] = mypre
                    p.wait("scalar", mypre)
                    if sq_guard2[0]:
                        p.wait("scalar", sq_guard2[0])
                    p.emit("scalar", lambda h, s_, hc=hc: h.activation(
                        vf(arC, P11_ST[5], 512), vf(arB, 12288 + hc * 512, 512),
                        AF.Square), inc=1)
                    msq = p.mark("scalar")
                    p.wait("tensor", mypre)
                    p.emit("tensor", lambda h, s_, hc=hc: h.matmul(
                        psb[6][0:1, 0:512], onescol[:, :],
                        vf(arB, 12288 + hc * 512, 512),
                        start=(hc == 0), stop=(hc == NEC - 1)))
                    p.wait("tensor", msq)
                    p.emit("tensor", lambda h, s_, hc=hc: h.matmul(
                        psb[7][0:1, 0:512], onescol[:, :], vf(arC, P11_ST[5], 512),
                        start=(hc == 0), stop=(hc == NEC - 1)), inc=1)
                    sq_guard2[0] = p.mark("tensor")
                m_stats = p.mark("tensor")
                p.wait("vector", m_stats)
                p.emit("vector", lambda h, s_: h.tensor_scalar(
                    out=mrow[:, :], in0=psb[6][0:1, 0:512], scalar1=1.0 / 1024.0,
                    scalar2=None, op0=ALU.mult), inc=1)
                mm1 = p.mark("vector")
                p.wait("scalar", mm1)
                p.emit("scalar", lambda h, s_: h.activation(
                    rrow[:, :], mrow[:, :], AF.Square), inc=1)
                mm2 = p.mark("scalar")
                p.wait("vector", mm2)
                p.emit("vector", lambda h, s_: h.tensor_scalar(
                    out=vrow[:, :], in0=psb[7][0:1, 0:512], scalar1=1.0 / 1024.0,
                    scalar2=None, op0=ALU.mult), inc=1)
                p.wait("vector", p.mark("vector"))
                p.emit("vector", lambda h, s_: h.tensor_tensor(
                    out=vrow[:, :], in0=vrow[:, :], in1=rrow[:, :],
                    op=ALU.subtract), inc=1)
                mm3 = p.mark("vector")
                p.wait("scalar", mm3)
                p.emit("scalar", lambda h, s_: h.activation(
                    rrow[:, :], vrow[:, :], AF.Ln, bias=eps1[:, :]), inc=1)
                p.wait("scalar", p.mark("scalar"))
                p.emit("scalar", lambda h, s_: h.activation(
                    rrow[:, :], rrow[:, :], AF.Exp, scale=-0.5), inc=1)
                mm4 = p.mark("scalar")
                p.wait("vector", mm4)
                p.wait("vector", p.mark("vector"))
                p.emit("vector", lambda h, s_: h.tensor_tensor(
                    out=mrrow[:, :], in0=mrow[:, :], in1=rrow[:, :],
                    op=ALU.mult), inc=1)
                mm5 = p.mark("vector")
                p.wait("tensor", mm5)
                p.emit("tensor", lambda h, s_: h.matmul(
                    psb[2][:, :], onesrow[:, :], rrow[:, :], start=True, stop=True))
                p.emit("tensor", lambda h, s_: h.matmul(
                    psb[3][:, :], onesrow[:, :], mrrow[:, :], start=True,
                    stop=True), inc=1)
                m_bc = p.mark("tensor")
                for hc in range(NEC):
                    ob = hc % 2
                    p.wait("vector", m_bc)
                    if out_guard[ob]:
                        p.wait("vector", out_guard[ob])
                    p.emit("vector", lambda h, s_, hc=hc: h.tensor_tensor(
                        out=vf(arC, P11_ST[6], 512),
                        in0=vf(arB, 12288 + hc * 512, 512), in1=psb[2][:, :],
                        op=ALU.mult), inc=1)
                    p.wait("vector", p.mark("vector"))
                    p.emit("vector", lambda h, s_, hc=hc, ob=ob: h.tensor_tensor(
                        out=vf(arA, 12288 + ob * 512, 512),
                        in0=vf(arC, P11_ST[6], 512), in1=psb[3][:, :],
                        op=ALU.subtract), inc=1)
                    mych = p.mark("vector")
                    p.wait("gpsimd", mych)
                    out_guard[ob] = p.dma("gpsimd", lambda h, s_, hc=hc, tc=tc,
                                          ob=ob: h.dma_start(
                        out=out[ts(hc), tc * 512:(tc + 1) * 512],
                        in_=vf(arA, 12288 + ob * 512, 512)))
                    m_out_last = out_guard[ob]
                p.wait("tensor", p.mark("vector"))
            return m_out_last

        sq_guard2 = [None]
        m_b_done = phase_b()

        def dump_mdbg():
            if mdbg is None:
                return
            p.emit("vector", lambda h, s_: h.tensor_copy(
                vf(arB, B_TMP, 512), psb[2][:, :]))
            p.emit("vector", lambda h, s_: h.tensor_copy(
                vf(arB, B_YCH[0], 512), psb[3][:, :]), inc=1)
            mpv = p.mark("vector")
            p.wait(G, mpv)
            p.dma(G, lambda h, s_: h.dma_start(out=mdbg["pv2"], in_=vf(arB, B_TMP, 512)))
            p.dma(G, lambda h, s_: h.dma_start(out=mdbg["pv3"], in_=vf(arB, B_YCH[0], 512)))
            p.dma(G, lambda h, s_: h.dma_start(out=mdbg["lnx"], in_=lnx[:, :]))
            p.dma(G, lambda h, s_: h.dma_start(out=mdbg["rex"], in_=rex[:, :]))
            p.dma(G, lambda h, s_: h.dma_start(out=mdbg["ccs"], in_=ccs[:, :]))
            p.dma(G, lambda h, s_: h.dma_start(out=mdbg["v2"], in_=vf(arA, 0, 8320)))
            p.dma(G, lambda h, s_: h.dma_start(out=mdbg["k2p0"], in_=k2v(0)))
            p.dma(G, lambda h, s_: h.dma_start(out=mdbg["qp0"], in_=qv(0)))

        def finalize(extra_markers):
            for m in extra_markers:
                p.wait(G, m)
            dump_mdbg()
            for name, (odbg, src) in dbg.items():
                rows, cols = src.shape
                for r0 in range(0, rows, 128):
                    p.dma(G, lambda h, s_, odbg=odbg, src=src, r0=r0: h.dma_start(
                        out=odbg[r0:r0 + 128, :], in_=src[r0:r0 + 128, :]))
            tgt = p.dma_counts[G]
            if tgt:
                p.emit(G, lambda h, s_, t=tgt: h.wait_ge(s_["dma_gpsimd"], t))
            for e in ("sync",):
                v = p.dma_counts[e]
                if v:
                    p.emit(G, lambda h, s_, e=e, v=v: h.wait_ge(s_["dma_" + e], v))

        finalize([m_b_done])

        with nc.Block() as block:
            @block.sync
            def _(eng):
                for fn in p.ops["sync"]:
                    fn(eng, sems)

            @block.tensor
            def _(eng):
                for fn in p.ops["tensor"]:
                    fn(eng, sems)

            @block.vector
            def _(eng):
                for fn in p.ops["vector"]:
                    fn(eng, sems)

            @block.scalar
            def _(eng):
                for fn in p.ops["scalar"]:
                    fn(eng, sems)

            @block.gpsimd
            def _(eng):
                for fn in p.ops["gpsimd"]:
                    fn(eng, sems)

    return nc



def _host_inputs(L0_attn_x, L0_attn_w_in, L0_attn_w_out,
                 L0_ffn_w_gate, L0_ffn_w_up, L0_ffn_w_down):
    win_t = np.ascontiguousarray(
        L0_attn_w_in.T.reshape(8, 128, 24, 128).transpose(2, 1, 0, 3))
    wout_t = np.ascontiguousarray(
        L0_attn_w_out.T.reshape(8, 128, 8, 128).transpose(2, 1, 0, 3))
    maps = []
    for c in range(NCORES):
        maps.append(dict(
            x_t=np.ascontiguousarray(L0_attn_x[c].T),
            w_in_t=win_t, w_out_t=wout_t,
            w_gate=L0_ffn_w_gate, w_up=L0_ffn_w_up, w_down=L0_ffn_w_down,
        ))
    return maps


def kernel(L0_attn_x, L0_attn_w_in, L0_attn_w_out,
           L0_ffn_w_gate, L0_ffn_w_up, L0_ffn_w_down):
    nc = build_program()
    maps = _host_inputs(L0_attn_x, L0_attn_w_in, L0_attn_w_out,
                        L0_ffn_w_gate, L0_ffn_w_up, L0_ffn_w_down)
    res = run_bass_kernel_spmd(nc, maps, core_ids=list(range(NCORES)))
    outs = [np.ascontiguousarray(res.results[c]["out"].T) for c in range(NCORES)]
    return np.stack(outs, axis=0)


if __name__ == "__main__":
    print("megakernel builder")
